# version 16
# baseline (speedup 1.0000x reference)
import numpy as np
import concourse.bass as bass
import concourse.mybir as mybir
from concourse.bass_utils import run_bass_kernel_spmd

F32 = mybir.dt.float32
I32 = mybir.dt.int32
AL = mybir.AluOpType
AF = mybir.ActivationFunctionType
AX = mybir.AxisListType

B, T, D, H, DH = 8, 2048, 1024, 16, 64
NT, NQ, ND = T // 128, T // 512, D // 128
C32 = 12582912.0
C64 = 6755399441055744.0
BIGNEG = -1.0e30
EPS = 1e-5
PAD = 512
VW = 2688
MW = 2048 + PAD
NBLK = 4

ENGINES = ("sync", "tensor", "vector", "scalar", "gpsimd")
SEMS = ENGINES + ("gq",)


class Prog:
    def __init__(self):
        self.steps = {e: [] for e in ENGINES}
        self.cnt = {s: 0 for s in SEMS}
        self.waited = {(c, p): 0 for c in ENGINES for p in SEMS}

    def emit(self, eng, fn, deps=(), inc=0, sem=None):
        waits = {}
        for tok in deps:
            if tok is None:
                continue
            peng, val = tok
            if val > self.waited[(eng, peng)]:
                waits[peng] = max(waits.get(peng, 0), val)
        for peng, val in waits.items():
            self.waited[(eng, peng)] = val
        tok = None
        if inc:
            s = sem or eng
            self.cnt[s] += inc
            tok = (s, self.cnt[s])
        self.steps[eng].append((fn, sorted(waits.items()), inc, sem or eng))
        return tok

    def tok(self, eng):
        return (eng, self.cnt[eng])

    def barrier(self):
        toks = [(s, self.cnt[s]) for s in SEMS]
        for e in ENGINES:
            for peng, val in toks:
                if peng != e and val > self.waited[(e, peng)]:
                    self.waited[(e, peng)] = val
                    self.steps[e].append((None, [(peng, val)], 0, e))


def build_program():
    nc = bass.Bass("TRN2", target_bir_lowering=False, debug=False)

    def din(name, shape):
        return nc.dram_tensor(name, list(shape), F32, kind="ExternalInput").ap()

    x_in = din("x_in", (T, D))
    qkvwT0 = din("qkvwT0", (D, 3 * D))
    qkvb0 = din("qkvb0", (128, 24))
    vb0 = din("vb0", (1, D))
    projT0 = din("projT0", (D, D))
    projb0 = din("projb0", (1, D))
    relpos = din("relpos", (2047, D))
    ffw_in = din("ffw_in", (D, 64))
    ffb_in = din("ffb_in", (64, 1))
    ffw1 = din("ffw1", (64, 16))
    ffb1 = din("ffb1", (64, 16))
    ffw2 = din("ffw2", (64, 16))
    ffb2 = din("ffb2", (64, 1))
    ffw_out = din("ffw_out", (64, D))
    ffb_out = din("ffb_out", (1, D))
    qkvwT1 = din("qkvwT1", (D, 3 * D))
    woutT1 = din("woutT1", (D, D))
    wgate = din("wgate", (3 * D, D))
    wup = din("wup", (3 * D, D))
    wdown = din("wdown", (D, 3 * D))
    y_out = nc.dram_tensor("y_out", [T, D], F32, kind="ExternalOutput").ap()
    DBG = False
    dbg = {}
    if DBG:
        for nm, shp in (("d_q", (D, T)), ("d_k", (D, T)), ("d_v", (T, D)), ("d_oT", (D, T)),
                        ("d_dn", (16, T)), ("d_sbout", (T, D)), ("d_mh", (1, 32)), ("d_vec", (1, VW)), ("d_m0", (128, MW)), ("d_tab", (16, 2047)), ("d_xres", (T, D)),
                        ("d_up0", (128, T)), ("d_oraw", (128, T)), ("d_g2", (128, T)),
                        ("d_wdq", (128, 3 * D)), ("d_cc", (4, 8)), ("d_xq", (128, T))):
            dbg[nm] = nc.dram_tensor(nm, list(shp), F32, kind="ExternalOutput").ap()

    def dsc(name, shape):
        return nc.dram_tensor(name, list(shape), F32, kind="Internal").ap()

    x_res = dsc("x_res", (T, D))
    sb_out = dsc("sb_out", (T, D))
    q_st = dsc("q_st", (D, T))
    k_st = dsc("k_st", (D, T))
    v_st = dsc("v_st", (T, D))
    oT_st = dsc("oT_st", (D, T))
    vec_dram = dsc("vec_dram", (18, VW))
    vecflat = dsc("vecflat", (1, 18 * VW + 128))
    m_dram = dsc("m_dram", (128, 18 * VW))
    dn_dram = dsc("dn_dram", (16, T))
    bc_dram = dsc("bc_dram", (4, 8))
    bmax_dram = dsc("bmax_dram", (16, 8))
    cc_in = dsc("cc_in", (1, 8))
    cc_out = dsc("cc_out", (1, 8))
    up_st = dsc("up_st", (3 * D, T))
    gt_st = dsc("gt_st", (3 * D, T))
    g2_st = dsc("g2_st", (3 * D, T))
    wdqT_up = dsc("wdqT_up", (D, 3 * D))
    wdqT_gt = dsc("wdqT_gt", (D, 3 * D))
    wdqT_dn = dsc("wdqT_dn", (3 * D, D))

    def sbuf(name, shape, dtype=F32):
        return nc.alloc_sbuf_tensor(name, list(shape), dtype)

    bigA = sbuf("bigA", (128, ND * T))
    wb0 = sbuf("wb0", (128, 4096))
    wb1 = sbuf("wb1", (128, 4096))
    scr = sbuf("scr", (128, 19456))
    ident = sbuf("ident", (128, 128))
    e_t = sbuf("e_t", (128, 8))
    e_t0 = sbuf("e_t0", (128, 8))
    sc = [sbuf("sc%d" % i, (128, 16)) for i in range(8)]
    bias_sb = sbuf("bias_sb", (128, 24))
    fsm = sbuf("fsm", (64, 72))

    psums = [nc.alloc_psum_tensor("ps%d" % i, [128, 512], F32) for i in range(7)]
    ps_last = {}

    pr = Prog()

    def dma(qeng, out, in_, deps=()):
        qs = "gq" if qeng == "gpsimd" else qeng
        gate = (qs, pr.cnt[qs])
        return pr.emit(qeng, lambda h, o=out, i=in_: h.dma_start(out=o, in_=i),
                       deps=list(deps) + [gate], inc=16, sem=qs)

    def V(fn, deps=(), inc=1):
        return pr.emit("vector", fn, deps=deps, inc=inc)

    def S(fn, deps=(), inc=1):
        return pr.emit("scalar", fn, deps=deps, inc=inc)

    def TE(fn, deps=(), inc=1):
        return pr.emit("tensor", fn, deps=deps, inc=inc)

    def G(fn, deps=(), inc=1):
        return pr.emit("gpsimd", fn, deps=deps, inc=inc)

    def sv(p0, np_, c0, nc_):
        return scr[p0:p0 + np_, c0:c0 + nc_]

    fence_prev = {"v": None, "s": None}

    def fv(tok):
        t = V(lambda h: h.memset(junkV[:, 0:8], 1.0), deps=[tok, fence_prev["v"]])
        fence_prev["v"] = t
        return t

    def fs(tok):
        t = S(lambda h: h.copy(junkS[:, 0:8], e_t[:, 0:8]), deps=[tok, fence_prev["s"]])
        fence_prev["s"] = t
        return t

    t0 = V(lambda h: h.memset(e_t0[:], 1.0))
    t1 = V(lambda h: h.tensor_single_scalar(e_t[:, 0:8], e_t0[:, 0:8], 2.0 ** -30, AL.add), deps=[t0])
    t2 = V(lambda h: h.tensor_single_scalar(e_t0[:, 0:8], e_t[:, 0:8], 1.0, AL.is_le), deps=[t1])
    tk_e = V(lambda h: h.tensor_copy(e_t[:, 0:8], e_t0[:, 0:8]), deps=[t2])
    tk_eps = V(lambda h: h.memset(sc[7][:, 8:16], EPS))
    ones8 = sbuf("ones8", (128, 8))
    junkV = sbuf("junkV", (128, 8))
    junkS = sbuf("junkS", (128, 8))
    tk_ones = V(lambda h: h.memset(ones8[:], 1.0))

    iotv = sv(0, 128, 0, 128).bitcast(I32)
    tio = G(lambda h: h.iota(iotv, [[1, 128]], base=0, channel_multiplier=-1))
    tk_id = V(lambda h: h.tensor_single_scalar(ident[:], iotv, 0.0, AL.is_equal), deps=[tio])

    tk_qb = dma("sync", bias_sb[:, :], qkvb0[:, :])
    tfs = [dma("sync", fsm[0:64, 0:1], ffb_in[:, :]),
           dma("sync", fsm[0:64, 1:17], ffw1[:, :]),
           dma("sync", fsm[0:64, 17:33], ffb1[:, :]),
           dma("sync", fsm[0:64, 33:49], ffw2[:, :]),
           dma("sync", fsm[0:64, 49:50], ffb2[:, :])]

    tabT = sv(0, 16, 4096, 2047)
    prev_tr = {0: None, 1: None}
    prev_rd = {0: None, 1: None}
    for rc in range(16):
        rows = 128 if rc < 15 else 2047 - 15 * 128
        par = rc % 2
        c0 = par * 1280
        ld = dma("sync", wb0[:rows, c0:c0 + D], relpos[rc * 128: rc * 128 + rows, :],
                 deps=[prev_rd[par]])
        rd = V(lambda h, rows=rows, c0=c0: h.reduce_sum(
            wb0[:rows, c0 + D:c0 + D + 16], wb0[:rows, c0:c0 + D].rearrange("p (a b) -> p a b", b=64), axis=AX.X),
            deps=[ld, prev_tr[par]])
        prev_rd[par] = rd
        tr = TE(lambda h, rows=rows, c0=c0: h.transpose(psums[6][0:16, 0:rows], wb0[:rows, c0 + D:c0 + D + 16],
                                                        ident[:rows, :rows]),
                deps=[rd, tk_id, ps_last.get(6)])
        prev_tr[par] = tr
        cp = V(lambda h, rc=rc, rows=rows: h.tensor_copy(
            sv(0, 16, 4096 + rc * 128, rows), psums[6][0:16, 0:rows]), deps=[tr])
        ps_last[6] = cp

    vecp = sv(0, 16, 6800, VW)
    OFF0 = PAD + 127
    tv0 = V(lambda h: h.memset(sv(0, 32, 6800, VW), BIGNEG * 8.0))
    tv1 = V(lambda h: h.tensor_single_scalar(
        sv(0, 16, 6800 + OFF0, 1024), sv(0, 16, 4096 + 1023, 1024), 8.0, AL.mult),
        deps=[tv0, ps_last[6]])
    tv2 = tv1
    wdt = 0
    tv2 = V(lambda h: h.tensor_copy(sv(0, 16, 6800 + OFF0 + 1024, 1), sv(0, 16, 6800 + OFF0 + 1023, 1)), deps=[tv1])
    wdt = 1
    while wdt < 1024:
        wn = min(wdt, 1024 - wdt)
        tv2 = V(lambda h, wdt=wdt, wn=wn: h.tensor_copy(
            sv(0, 16, 6800 + OFF0 + 1024 + wdt, wn), sv(0, 16, 6800 + OFF0 + 1024, wn)), deps=[tv2])
        wdt += wn
    tv3 = V(lambda h: h.memset(sv(0, 1, 9488, VW), BIGNEG * 8.0))
    tv3 = V(lambda h: h.memset(sv(0, 1, 9488 + OFF0, VW - OFF0), 0.0), deps=[tv3])
    tv4 = V(lambda h: h.memset(sv(0, 1, 12176, VW), BIGNEG * 8.0))
    tv4 = V(lambda h: h.memset(sv(0, 1, 12176 + OFF0, 512), 0.0), deps=[tv4])
    tbm = V(lambda h: h.reduce_max(sc[7][0:16, 0:1], sv(0, 16, 6800 + OFF0, 2048), axis=AX.X), deps=[tv2])
    wd = 1
    while wd < 8:
        tbm = V(lambda h, wd=wd: h.tensor_copy(sc[7][0:16, wd:2 * wd], sc[7][0:16, 0:wd]), deps=[tbm])
        wd *= 2
    tbm = dma("gpsimd", bmax_dram[:, :], sc[7][0:16, 0:8], deps=[fv(tbm)])
    if DBG:
        dma("gpsimd", dbg["d_tab"][:, :], sv(0, 16, 4096, 2047), deps=[pr.tok("vector")])
    tvf = fv(tv2)
    tvf = V(lambda h: h.memset(junkV[:, 0:8], 4.0), deps=[tv3, tv4, tvf, fence_prev["v"]])
    fence_prev["v"] = tvf
    tvds = [dma("gpsimd", vecflat[0:1, hh * VW:(hh + 1) * VW], scr[hh:hh + 1, 6800:6800 + VW],
                deps=[tvf]) for hh in range(16)]
    tvds.append(dma("gpsimd", vecflat[0:1, 16 * VW:17 * VW], sv(0, 1, 9488, VW), deps=[tvf]))
    tvds.append(dma("gpsimd", vecflat[0:1, 17 * VW:18 * VW], sv(0, 1, 12176, VW), deps=[tvf]))
    tz = V(lambda h: h.memset(sc[6][0:1, 0:8], 0.0), deps=[fence_prev["v"]])
    tz = fv(tz)
    for j in range(16):
        tvds.append(dma("gpsimd", vecflat[0:1, 18 * VW + j * 8:18 * VW + (j + 1) * 8], sc[6][0:1, 0:8], deps=[tz]))
    tvd = None
    for p in range(128):
        tvd = dma("gpsimd", m_dram[p:p + 1, :], vecflat[0:1, 127 - p:127 - p + 18 * VW], deps=tvds)
    pr.barrier()

    def ln_stats(src_dram, nblk_tag):
        toks = []
        for tt in range(NT):
            buf = sv(0, 128, (tt % 2) * D, D)
            ld = dma("sync", buf, src_dram[tt * 128:(tt + 1) * 128, :], deps=toks[-2:])
            tv = V(lambda h, buf=buf, tt=tt: h.reduce_sum(sc[0][:, tt:tt + 1], buf, axis=AX.X), deps=[ld])
            ts = S(lambda h, buf=buf, tt=tt: h.activation(
                sv(0, 128, 2 * D, D), buf, AF.Square, accum_out=sc[1][:, tt:tt + 1]), deps=[ld, toks[-1] if toks else None])
            toks.append(tv)
            toks.append(ts)
            pr.barrier()
        return toks[-2:]

    def ln_mr(deps):
        a1 = V(lambda h: h.tensor_single_scalar(sc[2][:], sc[0][:], 1.0 / D, AL.mult), deps=deps)
        a2 = V(lambda h: h.tensor_single_scalar(sc[4][:], sc[1][:], 1.0 / D, AL.mult), deps=deps)
        a3 = V(lambda h: h.tensor_tensor(sc[5][:], sc[2][:], sc[2][:], AL.mult), deps=[a1])
        a4 = V(lambda h: h.tensor_tensor(sc[6][:], sc[4][:], sc[5][:], AL.subtract), deps=[a2, a3])
        a4c = V(lambda h: h.tensor_single_scalar(sc[4][:], sc[6][:], 0.0, AL.max), deps=[a4])
        a5 = S(lambda h: h.activation(sc[5][:], sc[4][:], AF.Sqrt, bias=sc[7][:, 8:9]), deps=[a4c, tk_eps])
        a6 = V(lambda h: h.reciprocal(sc[3][:], sc[5][:]), deps=[a5])
        return [a1, a6]

    def pre_template(blk):
        src = x_in if blk == 0 else x_res
        stok = ln_stats(src, blk)
        mr = ln_mr(stok)
        pr.barrier()
        prev = []
        for tt in range(NT):
            buf = sv(0, 128, (tt % 2) * D, D)
            nbuf = sv(0, 128, (2 + tt % 2) * D, D)
            ld = dma("sync", buf, src[tt * 128:(tt + 1) * 128, :], deps=prev)
            nm = V(lambda h, buf=buf, nbuf=nbuf, tt=tt: h.tensor_scalar(
                nbuf, buf, sc[2][:, tt:tt + 1], sc[3][:, tt:tt + 1], AL.subtract, AL.mult),
                deps=[ld] + mr)
            prev = []
            for c in range(ND):
                pidx = c % 4
                tr = TE(lambda h, nbuf=nbuf, c=c, pidx=pidx: h.transpose(
                    psums[pidx][:, 0:128], nbuf[:, c * 128:(c + 1) * 128], ident[:]),
                    deps=[nm, tk_id, ps_last.get(pidx)])
                cp = V(lambda h, c=c, tt=tt, pidx=pidx: h.tensor_copy(
                    bigA[:, c * T + tt * 128: c * T + (tt + 1) * 128], psums[pidx][:, 0:128]),
                    deps=[tr])
                ps_last[pidx] = cp
                prev.append(cp)
        pr.barrier()

    def post_template(blk):
        stok = ln_stats(sb_out, blk)
        mr = ln_mr(stok)
        pr.barrier()
        src = x_in if blk == 0 else x_res
        dst = y_out if blk == NBLK - 1 else x_res
        prev = []
        for tt in range(NT):
            ob = sv(0, 128, (tt % 2) * D, D)
            xb = sv(0, 128, (2 + tt % 2) * D, D)
            lo = dma("sync", ob, sb_out[tt * 128:(tt + 1) * 128, :], deps=prev)
            lx = dma("sync", xb, x_res[tt * 128:(tt + 1) * 128, :] if blk > 0 else x_in[tt * 128:(tt + 1) * 128, :], deps=prev)
            nm = V(lambda h, ob=ob, tt=tt: h.tensor_scalar(
                ob, ob, sc[2][:, tt:tt + 1], sc[3][:, tt:tt + 1], AL.subtract, AL.mult),
                deps=[lo] + mr)
            ad = V(lambda h, ob=ob, xb=xb: h.tensor_tensor(xb, xb, ob, AL.add), deps=[nm, lx])
            st = dma("gpsimd", dst[tt * 128:(tt + 1) * 128, :], xb, deps=[fv(ad)])
            prev = [st]
            pr.barrier()
        pr.barrier()

    def attn_body(layer):
        is0 = layer == 0
        wT = qkvwT0 if is0 else qkvwT1
        pT = projT0 if is0 else woutT1
        for oc in range(16):
            wcol = wb0 if oc % 2 == 0 else wb1
            lw = dma("sync", bass.AP(tensor=wcol[:].tensor, offset=wcol[:].offset, ap=[[wcol[:].ap[0][0], 128], [128, 8], [1, 128]]),
                     bass.AP(tensor=wT.tensor, offset=oc * 128, ap=[[3 * D, 128], [128 * 3 * D, 8], [1, 128]]),
                     deps=[ps_last.get("w%d" % (oc % 2))])
            for tt in range(NQ):
                pidx = 4 + tt % 2
                mms = []
                for dc in range(ND):
                    mm = TE(lambda h, wcol=wcol, dc=dc, tt=tt, pidx=pidx: h.matmul(
                        psums[pidx][:, :], wcol[:, dc * 128:(dc + 1) * 128],
                        bigA[:, dc * T + tt * 512: dc * T + (tt + 1) * 512],
                        start=(dc == 0), stop=(dc == ND - 1)),
                        deps=([lw, ps_last.get(pidx)] if dc == 0 else []), inc=(1 if dc == ND - 1 else 0))
                    mms.append(mm)
                if is0:
                    ev = S(lambda h, oc=oc, tt=tt, pidx=pidx: h.activation(
                        sv(0, 128, 16384 + (tt % 2) * 512, 512), psums[pidx][:, :], AF.Identity,
                        bias=bias_sb[:, oc:oc + 1]), deps=[mms[-1], tk_qb, ps_last.get("qkev%d" % (tt % 2))])
                else:
                    ev = S(lambda h, tt=tt, pidx=pidx: h.copy(
                        sv(0, 128, 16384 + (tt % 2) * 512, 512), psums[pidx][:, :]),
                        deps=[mms[-1], ps_last.get("qkev%d" % (tt % 2))])
                ps_last[pidx] = ev
                dstt = q_st if oc < 8 else k_st
                row0 = (oc % 8) * 128
                st = dma("gpsimd", dstt[row0:row0 + 128, tt * 512:(tt + 1) * 512],
                         sv(0, 128, 16384 + (tt % 2) * 512, 512), deps=[fs(ev) if is0 else fs(ev)])
                ps_last["w%d" % (oc % 2)] = st
                ps_last["qkev%d" % (tt % 2)] = st
            pr.barrier()
        pr.barrier()
        for ot in range(2):
            lw = dma("sync", bass.AP(tensor=wb0[:].tensor, offset=wb0[:].offset, ap=[[wb0[:].ap[0][0], 128], [512, 8], [1, 512]]),
                     bass.AP(tensor=wT.tensor, offset=2 * D + ot * 512, ap=[[3 * D, 128], [128 * 3 * D, 8], [1, 512]]),
                     deps=[pr.tok("tensor")])
            for tt in range(NT):
                pidx = 4 + tt % 2
                mms = []
                for dc in range(ND):
                    mm = TE(lambda h, dc=dc, tt=tt, ot=ot, pidx=pidx: h.matmul(
                        psums[pidx][:, :], bigA[:, dc * T + tt * 128: dc * T + (tt + 1) * 128],
                        wb0[:, dc * 512:(dc + 1) * 512],
                        start=(dc == 0), stop=(dc == ND - 1)),
                        deps=([lw, ps_last.get(pidx)] if dc == 0 else []), inc=(1 if dc == ND - 1 else 0))
                    mms.append(mm)
                if is0:
                    ev = V(lambda h, tt=tt, ot=ot, pidx=pidx: h.scalar_tensor_tensor(
                        sv(0, 128, 16384 + (tt % 2) * 512, 512), psums[pidx][:, :], 1.0,
                        sv(0, 128, 18432 + ot * 512, 512), AL.bypass, AL.add),
                        deps=[mms[-1], ps_last.get("vb"), ps_last.get("qkev%d" % (tt % 2))])
                else:
                    ev = V(lambda h, tt=tt, pidx=pidx: h.tensor_copy(
                        sv(0, 128, 16384 + (tt % 2) * 512, 512), psums[pidx][:, :]),
                        deps=[mms[-1], ps_last.get("qkev%d" % (tt % 2))])
                ps_last[pidx] = ev
                st = dma("gpsimd", v_st[tt * 128:(tt + 1) * 128, ot * 512:(ot + 1) * 512],
                         sv(0, 128, 16384 + (tt % 2) * 512, 512), deps=[fv(ev)])
                ps_last["qkev%d" % (tt % 2)] = st
                pr.barrier()
        pr.barrier()

        qp = sv(0, 128, 0, 2048)
        kp = sv(0, 128, 2048, 2048)
        vsb_base = 4096
        mA_base = 6176
        mB_base = 8736
        mC = sv(0, 16, 11296, MW)
        qa = sv(0, 128, 13856, 2048)
        u0 = sv(0, 128, 15904, 512)
        u1 = sv(0, 128, 16416, 512)
        pp0 = sv(0, 128, 16928, 512)
        pp1 = sv(0, 128, 17440, 512)
        evb = sv(0, 128, 17952, 512)
        tko = V(lambda h: h.memset(
            bass.AP(tensor=scr[:].tensor, offset=scr[:].offset + vsb_base + 64,
                    ap=[[scr[:].ap[0][0], 128], [130, NT], [65, 2]]), 1.0))
        if not is0:
            mwA = dma("sync", sv(0, 128, mA_base, MW), m_dram[0:128, 17 * VW:17 * VW + MW], deps=[tvd])
            mwB = dma("sync", sv(0, 128, mB_base, MW), m_dram[0:128, 17 * VW:17 * VW + MW], deps=[tvd])
            mwB = V(lambda h: h.memset(sv(0, 16, mB_base, MW), BIGNEG * 8.0), deps=[mwB])
            mwC = dma("sync", mC, m_dram[0:16, 16 * VW:16 * VW + MW], deps=[tvd])
            mtoks = [mwA, mwB, mwC]
        for p in range(8):
            hA, hB = 2 * p, 2 * p + 1
            lq = dma("sync", qp, q_st[p * 128:(p + 1) * 128, :], deps=[ps_last.get("attn_done")])
            lk = dma("sync", kp, k_st[p * 128:(p + 1) * 128, :], deps=[ps_last.get("attn_done")])
            lv0 = dma("sync", bass.AP(tensor=scr[:].tensor, offset=scr[:].offset + vsb_base,
                                      ap=[[scr[:].ap[0][0], 128], [130, NT], [1, 64]]),
                      bass.AP(tensor=v_st.tensor, offset=p * 128, ap=[[D, 128], [128 * D, NT], [1, 64]]),
                      deps=[tko, ps_last.get("attn_done")])
            lv1 = dma("sync", bass.AP(tensor=scr[:].tensor, offset=scr[:].offset + vsb_base + 65,
                                      ap=[[scr[:].ap[0][0], 128], [130, NT], [1, 64]]),
                      bass.AP(tensor=v_st.tensor, offset=p * 128 + 64, ap=[[D, 128], [128 * D, NT], [1, 64]]),
                      deps=[tko, ps_last.get("attn_done")])
            lv = lv1
            if is0:
                mwA = dma("sync", sv(0, 128, mA_base, MW), m_dram[0:128, hA * VW:hA * VW + MW],
                          deps=[tvd, ps_last.get("attn_done")])
                mwB = dma("sync", sv(0, 128, mB_base, MW), m_dram[0:128, hB * VW:hB * VW + MW],
                          deps=[tvd, ps_last.get("attn_done")])
                mtoks = [mwA, mwB]
            z6 = V(lambda h: h.memset(sc[6][0:1, 0:8], 0.0))
            sqk = S(lambda h: h.activation(qa, kp, AF.Square), deps=[lk, z6])
            nrm_toks = []
            for hh2 in range(2):
                for qt in range(NQ):
                    mm = TE(lambda h, hh2=hh2, qt=qt: h.matmul(
                        psums[6][0:1, :], ones8[hh2 * 64:hh2 * 64 + 64, 0:1],
                        qa[hh2 * 64:hh2 * 64 + 64, qt * 512:(qt + 1) * 512],
                        start=True, stop=True), deps=[sqk, tk_ones, ps_last.get(6)])
                    cx = V(lambda h, qt=qt: h.tensor_copy(sv(0, 1, 15904 + qt * 512, 512), psums[6][0:1, :]),
                           deps=[mm, nrm_toks[-1] if nrm_toks else None])
                    ps_last[6] = cx
                    nrm_toks.append(cx)
                nrm_toks.append(V(lambda h, hh2=hh2: h.reduce_max(sc[6][0:1, 1 + hh2:2 + hh2],
                                                                  sv(0, 1, 15904, 2048), axis=AX.X),
                                  deps=nrm_toks[-4:] + [z6]))
            sqq = S(lambda h: h.activation(qa, qp, AF.Square), deps=[lq] + nrm_toks)
            for hh2 in range(2):
                for qt in range(NQ):
                    mm = TE(lambda h, hh2=hh2, qt=qt: h.matmul(
                        psums[6][0:1, :], ones8[hh2 * 64:hh2 * 64 + 64, 0:1],
                        qa[hh2 * 64:hh2 * 64 + 64, qt * 512:(qt + 1) * 512],
                        start=True, stop=True), deps=[sqq, tk_ones, ps_last.get(6)])
                    cx = V(lambda h, qt=qt: h.tensor_copy(sv(0, 1, 15904 + qt * 512, 512), psums[6][0:1, :]),
                           deps=[mm, nrm_toks[-1] if nrm_toks else None])
                    ps_last[6] = cx
                    nrm_toks.append(cx)
                nrm_toks.append(V(lambda h, hh2=hh2: h.reduce_max(sc[6][0:1, 3 + hh2:4 + hh2],
                                                                  sv(0, 1, 15904, 2048), axis=AX.X),
                                  deps=nrm_toks[-4:] + [z6]))
            bst = dma("gpsimd", bc_dram[0:1, 0:8], sc[6][0:1, 0:8], deps=[fv(nrm_toks[-1])] + nrm_toks)
            mhld = dma("sync", sc[2][:, 0:8], bass.AP(tensor=bc_dram.tensor, offset=0, ap=[[0, 128], [1, 8]]),
                       deps=[bst])
            wtoks = []
            for idx, (dt_, dcol) in enumerate(((sc[0], 8), (sc[1], 8), (sc[0], 0), (sc[1], 0))):
                tkc = V(lambda h, dt_=dt_, dcol=dcol, idx=idx: h.tensor_copy(dt_[:, dcol:dcol + 1],
                                                                             sc[2][:, 1 + idx:2 + idx]), deps=[mhld])
                wd = 1
                while wd < 8:
                    tkc = V(lambda h, dt_=dt_, dcol=dcol, wd=wd: h.tensor_copy(
                        dt_[:, dcol + wd:dcol + 2 * wd], dt_[:, dcol:dcol + wd]), deps=[tkc])
                    wd *= 2
                wtoks.append(tkc)
            pA = V(lambda h: h.tensor_tensor(sc[3][:, 0:8], sc[0][:, 0:8], sc[0][:, 8:16], AL.mult), deps=wtoks)
            pB = V(lambda h: h.tensor_tensor(sc[3][:, 8:16], sc[1][:, 0:8], sc[1][:, 8:16], AL.mult), deps=wtoks)
            sq2 = S(lambda h: h.activation(sc[2][:, 0:16], sc[3][:, 0:16], AF.Sqrt), deps=[pA, pB])
            if is0:
                bmA = dma("sync", sc[4][:, 0:8], bass.AP(tensor=bmax_dram.tensor, offset=hA * 8, ap=[[0, 128], [1, 8]]),
                          deps=[tbm])
                bmB = dma("sync", sc[4][:, 8:16], bass.AP(tensor=bmax_dram.tensor, offset=hB * 8, ap=[[0, 128], [1, 8]]),
                          deps=[tbm])
                su = V(lambda h: h.tensor_tensor(sc[5][:, 0:16], sc[2][:, 0:16], sc[4][:, 0:16], AL.add),
                       deps=[sq2, bmA, bmB])
            else:
                su = V(lambda h: h.tensor_single_scalar(sc[5][:, 0:16], sc[2][:, 0:16], 0.0, AL.add), deps=[sq2])
            m1 = V(lambda h: h.tensor_single_scalar(sc[6][:, 0:16], sc[5][:, 0:16], -0.125, AL.mult), deps=[su])
            mh_ap = [sc[6][:, 0:1], sc[6][:, 8:9]]
            mh_ap16 = [sc[6][0:16, 0:1], sc[6][0:16, 8:9]]
            mh_toks = [m1]
            dn_toks = []
            for hh in range(2):
                hcur = hA + hh
                mbase = mA_base if (is0 and hh == 0) else (mB_base if is0 else mA_base)
                for Q0 in range(NQ):
                    if is0:
                        k0s = list(range(0, min(NT, 4 * Q0 + 4)))
                        k0s_pre = []
                    else:
                        k0s = list(range(max(0, 4 * (Q0 - 1)), 4 * Q0 + 4))
                        k0s_pre = [("pre", Q0)]
                    psS = 4
                    psO = 5
                    nk = len(k0s) + len(k0s_pre)
                    acc_i = 0
                    evtok = ps_last.get(psO)
                    for item in k0s_pre + k0s:
                        first = acc_i == 0
                        last = acc_i == nk - 1
                        if item == ("pre", Q0):
                            mm1 = TE(lambda h, hh=hh, Q0=Q0: h.matmul(
                                psums[psS][0:16, :], kp[hh * 64:(hh + 1) * 64, 0:16],
                                qp[hh * 64:(hh + 1) * 64, Q0 * 512:(Q0 + 1) * 512], start=True, stop=True),
                                deps=[lq, lk, ps_last.get(psS)])
                            if Q0 == 0:
                                uu = V(lambda h, Q0=Q0: h.tensor_tensor(
                                    u0[0:16, :], psums[psS][0:16, :], mC[0:16, Q0 * 512 + PAD: Q0 * 512 + PAD + 512], AL.add),
                                    deps=[mm1, ps_last.get("ee0")] + (mtoks if not is0 else []))
                                ee = S(lambda h, hh=hh: h.activation(pp0[0:16, :], u0[0:16, :], AF.Exp,
                                                                     bias=mh_ap16[hh], scale=0.125),
                                       deps=[uu, ps_last.get("mo0")] + mh_toks)
                            else:
                                ee = S(lambda h, hh=hh: h.activation(pp0[0:16, :], psums[psS][0:16, :], AF.Exp,
                                                                     bias=mh_ap16[hh], scale=0.125),
                                       deps=[mm1, ps_last.get("mo0")] + mh_toks)
                            ps_last[psS] = ee
                            ps_last["ee0"] = ee
                            mo = TE(lambda h, hh=hh, first=first, last=last: h.matmul(
                                psums[psO][0:65, :], bass.AP(tensor=scr[:].tensor,
                                                             offset=scr[:].offset + vsb_base + hh * 65,
                                                             ap=[[scr[:].ap[0][0], 16], [1, 65]]),
                                pp0[0:16, :], start=first, stop=last),
                                deps=[ee, lv0, lv1] + ([evtok] if first else []), inc=1)
                            ps_last["mo0"] = mo
                        else:
                            K0 = item
                            base = Q0 * 512 - K0 * 128
                            mb = mbase if (is0 or K0 > 0) else mB_base
                            mm1 = TE(lambda h, hh=hh, K0=K0, Q0=Q0: h.matmul(
                                psums[psS][:, :], kp[hh * 64:(hh + 1) * 64, K0 * 128:(K0 + 1) * 128],
                                qp[hh * 64:(hh + 1) * 64, Q0 * 512:(Q0 + 1) * 512], start=True, stop=True),
                                deps=[lq, lk, ps_last.get(psS)])
                            par2 = acc_i % 2
                            ub = u0 if par2 == 0 else u1
                            ppb = pp0 if par2 == 0 else pp1
                            uu = V(lambda h, ub=ub, mb=mb, base=base: h.tensor_tensor(
                                ub, psums[psS][:, :], sv(0, 128, mb + base + PAD, 512), AL.add),
                                deps=[mm1, ps_last.get("ee%d" % par2)] + mtoks)
                            ps_last[psS] = uu
                            ee = S(lambda h, ub=ub, ppb=ppb, hh=hh: h.activation(
                                ppb, ub, AF.Exp, bias=mh_ap[hh], scale=0.125),
                                deps=[uu, ps_last.get("mo%d" % par2)] + mh_toks)
                            ps_last["ee%d" % par2] = ee
                            mo = TE(lambda h, hh=hh, K0=K0, ppb=ppb, first=first, last=last: h.matmul(
                                psums[psO][0:65, :], bass.AP(tensor=scr[:].tensor,
                                                             offset=scr[:].offset + vsb_base + K0 * 130 + hh * 65,
                                                             ap=[[scr[:].ap[0][0], 128], [1, 65]]),
                                ppb, start=first, stop=last),
                                deps=[ee, lv0, lv1] + ([evtok] if first else []), inc=1)
                            ps_last["mo%d" % par2] = mo
                        acc_i += 1
                    e1 = V(lambda h: h.tensor_copy(evb[0:64, :], psums[psO][0:64, :]),
                           deps=[mo, ps_last.get("evbst")])
                    e2 = V(lambda h, Q0=Q0: h.tensor_copy(qa[64:65, Q0 * 512:(Q0 + 1) * 512], psums[psO][64:65, :]),
                           deps=[mo, ps_last.get("dnst")])
                    ps_last[psO] = e2
                    st = dma("gpsimd", oT_st[hcur * 64:hcur * 64 + 64, Q0 * 512:(Q0 + 1) * 512], evb[0:64, :],
                             deps=[fv(e1)])
                    ps_last["evbst"] = st
                    dn_toks.append(st)
                dn_st = dma("gpsimd", dn_dram[hcur:hcur + 1, :], qa[64:65, :], deps=[fv(ps_last[psO])])
                ps_last["dnst"] = dn_st
                dn_toks.append(dn_st)
                pr.barrier()
            ps_last["attn_done"] = dn_toks[-1]
            pr.barrier()
        ldn = dma("sync", sv(0, 16, 0, T), dn_dram[:, :])
        rdn = V(lambda h: h.reciprocal(sv(0, 16, 2048, T), sv(0, 16, 0, T)), deps=[ldn])
        sdn = dma("gpsimd", dn_dram[:, :], sv(0, 16, 2048, T), deps=[fv(rdn)])
        pr.barrier()

        if is0:
            ps_last["pb"] = dma("sync", sv(0, 128, 18432, 1024),
                                bass.AP(tensor=projb0.tensor, offset=0, ap=[[0, 128], [1, D]]),
                                deps=[pr.tok("vector"), pr.tok("scalar")])
        for ot in range(2):
            lw = dma("sync", bass.AP(tensor=wb1[:].tensor, offset=wb1[:].offset, ap=[[wb1[:].ap[0][0], 128], [512, 8], [1, 512]]),
                     bass.AP(tensor=pT.tensor, offset=ot * 512, ap=[[D, 128], [128 * D, 8], [1, 512]]),
                     deps=[pr.tok("tensor")])
            if ot == 0:
                ntoks = []
                for c in range(ND):
                    lo = dma("sync", sv(0, 128, 0, T), oT_st[c * 128:(c + 1) * 128, :], deps=[sdn, ntoks[-1] if ntoks else None])
                    lr0 = dma("sync", sv(0, 128, 2048, T)[0:64, :],
                              bass.AP(tensor=dn_dram.tensor, offset=2 * c * T, ap=[[0, 64], [1, T]]),
                              deps=[sdn, ntoks[-1] if ntoks else None])
                    lr1 = dma("sync", bass.AP(tensor=scr[:].tensor, offset=scr[:].offset + 2048,
                                              ap=[[scr[:].ap[0][0], 128], [1, T]])[64:128, :],
                              bass.AP(tensor=dn_dram.tensor, offset=(2 * c + 1) * T, ap=[[0, 64], [1, T]]),
                              deps=[sdn, ntoks[-1] if ntoks else None])
                    nm = V(lambda h, c=c: h.tensor_tensor(bigA[:, c * T:(c + 1) * T], sv(0, 128, 0, T),
                                                          sv(0, 128, 2048, T), AL.mult), deps=[lo, lr0, lr1])
                    ntoks.append(nm)
                pr.barrier()
            for tt in range(NT):
                pidx = tt % 2
                mms = []
                for dc in range(ND):
                    mm = TE(lambda h, dc=dc, tt=tt, pidx=pidx: h.matmul(
                        psums[pidx][:, :], bigA[:, dc * T + tt * 128: dc * T + (tt + 1) * 128],
                        wb1[:, dc * 512:(dc + 1) * 512], start=(dc == 0), stop=(dc == ND - 1)),
                        deps=([lw, ps_last.get(pidx)] if dc == 0 else []), inc=(1 if dc == ND - 1 else 0))
                    mms.append(mm)
                if is0:
                    ev = V(lambda h, tt=tt, ot=ot, pidx=pidx: h.scalar_tensor_tensor(
                        sv(0, 128, 16384 + (tt % 2) * 512, 512), psums[pidx][:, :], 1.0,
                        sv(0, 128, 18432 + ot * 512, 512), AL.bypass, AL.add),
                        deps=[mms[-1], ps_last.get("pb"), ps_last.get("qkev%d" % (tt % 2))])
                else:
                    ev = V(lambda h, tt=tt, pidx=pidx: h.tensor_copy(
                        sv(0, 128, 16384 + (tt % 2) * 512, 512), psums[pidx][:, :]),
                        deps=[mms[-1], ps_last.get("qkev%d" % (tt % 2))])
                ps_last[pidx] = ev
                st = dma("gpsimd", sb_out[tt * 128:(tt + 1) * 128, ot * 512:(ot + 1) * 512],
                    sv(0, 128, 16384 + (tt % 2) * 512, 512), deps=[fv(ev)])
                ps_last["qkev%d" % (tt % 2)] = st
                pr.barrier()
        pr.barrier()

    def load_bias_bcast(src_row, tag):
        tk = dma("sync", sv(0, 128, 18432, 1024),
                 bass.AP(tensor=src_row.tensor, offset=0, ap=[[0, 128], [1, D]]))
        ps_last[tag] = tk
        return tk

    def ffn0_body():
        s_v = sv(0, 64, 0, T)
        g0v = sv(0, 64, 2048, T)
        g1v = sv(0, 64, 4096, T)
        tmpv = sv(0, 64, 6144, T)
        s_evs = []
        lw = dma("sync", bass.AP(tensor=wb0[:].tensor, offset=wb0[:].offset, ap=[[wb0[:].ap[0][0], 128], [64, 8], [1, 64]]),
                 bass.AP(tensor=ffw_in.tensor, offset=0, ap=[[64, 128], [128 * 64, 8], [1, 64]]))
        for tt in range(NQ):
            pidx = tt % 2
            mms = []
            for dc in range(ND):
                mm = TE(lambda h, dc=dc, tt=tt, pidx=pidx: h.matmul(
                    psums[pidx][0:64, :], wb0[:, dc * 64:(dc + 1) * 64],
                    bigA[:, dc * T + tt * 512: dc * T + (tt + 1) * 512],
                    start=(dc == 0), stop=(dc == ND - 1)),
                    deps=([lw, ps_last.get(pidx)] if dc == 0 else []), inc=(1 if dc == ND - 1 else 0))
                mms.append(mm)
            ev = S(lambda h, tt=tt, pidx=pidx: h.activation(
                s_v[0:64, tt * 512:(tt + 1) * 512], psums[pidx][0:64, :], AF.Identity,
                bias=fsm[0:64, 0:1]), deps=[mms[-1]] + tfs)
            ps_last[pidx] = ev
            s_evs.append(ev)
        pr.barrier()
        gcur, gprev = g0v, g1v
        gtok = None
        for i in range(16):
            tm = S(lambda h, i=i: h.activation(tmpv, s_v, AF.Relu,
                                               bias=fsm[0:64, 17 + i:18 + i], scale=fsm[0:64, 1 + i:2 + i]),
                   deps=tfs + [pr.tok("vector")] + s_evs)
            if i == 0:
                gtok = V(lambda h: h.tensor_scalar(gcur, tmpv, fsm[0:64, 33:34], None, AL.mult), deps=[tm] + tfs)
            else:
                gtok = V(lambda h, i=i, gc=gcur, gp=gprev: h.scalar_tensor_tensor(
                    gc, tmpv, fsm[0:64, 33 + i:34 + i], gp, AL.mult, AL.add), deps=[tm, gtok])
            gcur, gprev = gprev, gcur
            pr.barrier()
        gtok = V(lambda h, gp=gprev: h.tensor_scalar(s_v, gp, fsm[0:64, 49:50], None, AL.add), deps=[gtok])
        gfin = s_v
        pr.barrier()
        lw2 = dma("sync", wb0[0:64, 0:D], ffw_out[:, :])
        for tt in range(NT):
            for ot in range(2):
                pidx = tt % 2
                mm = TE(lambda h, tt=tt, ot=ot, pidx=pidx: h.matmul(
                    psums[pidx][:, :], gfin[0:64, tt * 128:(tt + 1) * 128],
                    wb0[0:64, ot * 512:(ot + 1) * 512], start=True, stop=True),
                    deps=[lw2, gtok, ps_last.get(pidx)], inc=1)
                ev = V(lambda h, pidx=pidx, ot=ot: h.scalar_tensor_tensor(
                    sv(0, 128, 16384 + pidx * 512, 512), psums[pidx][:, :], 1.0,
                    sv(0, 128, 18432 + ot * 512, 512), AL.bypass, AL.add),
                    deps=[mm, ps_last.get("ob"), ps_last.get("qkev%d" % pidx)])
                ps_last[pidx] = ev
                st = dma("gpsimd", sb_out[tt * 128:(tt + 1) * 128, ot * 512:(ot + 1) * 512],
                    sv(0, 128, 16384 + pidx * 512, 512), deps=[fv(ev)])
                ps_last["qkev%d" % pidx] = st
            pr.barrier()
        pr.barrier()

    def allreduce_max(vals_toks, n):
        cz = V(lambda h: h.memset(sc[6][0:1, 0:8], 0.0))
        czs = dma("gpsimd", cc_in[0:1, 0:8], sc[6][0:1, 0:8], deps=[fv(cz)])
        toks = []
        for idx, tok in enumerate(vals_toks):
            toks.append(dma("gpsimd", cc_in[0:1, idx:idx + 1], tok[0], deps=[fv(tok[1]), czs]))
        ccall = G(lambda h: h.collective_compute(
            "AllReduce", AL.max, ins=[cc_in[0:1, 0:8]], outs=[cc_out[0:1, 0:8]],
            replica_groups=[[0, 1, 2, 3, 4, 5, 6, 7]]), deps=toks, inc=1)
        lb = dma("sync", sc[6][:, 0:8], bass.AP(tensor=cc_out.tensor, offset=0, ap=[[0, 128], [1, 8]]),
                 deps=[ccall])
        if DBG:
            allreduce_max.n = getattr(allreduce_max, "n", 0)
            dma("gpsimd", dbg["d_cc"][allreduce_max.n:allreduce_max.n + 1, :], cc_out[0:1, 0:8], deps=[ccall])
            allreduce_max.n += 1
        return lb

    def qdq_chain(dst, srcap, rs_ap, s_ap, t_y, t_a, t_b, p, w, deps):
        ty = V(lambda h: h.tensor_scalar(t_y, srcap, rs_ap, None, AL.mult), deps=deps)
        ta = V(lambda h: h.tensor_scalar(t_a, t_y, C32, C32, AL.add, AL.subtract), deps=[ty])
        tb = V(lambda h: h.tensor_scalar(t_b, t_y, C64, C64, AL.add, AL.subtract), deps=[ty])
        tq = V(lambda h: h.scalar_tensor_tensor(t_y, t_a, e_t[:p, 0:1], t_b, AL.mult, AL.add),
               deps=[ta, tb, tk_e])
        return V(lambda h: h.tensor_scalar(dst, t_y, s_ap, None, AL.mult), deps=[tq])

    def scale_from_amax(col_out, amax_src_ap, deps):
        tk = V(lambda h: h.tensor_copy(sc[col_out + 1][:, 0:1], amax_src_ap), deps=deps)
        wd = 1
        while wd < 8:
            tk = V(lambda h, wd=wd: h.tensor_copy(sc[col_out + 1][:, wd:2 * wd], sc[col_out + 1][:, 0:wd]),
                   deps=[tk])
            wd *= 2
        t1 = V(lambda h: h.tensor_scalar(sc[col_out][:, 0:8], sc[col_out + 1][:, 0:8],
                                         1.0 / 32767.0, 1e-12, AL.mult, AL.max), deps=[tk])
        t2 = V(lambda h: h.reciprocal(sc[col_out + 1][:, 0:8], sc[col_out][:, 0:8]), deps=[t1])
        return [t1, fv(t2)]

    def global_amax(chunk_reduce_toks, colbuf_ap, tag):
        tr = TE(lambda h: h.transpose(psums[6][0:8, 0:128], colbuf_ap, ident[:]),
                deps=chunk_reduce_toks + [tk_id, ps_last.get(6)])
        rd = V(lambda h: h.reduce_max(sc[5][0:8, 0:1], psums[6][0:8, 0:128], axis=AX.X), deps=[tr])
        ps_last[6] = rd
        rd2 = fv(rd)
        return (sc[5][0:1, 0:1], rd2)

    def ffn1_weight_dequant():
        specs = [(wup, wdqT_up, 24, D), (wgate, wdqT_gt, 24, D), (wdown, wdqT_dn, 8, 3 * D)]
        for w_in, w_out, nchunk, W in specs:
            nblk32 = W // 32
            for ch in range(nchunk):
                lw = dma("sync", wb0[:, 0:W], w_in[ch * 128:(ch + 1) * 128, :], deps=[ps_last.get("dq")])
                bm = V(lambda h, W=W, nblk32=nblk32: h.reduce_max(
                    sv(0, 128, 0, nblk32),
                    wb0[:, 0:W].rearrange("p (a b) -> p a b", b=32), axis=AX.X,
                    apply_absolute_value=True), deps=[lw])
                s1 = V(lambda h, nblk32=nblk32: h.tensor_scalar(
                    sv(0, 128, 128, nblk32), sv(0, 128, 0, nblk32), 1.0 / 7.0, 1e-12, AL.mult, AL.max), deps=[bm])
                s1b = V(lambda h: h.engine_nop(), deps=[s1])
                s2 = V(lambda h, nblk32=nblk32: h.reciprocal(sv(0, 128, 256, nblk32), sv(0, 128, 128, nblk32)),
                       deps=[s1b])
                exp_toks = []
                for colsrc, coldst in ((128, 512), (256, 512 + W)):
                    tcur = V(lambda h, colsrc=colsrc, coldst=coldst, nblk32=nblk32: h.tensor_copy(
                        bass.AP(tensor=scr[:].tensor, offset=scr[:].offset + coldst,
                                ap=[[scr[:].ap[0][0], 128], [32, nblk32], [1, 1]]),
                        sv(0, 128, colsrc, nblk32)), deps=[s2])
                    wd = 1
                    while wd < 32:
                        wn = min(wd, 32 - wd)
                        tcur = V(lambda h, coldst=coldst, wd=wd, wn=wn, nblk32=nblk32: h.tensor_copy(
                            bass.AP(tensor=scr[:].tensor, offset=scr[:].offset + coldst + wd,
                                    ap=[[scr[:].ap[0][0], 128], [32, nblk32], [1, wn]]),
                            bass.AP(tensor=scr[:].tensor, offset=scr[:].offset + coldst,
                                    ap=[[scr[:].ap[0][0], 128], [32, nblk32], [1, wn]])), deps=[tcur])
                        wd += wn
                    exp_toks.append(tcur)
                c_y, c_a, c_b, c_o = 512 + 2 * W, 512 + 3 * W, 512 + 4 * W, 512 + 5 * W
                ty = V(lambda h, W=W, c_y=c_y: h.tensor_tensor(
                    sv(0, 128, c_y, W), wb0[:, 0:W], sv(0, 128, 512 + W, W), AL.mult), deps=exp_toks + [lw])
                ta = V(lambda h, W=W, c_y=c_y, c_a=c_a: h.tensor_scalar(
                    sv(0, 128, c_a, W), sv(0, 128, c_y, W), C32, C32, AL.add, AL.subtract), deps=[ty])
                tb = V(lambda h, W=W, c_y=c_y, c_b=c_b: h.tensor_scalar(
                    sv(0, 128, c_b, W), sv(0, 128, c_y, W), C64, C64, AL.add, AL.subtract), deps=[ty])
                tq = V(lambda h, W=W, c_a=c_a, c_b=c_b, c_y=c_y: h.scalar_tensor_tensor(
                    sv(0, 128, c_y, W), sv(0, 128, c_a, W), e_t[:, 0:1], sv(0, 128, c_b, W),
                    AL.mult, AL.add), deps=[ta, tb, tk_e])
                tw = V(lambda h, W=W, c_y=c_y, c_o=c_o: h.tensor_tensor(
                    sv(0, 128, c_o, W), sv(0, 128, c_y, W), sv(0, 128, 512, W), AL.mult), deps=[tq])
                last_st = None
                for j in range(W // 128):
                    pidx = j % 2
                    tr = TE(lambda h, c_o=c_o, j=j, pidx=pidx: h.transpose(
                        psums[pidx][:, 0:128], sv(0, 128, c_o + j * 128, 128), ident[:]),
                        deps=[tw, tk_id, ps_last.get(pidx)])
                    cp = V(lambda h, pidx=pidx: h.tensor_copy(
                        sv(0, 128, 256 + pidx * 128, 128), psums[pidx][:, 0:128]),
                        deps=[tr, ps_last.get("dqev%d" % pidx)])
                    ps_last[pidx] = cp
                    last_st = dma("gpsimd", w_out[j * 128:(j + 1) * 128, ch * 128:(ch + 1) * 128],
                                  sv(0, 128, 256 + pidx * 128, 128), deps=[fv(cp)])
                    ps_last["dqev%d" % pidx] = last_st
                ps_last["dq"] = last_st
            pr.barrier()

    def ffn1_body():
        zms = V(lambda h: h.memset(sv(0, 128, 8192, 32), 0.0))
        rtoks = [zms]
        for c in range(ND):
            rt = V(lambda h, c=c: h.reduce_max(sv(0, 128, 8192 + c, 1), bigA[:, c * T:(c + 1) * T],
                                               axis=AX.X, apply_absolute_value=True), deps=[zms])
            rtoks.append(rt)
        rall = V(lambda h: h.reduce_max(sv(0, 128, 8200, 1), sv(0, 128, 8192, 8), axis=AX.X), deps=rtoks)
        rallb = V(lambda h: h.tensor_copy(sv(0, 128, 8208, 8), sv(0, 128, 8200, 8)), deps=[rall])
        gx = global_amax([rallb], sv(0, 128, 8208, 8), "x")
        lb = allreduce_max([gx], 1)
        stoks = scale_from_amax(0, sc[6][:, 0:1], [lb])
        pr.barrier()
        for c in range(ND):
            qdq_chain(bigA[:, c * T:(c + 1) * T], bigA[:, c * T:(c + 1) * T],
                      sc[1][:, 0:1], sc[0][:, 0:1], sv(0, 128, 0, T), sv(0, 128, 2048, T),
                      sv(0, 128, 4096, T), 128, T, deps=stoks + [pr.tok("vector")])
        pr.barrier()
        if DBG:
            dma("gpsimd", dbg["d_xq"][:, :], bigA[:, 0:T], deps=[pr.tok("vector")])
        for (wT_dram, st_dram, colpos) in ((wdqT_up, up_st, 0), (wdqT_gt, gt_st, 1)):
            am0 = V(lambda h: h.memset(sv(0, 128, 12800, 40), 0.0), deps=[ps_last.get("r2t")])
            am_toks = [am0]
            for ic in range(24):
                wcol = wb0 if ic % 2 == 0 else wb1
                lw = dma("sync", bass.AP(tensor=wcol[:].tensor, offset=wcol[:].offset,
                                         ap=[[wcol[:].ap[0][0], 128], [128, 8], [1, 128]]),
                         bass.AP(tensor=wT_dram.tensor, offset=ic * 128, ap=[[3 * D, 128], [128 * 3 * D, 8], [1, 128]]),
                         deps=[ps_last.get("u%d" % (ic % 2))])
                for tt in range(NQ):
                    pidx = 4 + tt % 2
                    mms = []
                    for dc in range(ND):
                        mm = TE(lambda h, wcol=wcol, dc=dc, tt=tt, pidx=pidx: h.matmul(
                            psums[pidx][:, :], wcol[:, dc * 128:(dc + 1) * 128],
                            bigA[:, dc * T + tt * 512: dc * T + (tt + 1) * 512],
                            start=(dc == 0), stop=(dc == ND - 1)),
                            deps=([lw, ps_last.get(pidx)] if dc == 0 else []), inc=(1 if dc == ND - 1 else 0))
                        mms.append(mm)
                    ev = S(lambda h, tt=tt, pidx=pidx, ic=ic: h.copy(
                        sv(0, 128, 8704 + (ic % 2) * T + tt * 512, 512), psums[pidx][:, :]),
                        deps=[mms[-1], ps_last.get("rtu%d" % (ic % 2))])
                    ps_last[pidx] = ev
                    am_toks.append(ev)
                rt = V(lambda h, ic=ic: h.reduce_max(sv(0, 128, 12800 + ic, 1),
                                                     sv(0, 128, 8704 + (ic % 2) * T, T), axis=AX.X,
                                                     apply_absolute_value=True), deps=am_toks[-4:] + [am0])
                ps_last["rtu%d" % (ic % 2)] = rt
                st = dma("gpsimd", st_dram[ic * 128:(ic + 1) * 128, :], sv(0, 128, 8704 + (ic % 2) * T, T),
                         deps=[fs(am_toks[-1])] + am_toks[-4:])
                ps_last["u%d" % (ic % 2)] = st
                am_toks.append(rt)
            r1 = V(lambda h: h.reduce_max(sv(0, 128, 12832, 1), sv(0, 128, 12800, 24), axis=AX.X),
                   deps=am_toks)
            r2 = V(lambda h, colpos=colpos: h.tensor_copy(sv(0, 128, 12840 + 8 * colpos, 8), sv(0, 128, 12832, 8)),
                   deps=[r1])
            ps_last["amax%d" % colpos] = r2
            ps_last["r2t"] = r2
            pr.barrier()
        gup = global_amax([ps_last["amax0"]], sv(0, 128, 12840, 8), "up")
        up_tok = dma("gpsimd", bc_dram[1:2, 0:8], sc[5][0:1, 0:8], deps=[gup[1]])
        ggt = global_amax([ps_last["amax1"]], sv(0, 128, 12848, 8), "gt")
        gt_tok = dma("gpsimd", bc_dram[2:3, 0:8], sc[5][0:1, 0:8], deps=[ggt[1], up_tok])
        lb1 = dma("sync", sc[4][0:1, 0:8], bc_dram[1:2, 0:8], deps=[up_tok])
        lb2 = dma("sync", sc[4][1:2, 0:8], bc_dram[2:3, 0:8], deps=[gt_tok])
        lb = allreduce_max([(sc[4][0:1, 0:1], lb1), (sc[4][1:2, 0:1], lb2)], 2)
        s_up = scale_from_amax(0, sc[6][:, 0:1], [lb])
        s_gt = scale_from_amax(2, sc[6][:, 1:2], [lb])
        pr.barrier()
        ub = sv(0, 128, 0, T)
        gb = sv(0, 128, 2048, T)
        hb = sv(0, 128, 4096, T)
        tb_ = sv(0, 128, 6144, T)
        am0 = V(lambda h: h.memset(sv(0, 128, 12800, 40), 0.0), deps=[ps_last.get("r2t")])
        am_toks = [am0]
        for ic in range(24):
            lg = dma("sync", gb, gt_st[ic * 128:(ic + 1) * 128, :], deps=am_toks[-2:])
            q1 = qdq_chain(gb, gb, sc[3][:, 0:1], sc[2][:, 0:1], ub, hb, tb_, 128, T,
                           deps=[lg] + s_gt)
            sg = S(lambda h: h.activation(ub, gb, AF.Sigmoid), deps=[q1])
            y2 = V(lambda h: h.tensor_single_scalar(hb, ub, 65536.0, AL.mult), deps=[sg])
            a2 = V(lambda h: h.tensor_scalar(tb_, hb, C32, C32, AL.add, AL.subtract), deps=[y2])
            b2 = V(lambda h: h.tensor_scalar(ub, hb, C64, C64, AL.add, AL.subtract), deps=[y2])
            q2 = V(lambda h: h.scalar_tensor_tensor(hb, tb_, e_t[:, 0:1], ub, AL.mult, AL.add),
                   deps=[a2, b2, tk_e])
            sq = V(lambda h: h.tensor_scalar(ub, hb, 65535.0, 2.0 ** -16, AL.min, AL.mult), deps=[q2])
            g2 = V(lambda h: h.tensor_tensor(hb, gb, ub, AL.mult), deps=[sq])
            rt = V(lambda h, ic=ic: h.reduce_max(sv(0, 128, 12800 + ic, 1), hb, axis=AX.X,
                                                 apply_absolute_value=True), deps=[g2, am0])
            st = dma("gpsimd", g2_st[ic * 128:(ic + 1) * 128, :], hb, deps=[fv(g2)])
            am_toks.append(rt)
            am_toks.append(st)
        r1 = V(lambda h: h.reduce_max(sv(0, 128, 12832, 1), sv(0, 128, 12800, 24), axis=AX.X), deps=am_toks)
        r2 = V(lambda h: h.tensor_copy(sv(0, 128, 12840, 8), sv(0, 128, 12832, 8)), deps=[r1])
        ps_last["r2t"] = r2
        gg2 = global_amax([r2], sv(0, 128, 12840, 8), "g2")
        lb = allreduce_max([gg2], 1)
        s_g2 = scale_from_amax(2, sc[6][:, 0:1], [lb])
        pr.barrier()
        am0 = V(lambda h: h.memset(sv(0, 128, 12800, 40), 0.0), deps=[ps_last.get("r2t")])
        am_toks = [am0]
        for ic in range(24):
            lg = dma("sync", gb, g2_st[ic * 128:(ic + 1) * 128, :], deps=am_toks[-2:])
            lu = dma("sync", ub, up_st[ic * 128:(ic + 1) * 128, :], deps=am_toks[-2:])
            q1 = qdq_chain(gb, gb, sc[3][:, 0:1], sc[2][:, 0:1], hb, tb_, sv(0, 128, 8192, T), 128, T,
                           deps=[lg] + s_g2)
            q2 = qdq_chain(ub, ub, sc[1][:, 0:1], sc[0][:, 0:1], hb, tb_, sv(0, 128, 8192, T), 128, T,
                           deps=[lu, q1] + s_up)
            oo = V(lambda h: h.tensor_tensor(hb, gb, ub, AL.mult), deps=[q2])
            rt = V(lambda h, ic=ic: h.reduce_max(sv(0, 128, 12800 + ic, 1), hb, axis=AX.X,
                                                 apply_absolute_value=True), deps=[oo, am0])
            st = dma("gpsimd", gt_st[ic * 128:(ic + 1) * 128, :], hb, deps=[fv(oo)])
            am_toks.append(rt)
            am_toks.append(st)
        r1 = V(lambda h: h.reduce_max(sv(0, 128, 12832, 1), sv(0, 128, 12800, 24), axis=AX.X), deps=am_toks)
        r2 = V(lambda h: h.tensor_copy(sv(0, 128, 12840, 8), sv(0, 128, 12832, 8)), deps=[r1])
        ps_last["r2t"] = r2
        go = global_amax([r2], sv(0, 128, 12840, 8), "o")
        lb = allreduce_max([go], 1)
        s_o = scale_from_amax(0, sc[6][:, 0:1], [lb])
        pr.barrier()
        for grp in range(3):
            qtoks = []
            for j in range(ND):
                ic = grp * ND + j
                lo = dma("sync", gb, gt_st[ic * 128:(ic + 1) * 128, :], deps=qtoks[-1:])
                qt = qdq_chain(bigA[:, j * T:(j + 1) * T], gb, sc[1][:, 0:1], sc[0][:, 0:1],
                               ub, hb, tb_, 128, T, deps=[lo] + s_o)
                qtoks.append(qt)
            lw = dma("sync", bass.AP(tensor=wb0[:].tensor, offset=wb0[:].offset,
                                     ap=[[wb0[:].ap[0][0], 128], [1024, 4], [1, 1024]]),
                     bass.AP(tensor=wdqT_dn.tensor, offset=grp * 8 * 128 * D,
                             ap=[[D, 128], [128 * D, 4], [1, 1024]]))
            lw2 = dma("sync", bass.AP(tensor=wb1[:].tensor, offset=wb1[:].offset,
                                      ap=[[wb1[:].ap[0][0], 128], [1024, 4], [1, 1024]]),
                      bass.AP(tensor=wdqT_dn.tensor, offset=(grp * 8 + 4) * 128 * D,
                              ap=[[D, 128], [128 * D, 4], [1, 1024]]))
            pr.barrier()
            for tt in range(NT):
                for ot in range(2):
                    pidx = (2 * tt + ot) % 2
                    mms = []
                    for j in range(ND):
                        wcol = wb0 if j < 4 else wb1
                        jj = j % 4
                        mm = TE(lambda h, wcol=wcol, j=j, jj=jj, tt=tt, ot=ot, pidx=pidx: h.matmul(
                            psums[pidx][:, :], bigA[:, j * T + tt * 128: j * T + (tt + 1) * 128],
                            wcol[:, jj * 1024 + ot * 512: jj * 1024 + ot * 512 + 512],
                            start=(j == 0), stop=(j == ND - 1)),
                            deps=([lw, lw2, ps_last.get(pidx)] if j == 0 else []), inc=(1 if j == ND - 1 else 0))
                        mms.append(mm)
                    ev = V(lambda h, pidx=pidx: h.tensor_copy(sv(0, 128, 16384 + pidx * 512, 512),
                                                              psums[pidx][:, :]),
                           deps=[mms[-1], ps_last.get("qkev%d" % pidx)])
                    ps_last[pidx] = ev
                    evf = fv(ev)
                    if grp == 0:
                        st = dma("gpsimd", sb_out[tt * 128:(tt + 1) * 128, ot * 512:(ot + 1) * 512],
                            sv(0, 128, 16384 + pidx * 512, 512), deps=[evf])
                    else:
                        st = pr.emit("gpsimd", lambda h, tt=tt, ot=ot, pidx=pidx: h.dma_start(
                            out=sb_out[tt * 128:(tt + 1) * 128, ot * 512:(ot + 1) * 512],
                            in_=sv(0, 128, 16384 + pidx * 512, 512), accum_op=AL.add),
                            deps=[evf, ("gq", pr.cnt["gq"])], inc=16, sem="gq")
                    ps_last["qkev%d" % pidx] = st
            pr.barrier()

    if NBLK >= 1:
        pre_template(0)
        load_bias_bcast(projb0, "pb")
        tvb = dma("sync", sv(0, 128, 18432 + 0, 1024), bass.AP(tensor=vb0.tensor, offset=0, ap=[[0, 128], [1, D]]))
        ps_last["vb"] = tvb
        attn_body(0)
        post_template(0)
    if NBLK >= 2:
        pre_template(1)
        load_bias_bcast(ffb_out, "ob")
        ffn0_body()
        post_template(1)
    if NBLK >= 3:
        pre_template(2)
        attn_body(1)
        post_template(2)
    if NBLK >= 4:
        ffn1_weight_dequant()
        pre_template(3)
        ffn1_body()
        post_template(3)

    if DBG:
        pr.barrier()
        dma("gpsimd", dbg["d_q"][:, :], q_st[:, :])
        dma("gpsimd", dbg["d_k"][:, :], k_st[:, :])
        dma("gpsimd", dbg["d_v"][:, :], v_st[:, :])
        dma("gpsimd", dbg["d_oT"][:, :], oT_st[:, :])
        dma("gpsimd", dbg["d_dn"][:, :], dn_dram[:, :])
        dma("gpsimd", dbg["d_sbout"][:, :], sb_out[:, :])
        dma("gpsimd", dbg["d_mh"][0:1, 0:8], bc_dram[0:1, 0:8])
        dma("gpsimd", dbg["d_vec"][:, :], vecflat[0:1, 0:VW])
        dma("gpsimd", dbg["d_m0"][:, :], m_dram[0:128, 0:MW])
        dma("gpsimd", dbg["d_xres"][:, :], x_res[:, :])
        if NBLK >= 4:
            dma("gpsimd", dbg["d_up0"][:, :], up_st[0:128, :])
            dma("gpsimd", dbg["d_oraw"][:, :], gt_st[0:128, :])
            dma("gpsimd", dbg["d_g2"][:, :], g2_st[0:128, :])
            dma("gpsimd", dbg["d_wdq"][:, :], wdqT_up[0:128, :])


    pr.emit("gpsimd", None, deps=[("gq", pr.cnt["gq"]), ("gpsimd", pr.cnt["gpsimd"])])

    from contextlib import ExitStack
    with ExitStack() as st:
        sems = {s: st.enter_context(nc.semaphore("sem_" + s)) for s in SEMS}
        block = st.enter_context(nc.Block())
        for eng in ENGINES:
            def make(eng):
                def body(h):
                    for fn, waits, inc, semname in pr.steps[eng]:
                        for peng, val in waits:
                            h.wait_ge(sems[peng], val)
                        if fn is None:
                            continue
                        ins = fn(h)
                        if inc:
                            ins.then_inc(sems[semname], inc)
                return body
            getattr(block, eng)(make(eng))
    return nc


_NC_CACHE = None


def kernel(L0_attn_x, L0_attn_qkv_w, L0_attn_qkv_b, L0_attn_proj_w, L0_attn_proj_b,
           L0_attn_rel_pos_emb, L0_ffn_Win, L0_ffn_bin_, L0_ffn_W1, L0_ffn_b1,
           L0_ffn_W2, L0_ffn_b2, L0_ffn_Wout, L0_ffn_bout,
           L1_attn_w_qkv, L1_attn_w_out, L1_ffn_w_gate, L1_ffn_w_up, L1_ffn_w_down):
    global _NC_CACHE
    if _NC_CACHE is None:
        _NC_CACHE = build_program()
    nc = _NC_CACHE
    f32 = np.float32
    common = {
        "qkvwT0": np.ascontiguousarray(np.asarray(L0_attn_qkv_w, f32).T),
        "qkvb0": np.ascontiguousarray(np.asarray(L0_attn_qkv_b, f32).reshape(24, 128).T),
        "vb0": np.ascontiguousarray(np.asarray(L0_attn_qkv_b, f32)[2048:3072].reshape(1, D)),
        "projT0": np.ascontiguousarray(np.asarray(L0_attn_proj_w, f32).T),
        "projb0": np.ascontiguousarray(np.asarray(L0_attn_proj_b, f32).reshape(1, D)),
        "relpos": np.ascontiguousarray(np.asarray(L0_attn_rel_pos_emb, f32)),
        "ffw_in": np.ascontiguousarray(np.asarray(L0_ffn_Win, f32)),
        "ffb_in": np.ascontiguousarray(np.asarray(L0_ffn_bin_, f32).reshape(64, 1)),
        "ffw1": np.ascontiguousarray(np.asarray(L0_ffn_W1, f32)),
        "ffb1": np.ascontiguousarray(np.asarray(L0_ffn_b1, f32)),
        "ffw2": np.ascontiguousarray(np.asarray(L0_ffn_W2, f32)),
        "ffb2": np.ascontiguousarray(np.asarray(L0_ffn_b2, f32).reshape(64, 1)),
        "ffw_out": np.ascontiguousarray(np.asarray(L0_ffn_Wout, f32)),
        "ffb_out": np.ascontiguousarray(np.asarray(L0_ffn_bout, f32).reshape(1, D)),
        "qkvwT1": np.ascontiguousarray(np.asarray(L1_attn_w_qkv, f32).T),
        "woutT1": np.ascontiguousarray(np.asarray(L1_attn_w_out, f32).T),
        "wgate": np.ascontiguousarray(np.asarray(L1_ffn_w_gate, f32)),
        "wup": np.ascontiguousarray(np.asarray(L1_ffn_w_up, f32)),
        "wdown": np.ascontiguousarray(np.asarray(L1_ffn_w_down, f32)),
    }
    x = np.asarray(L0_attn_x, f32)
    in_maps = [dict(common, x_in=np.ascontiguousarray(x[c])) for c in range(8)]
    res = run_bass_kernel_spmd(nc, in_maps, core_ids=list(range(8)))
    kernel.last_results = res.results
    return np.stack([res.results[c]["y_out"] for c in range(8)], axis=0)


# version 17
# speedup vs baseline: 1.0030x; 1.0030x over previous
import numpy as np
import concourse.bass as bass
import concourse.mybir as mybir
from concourse.bass_utils import run_bass_kernel_spmd

F32 = mybir.dt.float32
I32 = mybir.dt.int32
AL = mybir.AluOpType
AF = mybir.ActivationFunctionType
AX = mybir.AxisListType

B, T, D, H, DH = 8, 2048, 1024, 16, 64
NT, NQ, ND = T // 128, T // 512, D // 128
C32 = 12582912.0
C64 = 6755399441055744.0
BIGNEG = -1.0e30
EPS = 1e-5
PAD = 512
VW = 2688
MW = 2048 + PAD
NBLK = 4

ENGINES = ("sync", "tensor", "vector", "scalar", "gpsimd")
SEMS = ENGINES + ("gq",)


class Prog:
    def __init__(self):
        self.steps = {e: [] for e in ENGINES}
        self.cnt = {s: 0 for s in SEMS}
        self.waited = {(c, p): 0 for c in ENGINES for p in SEMS}

    def emit(self, eng, fn, deps=(), inc=0, sem=None):
        waits = {}
        for tok in deps:
            if tok is None:
                continue
            peng, val = tok
            if val > self.waited[(eng, peng)]:
                waits[peng] = max(waits.get(peng, 0), val)
        for peng, val in waits.items():
            self.waited[(eng, peng)] = val
        tok = None
        if inc:
            s = sem or eng
            self.cnt[s] += inc
            tok = (s, self.cnt[s])
        self.steps[eng].append((fn, sorted(waits.items()), inc, sem or eng))
        return tok

    def tok(self, eng):
        return (eng, self.cnt[eng])

    def barrier(self):
        toks = [(s, self.cnt[s]) for s in SEMS]
        for e in ENGINES:
            for peng, val in toks:
                if peng != e and val > self.waited[(e, peng)]:
                    self.waited[(e, peng)] = val
                    self.steps[e].append((None, [(peng, val)], 0, e))


def build_program():
    nc = bass.Bass("TRN2", target_bir_lowering=False, debug=False)

    def din(name, shape):
        return nc.dram_tensor(name, list(shape), F32, kind="ExternalInput").ap()

    x_in = din("x_in", (T, D))
    qkvwT0 = din("qkvwT0", (D, 3 * D))
    qkvb0 = din("qkvb0", (128, 24))
    vb0 = din("vb0", (1, D))
    projT0 = din("projT0", (D, D))
    projb0 = din("projb0", (1, D))
    relpos = din("relpos", (2047, D))
    ffw_in = din("ffw_in", (D, 64))
    ffb_in = din("ffb_in", (64, 1))
    ffw1 = din("ffw1", (64, 16))
    ffb1 = din("ffb1", (64, 16))
    ffw2 = din("ffw2", (64, 16))
    ffb2 = din("ffb2", (64, 1))
    ffw_out = din("ffw_out", (64, D))
    ffb_out = din("ffb_out", (1, D))
    qkvwT1 = din("qkvwT1", (D, 3 * D))
    woutT1 = din("woutT1", (D, D))
    wgate = din("wgate", (3 * D, D))
    wup = din("wup", (3 * D, D))
    wdown = din("wdown", (D, 3 * D))
    y_out = nc.dram_tensor("y_out", [T, D], F32, kind="ExternalOutput").ap()
    DBG = False
    dbg = {}
    if DBG:
        for nm, shp in (("d_q", (D, T)), ("d_k", (D, T)), ("d_v", (T, D)), ("d_oT", (D, T)),
                        ("d_dn", (16, T)), ("d_sbout", (T, D)), ("d_mh", (1, 32)), ("d_vec", (1, VW)), ("d_m0", (128, MW)), ("d_tab", (16, 2047)), ("d_xres", (T, D)),
                        ("d_up0", (128, T)), ("d_oraw", (128, T)), ("d_g2", (128, T)),
                        ("d_wdq", (128, 3 * D)), ("d_cc", (4, 8)), ("d_xq", (128, T))):
            dbg[nm] = nc.dram_tensor(nm, list(shp), F32, kind="ExternalOutput").ap()

    def dsc(name, shape):
        return nc.dram_tensor(name, list(shape), F32, kind="Internal").ap()

    x_res = dsc("x_res", (T, D))
    sb_out = dsc("sb_out", (T, D))
    q_st = dsc("q_st", (D, T))
    k_st = dsc("k_st", (D, T))
    v_st = dsc("v_st", (T, D))
    oT_st = dsc("oT_st", (D, T))
    vec_dram = dsc("vec_dram", (18, VW))
    vecflat = dsc("vecflat", (1, 18 * VW + 128))
    m_dram = dsc("m_dram", (128, 18 * VW))
    dn_dram = dsc("dn_dram", (16, T))
    bc_dram = dsc("bc_dram", (4, 8))
    bmax_dram = dsc("bmax_dram", (16, 8))
    cc_in = dsc("cc_in", (1, 8))
    cc_out = dsc("cc_out", (1, 8))
    up_st = dsc("up_st", (3 * D, T))
    gt_st = dsc("gt_st", (3 * D, T))
    g2_st = dsc("g2_st", (3 * D, T))
    wdqT_up = dsc("wdqT_up", (D, 3 * D))
    wdqT_gt = dsc("wdqT_gt", (D, 3 * D))
    wdqT_dn = dsc("wdqT_dn", (3 * D, D))

    def sbuf(name, shape, dtype=F32):
        return nc.alloc_sbuf_tensor(name, list(shape), dtype)

    bigA = sbuf("bigA", (128, ND * T))
    wb0 = sbuf("wb0", (128, 4096))
    wb1 = sbuf("wb1", (128, 4096))
    scr = sbuf("scr", (128, 19456))
    ident = sbuf("ident", (128, 128))
    e_t = sbuf("e_t", (128, 8))
    e_t0 = sbuf("e_t0", (128, 8))
    sc = [sbuf("sc%d" % i, (128, 16)) for i in range(8)]
    bias_sb = sbuf("bias_sb", (128, 24))
    fsm = sbuf("fsm", (64, 72))

    psums = [nc.alloc_psum_tensor("ps%d" % i, [128, 512], F32) for i in range(7)]
    ps_last = {}

    pr = Prog()

    def dma(qeng, out, in_, deps=()):
        qs = "gq" if qeng == "gpsimd" else qeng
        gate = (qs, pr.cnt[qs])
        return pr.emit(qeng, lambda h, o=out, i=in_: h.dma_start(out=o, in_=i),
                       deps=list(deps) + [gate], inc=16, sem=qs)

    def V(fn, deps=(), inc=1):
        return pr.emit("vector", fn, deps=deps, inc=inc)

    def S(fn, deps=(), inc=1):
        return pr.emit("scalar", fn, deps=deps, inc=inc)

    def TE(fn, deps=(), inc=1):
        return pr.emit("tensor", fn, deps=deps, inc=inc)

    def G(fn, deps=(), inc=1):
        return pr.emit("gpsimd", fn, deps=deps, inc=inc)

    def sv(p0, np_, c0, nc_):
        return scr[p0:p0 + np_, c0:c0 + nc_]

    fence_prev = {"v": None, "s": None}

    def fv(tok):
        t = V(lambda h: h.memset(junkV[:, 0:8], 1.0), deps=[tok, fence_prev["v"]])
        fence_prev["v"] = t
        return t

    def fs(tok):
        t = S(lambda h: h.copy(junkS[:, 0:8], e_t[:, 0:8]), deps=[tok, fence_prev["s"]])
        fence_prev["s"] = t
        return t

    t0 = V(lambda h: h.memset(e_t0[:], 1.0))
    t1 = V(lambda h: h.tensor_single_scalar(e_t[:, 0:8], e_t0[:, 0:8], 2.0 ** -30, AL.add), deps=[t0])
    t2 = V(lambda h: h.tensor_single_scalar(e_t0[:, 0:8], e_t[:, 0:8], 1.0, AL.is_le), deps=[t1])
    tk_e = V(lambda h: h.tensor_copy(e_t[:, 0:8], e_t0[:, 0:8]), deps=[t2])
    tk_eps = V(lambda h: h.memset(sc[7][:, 8:16], EPS))
    ones8 = sbuf("ones8", (128, 8))
    junkV = sbuf("junkV", (128, 8))
    junkS = sbuf("junkS", (128, 8))
    tk_ones = V(lambda h: h.memset(ones8[:], 1.0))

    iotv = sv(0, 128, 0, 128).bitcast(I32)
    tio = G(lambda h: h.iota(iotv, [[1, 128]], base=0, channel_multiplier=-1))
    tk_id = V(lambda h: h.tensor_single_scalar(ident[:], iotv, 0.0, AL.is_equal), deps=[tio])

    tk_qb = dma("sync", bias_sb[:, :], qkvb0[:, :])
    tfs = [dma("sync", fsm[0:64, 0:1], ffb_in[:, :]),
           dma("sync", fsm[0:64, 1:17], ffw1[:, :]),
           dma("sync", fsm[0:64, 17:33], ffb1[:, :]),
           dma("sync", fsm[0:64, 33:49], ffw2[:, :]),
           dma("sync", fsm[0:64, 49:50], ffb2[:, :])]

    tabT = sv(0, 16, 4096, 2047)
    prev_tr = {0: None, 1: None}
    prev_rd = {0: None, 1: None}
    for rc in range(16):
        rows = 128 if rc < 15 else 2047 - 15 * 128
        par = rc % 2
        c0 = par * 1280
        ld = dma("sync", wb0[:rows, c0:c0 + D], relpos[rc * 128: rc * 128 + rows, :],
                 deps=[prev_rd[par]])
        rd = V(lambda h, rows=rows, c0=c0: h.reduce_sum(
            wb0[:rows, c0 + D:c0 + D + 16], wb0[:rows, c0:c0 + D].rearrange("p (a b) -> p a b", b=64), axis=AX.X),
            deps=[ld, prev_tr[par]])
        prev_rd[par] = rd
        tr = TE(lambda h, rows=rows, c0=c0: h.transpose(psums[6][0:16, 0:rows], wb0[:rows, c0 + D:c0 + D + 16],
                                                        ident[:rows, :rows]),
                deps=[rd, tk_id, ps_last.get(6)])
        prev_tr[par] = tr
        cp = V(lambda h, rc=rc, rows=rows: h.tensor_copy(
            sv(0, 16, 4096 + rc * 128, rows), psums[6][0:16, 0:rows]), deps=[tr])
        ps_last[6] = cp

    vecp = sv(0, 16, 6800, VW)
    OFF0 = PAD + 127
    tv0 = V(lambda h: h.memset(sv(0, 32, 6800, VW), BIGNEG * 8.0))
    tv1 = V(lambda h: h.tensor_single_scalar(
        sv(0, 16, 6800 + OFF0, 1024), sv(0, 16, 4096 + 1023, 1024), 8.0, AL.mult),
        deps=[tv0, ps_last[6]])
    tv2 = tv1
    wdt = 0
    tv2 = V(lambda h: h.tensor_copy(sv(0, 16, 6800 + OFF0 + 1024, 1), sv(0, 16, 6800 + OFF0 + 1023, 1)), deps=[tv1])
    wdt = 1
    while wdt < 1024:
        wn = min(wdt, 1024 - wdt)
        tv2 = V(lambda h, wdt=wdt, wn=wn: h.tensor_copy(
            sv(0, 16, 6800 + OFF0 + 1024 + wdt, wn), sv(0, 16, 6800 + OFF0 + 1024, wn)), deps=[tv2])
        wdt += wn
    tv3 = V(lambda h: h.memset(sv(0, 1, 9488, VW), BIGNEG * 8.0))
    tv3 = V(lambda h: h.memset(sv(0, 1, 9488 + OFF0, VW - OFF0), 0.0), deps=[tv3])
    tv4 = V(lambda h: h.memset(sv(0, 1, 12176, VW), BIGNEG * 8.0))
    tv4 = V(lambda h: h.memset(sv(0, 1, 12176 + OFF0, 512), 0.0), deps=[tv4])
    tbm = V(lambda h: h.reduce_max(sc[7][0:16, 0:1], sv(0, 16, 6800 + OFF0, 2048), axis=AX.X), deps=[tv2])
    wd = 1
    while wd < 8:
        tbm = V(lambda h, wd=wd: h.tensor_copy(sc[7][0:16, wd:2 * wd], sc[7][0:16, 0:wd]), deps=[tbm])
        wd *= 2
    tbm = dma("gpsimd", bmax_dram[:, :], sc[7][0:16, 0:8], deps=[fv(tbm)])
    if DBG:
        dma("gpsimd", dbg["d_tab"][:, :], sv(0, 16, 4096, 2047), deps=[pr.tok("vector")])
    tvf = fv(tv2)
    tvf = V(lambda h: h.memset(junkV[:, 0:8], 4.0), deps=[tv3, tv4, tvf, fence_prev["v"]])
    fence_prev["v"] = tvf
    tvds = [dma("gpsimd", vecflat[0:1, hh * VW:(hh + 1) * VW], scr[hh:hh + 1, 6800:6800 + VW],
                deps=[tvf]) for hh in range(16)]
    tvds.append(dma("gpsimd", vecflat[0:1, 16 * VW:17 * VW], sv(0, 1, 9488, VW), deps=[tvf]))
    tvds.append(dma("gpsimd", vecflat[0:1, 17 * VW:18 * VW], sv(0, 1, 12176, VW), deps=[tvf]))
    tz = V(lambda h: h.memset(sc[6][0:1, 0:8], 0.0), deps=[fence_prev["v"]])
    tz = fv(tz)
    for j in range(16):
        tvds.append(dma("gpsimd", vecflat[0:1, 18 * VW + j * 8:18 * VW + (j + 1) * 8], sc[6][0:1, 0:8], deps=[tz]))
    tvd = None
    for p in range(128):
        tvd = dma("gpsimd", m_dram[p:p + 1, :], vecflat[0:1, 127 - p:127 - p + 18 * VW], deps=tvds)
    pr.barrier()

    def ln_stats(src_dram, nblk_tag):
        toks = []
        for tt in range(NT):
            buf = sv(0, 128, (tt % 2) * D, D)
            ld = dma("sync", buf, src_dram[tt * 128:(tt + 1) * 128, :], deps=toks[-2:])
            tv = V(lambda h, buf=buf, tt=tt: h.reduce_sum(sc[0][:, tt:tt + 1], buf, axis=AX.X), deps=[ld])
            ts = S(lambda h, buf=buf, tt=tt: h.activation(
                sv(0, 128, 2 * D, D), buf, AF.Square, accum_out=sc[1][:, tt:tt + 1]), deps=[ld, toks[-1] if toks else None])
            toks.append(tv)
            toks.append(ts)
            pr.barrier()
        return toks[-2:]

    def ln_mr(deps):
        a1 = V(lambda h: h.tensor_single_scalar(sc[2][:], sc[0][:], 1.0 / D, AL.mult), deps=deps)
        a2 = V(lambda h: h.tensor_single_scalar(sc[4][:], sc[1][:], 1.0 / D, AL.mult), deps=deps)
        a3 = V(lambda h: h.tensor_tensor(sc[5][:], sc[2][:], sc[2][:], AL.mult), deps=[a1])
        a4 = V(lambda h: h.tensor_tensor(sc[6][:], sc[4][:], sc[5][:], AL.subtract), deps=[a2, a3])
        a4c = V(lambda h: h.tensor_single_scalar(sc[4][:], sc[6][:], 0.0, AL.max), deps=[a4])
        a5 = S(lambda h: h.activation(sc[5][:], sc[4][:], AF.Sqrt, bias=sc[7][:, 8:9]), deps=[a4c, tk_eps])
        a6 = V(lambda h: h.reciprocal(sc[3][:], sc[5][:]), deps=[a5])
        return [a1, a6]

    def pre_template(blk):
        src = x_in if blk == 0 else x_res
        stok = ln_stats(src, blk)
        mr = ln_mr(stok)
        pr.barrier()
        prev = []
        for tt in range(NT):
            buf = sv(0, 128, (tt % 2) * D, D)
            nbuf = sv(0, 128, (2 + tt % 2) * D, D)
            ld = dma("sync", buf, src[tt * 128:(tt + 1) * 128, :], deps=prev)
            nm = V(lambda h, buf=buf, nbuf=nbuf, tt=tt: h.tensor_scalar(
                nbuf, buf, sc[2][:, tt:tt + 1], sc[3][:, tt:tt + 1], AL.subtract, AL.mult),
                deps=[ld] + mr)
            prev = []
            for c in range(ND):
                pidx = c % 4
                tr = TE(lambda h, nbuf=nbuf, c=c, pidx=pidx: h.transpose(
                    psums[pidx][:, 0:128], nbuf[:, c * 128:(c + 1) * 128], ident[:]),
                    deps=[nm, tk_id, ps_last.get(pidx)])
                cp = V(lambda h, c=c, tt=tt, pidx=pidx: h.tensor_copy(
                    bigA[:, c * T + tt * 128: c * T + (tt + 1) * 128], psums[pidx][:, 0:128]),
                    deps=[tr])
                ps_last[pidx] = cp
                prev.append(cp)
        pr.barrier()

    def post_template(blk):
        stok = ln_stats(sb_out, blk)
        mr = ln_mr(stok)
        pr.barrier()
        src = x_in if blk == 0 else x_res
        dst = y_out if blk == NBLK - 1 else x_res
        prev = []
        for tt in range(NT):
            ob = sv(0, 128, (tt % 2) * D, D)
            xb = sv(0, 128, (2 + tt % 2) * D, D)
            lo = dma("sync", ob, sb_out[tt * 128:(tt + 1) * 128, :], deps=prev)
            lx = dma("sync", xb, x_res[tt * 128:(tt + 1) * 128, :] if blk > 0 else x_in[tt * 128:(tt + 1) * 128, :], deps=prev)
            nm = V(lambda h, ob=ob, tt=tt: h.tensor_scalar(
                ob, ob, sc[2][:, tt:tt + 1], sc[3][:, tt:tt + 1], AL.subtract, AL.mult),
                deps=[lo] + mr)
            ad = V(lambda h, ob=ob, xb=xb: h.tensor_tensor(xb, xb, ob, AL.add), deps=[nm, lx])
            st = dma("gpsimd", dst[tt * 128:(tt + 1) * 128, :], xb, deps=[fv(ad)])
            prev = [st]
            pr.barrier()
        pr.barrier()

    def attn_body(layer):
        is0 = layer == 0
        wT = qkvwT0 if is0 else qkvwT1
        pT = projT0 if is0 else woutT1
        for oc in range(16):
            wcol = wb0 if oc % 2 == 0 else wb1
            lw = dma("sync", bass.AP(tensor=wcol[:].tensor, offset=wcol[:].offset, ap=[[wcol[:].ap[0][0], 128], [128, 8], [1, 128]]),
                     bass.AP(tensor=wT.tensor, offset=oc * 128, ap=[[3 * D, 128], [128 * 3 * D, 8], [1, 128]]),
                     deps=[ps_last.get("w%d" % (oc % 2))])
            for tt in range(NQ):
                pidx = 4 + tt % 2
                mms = []
                for dc in range(ND):
                    mm = TE(lambda h, wcol=wcol, dc=dc, tt=tt, pidx=pidx: h.matmul(
                        psums[pidx][:, :], wcol[:, dc * 128:(dc + 1) * 128],
                        bigA[:, dc * T + tt * 512: dc * T + (tt + 1) * 512],
                        start=(dc == 0), stop=(dc == ND - 1)),
                        deps=([lw, ps_last.get(pidx)] if dc == 0 else []), inc=(1 if dc == ND - 1 else 0))
                    mms.append(mm)
                if is0:
                    ev = S(lambda h, oc=oc, tt=tt, pidx=pidx: h.activation(
                        sv(0, 128, 16384 + (tt % 2) * 512, 512), psums[pidx][:, :], AF.Identity,
                        bias=bias_sb[:, oc:oc + 1]), deps=[mms[-1], tk_qb, ps_last.get("qkev%d" % (tt % 2))])
                else:
                    ev = S(lambda h, tt=tt, pidx=pidx: h.copy(
                        sv(0, 128, 16384 + (tt % 2) * 512, 512), psums[pidx][:, :]),
                        deps=[mms[-1], ps_last.get("qkev%d" % (tt % 2))])
                ps_last[pidx] = ev
                dstt = q_st if oc < 8 else k_st
                row0 = (oc % 8) * 128
                st = dma("gpsimd", dstt[row0:row0 + 128, tt * 512:(tt + 1) * 512],
                         sv(0, 128, 16384 + (tt % 2) * 512, 512), deps=[fs(ev) if is0 else fs(ev)])
                ps_last["w%d" % (oc % 2)] = st
                ps_last["qkev%d" % (tt % 2)] = st
            pr.barrier()
        pr.barrier()
        for ot in range(2):
            lw = dma("sync", bass.AP(tensor=wb0[:].tensor, offset=wb0[:].offset, ap=[[wb0[:].ap[0][0], 128], [512, 8], [1, 512]]),
                     bass.AP(tensor=wT.tensor, offset=2 * D + ot * 512, ap=[[3 * D, 128], [128 * 3 * D, 8], [1, 512]]),
                     deps=[pr.tok("tensor")])
            for tt in range(NT):
                pidx = 4 + tt % 2
                mms = []
                for dc in range(ND):
                    mm = TE(lambda h, dc=dc, tt=tt, ot=ot, pidx=pidx: h.matmul(
                        psums[pidx][:, :], bigA[:, dc * T + tt * 128: dc * T + (tt + 1) * 128],
                        wb0[:, dc * 512:(dc + 1) * 512],
                        start=(dc == 0), stop=(dc == ND - 1)),
                        deps=([lw, ps_last.get(pidx)] if dc == 0 else []), inc=(1 if dc == ND - 1 else 0))
                    mms.append(mm)
                if is0:
                    ev = V(lambda h, tt=tt, ot=ot, pidx=pidx: h.scalar_tensor_tensor(
                        sv(0, 128, 16384 + (tt % 2) * 512, 512), psums[pidx][:, :], 1.0,
                        sv(0, 128, 18432 + ot * 512, 512), AL.bypass, AL.add),
                        deps=[mms[-1], ps_last.get("vb"), ps_last.get("qkev%d" % (tt % 2))])
                else:
                    ev = V(lambda h, tt=tt, pidx=pidx: h.tensor_copy(
                        sv(0, 128, 16384 + (tt % 2) * 512, 512), psums[pidx][:, :]),
                        deps=[mms[-1], ps_last.get("qkev%d" % (tt % 2))])
                ps_last[pidx] = ev
                st = dma("gpsimd", v_st[tt * 128:(tt + 1) * 128, ot * 512:(ot + 1) * 512],
                         sv(0, 128, 16384 + (tt % 2) * 512, 512), deps=[fv(ev)])
                ps_last["qkev%d" % (tt % 2)] = st
                pr.barrier()
        pr.barrier()

        qp = sv(0, 128, 0, 2048)
        kp = sv(0, 128, 2048, 2048)
        vsb_base = 4096
        mA_base = 6176
        mB_base = 8736
        mC = sv(0, 16, 11296, MW)
        qa = sv(0, 128, 13856, 2048)
        u0 = sv(0, 128, 15904, 512)
        u1 = sv(0, 128, 16416, 512)
        pp0 = sv(0, 128, 16928, 512)
        pp1 = sv(0, 128, 17440, 512)
        evb = sv(0, 128, 17952, 512)
        tko = V(lambda h: h.memset(
            bass.AP(tensor=scr[:].tensor, offset=scr[:].offset + vsb_base + 64,
                    ap=[[scr[:].ap[0][0], 128], [130, NT], [65, 2]]), 1.0))
        if not is0:
            mwA = dma("sync", sv(0, 128, mA_base, MW), m_dram[0:128, 17 * VW:17 * VW + MW], deps=[tvd])
            mwB = dma("sync", sv(0, 128, mB_base, MW), m_dram[0:128, 17 * VW:17 * VW + MW], deps=[tvd])
            mwB = V(lambda h: h.memset(sv(0, 16, mB_base, MW), BIGNEG * 8.0), deps=[mwB])
            mwC = dma("sync", mC, m_dram[0:16, 16 * VW:16 * VW + MW], deps=[tvd])
            mtoks = [mwA, mwB, mwC]
        for p in range(8):
            hA, hB = 2 * p, 2 * p + 1
            lq = dma("sync", qp, q_st[p * 128:(p + 1) * 128, :], deps=[ps_last.get("attn_done")])
            lk = dma("sync", kp, k_st[p * 128:(p + 1) * 128, :], deps=[ps_last.get("attn_done")])
            lv0 = dma("sync", bass.AP(tensor=scr[:].tensor, offset=scr[:].offset + vsb_base,
                                      ap=[[scr[:].ap[0][0], 128], [130, NT], [1, 64]]),
                      bass.AP(tensor=v_st.tensor, offset=p * 128, ap=[[D, 128], [128 * D, NT], [1, 64]]),
                      deps=[tko, ps_last.get("attn_done")])
            lv1 = dma("sync", bass.AP(tensor=scr[:].tensor, offset=scr[:].offset + vsb_base + 65,
                                      ap=[[scr[:].ap[0][0], 128], [130, NT], [1, 64]]),
                      bass.AP(tensor=v_st.tensor, offset=p * 128 + 64, ap=[[D, 128], [128 * D, NT], [1, 64]]),
                      deps=[tko, ps_last.get("attn_done")])
            lv = lv1
            if is0:
                mwA = dma("sync", sv(0, 128, mA_base, MW), m_dram[0:128, hA * VW:hA * VW + MW],
                          deps=[tvd, ps_last.get("attn_done")])
                mwB = dma("sync", sv(0, 128, mB_base, MW), m_dram[0:128, hB * VW:hB * VW + MW],
                          deps=[tvd, ps_last.get("attn_done")])
                mtoks = [mwA, mwB]
            z6 = V(lambda h: h.memset(sc[6][0:1, 0:8], 0.0))
            sqk = S(lambda h: h.activation(qa, kp, AF.Square), deps=[lk, z6])
            nrm_toks = []
            for hh2 in range(2):
                for qt in range(NQ):
                    mm = TE(lambda h, hh2=hh2, qt=qt: h.matmul(
                        psums[6][0:1, :], ones8[hh2 * 64:hh2 * 64 + 64, 0:1],
                        qa[hh2 * 64:hh2 * 64 + 64, qt * 512:(qt + 1) * 512],
                        start=True, stop=True), deps=[sqk, tk_ones, ps_last.get(6)])
                    cx = V(lambda h, qt=qt: h.tensor_copy(sv(0, 1, 15904 + qt * 512, 512), psums[6][0:1, :]),
                           deps=[mm, nrm_toks[-1] if nrm_toks else None])
                    ps_last[6] = cx
                    nrm_toks.append(cx)
                nrm_toks.append(V(lambda h, hh2=hh2: h.reduce_max(sc[6][0:1, 1 + hh2:2 + hh2],
                                                                  sv(0, 1, 15904, 2048), axis=AX.X),
                                  deps=nrm_toks[-4:] + [z6]))
            sqq = S(lambda h: h.activation(qa, qp, AF.Square), deps=[lq] + nrm_toks)
            for hh2 in range(2):
                for qt in range(NQ):
                    mm = TE(lambda h, hh2=hh2, qt=qt: h.matmul(
                        psums[6][0:1, :], ones8[hh2 * 64:hh2 * 64 + 64, 0:1],
                        qa[hh2 * 64:hh2 * 64 + 64, qt * 512:(qt + 1) * 512],
                        start=True, stop=True), deps=[sqq, tk_ones, ps_last.get(6)])
                    cx = V(lambda h, qt=qt: h.tensor_copy(sv(0, 1, 15904 + qt * 512, 512), psums[6][0:1, :]),
                           deps=[mm, nrm_toks[-1] if nrm_toks else None])
                    ps_last[6] = cx
                    nrm_toks.append(cx)
                nrm_toks.append(V(lambda h, hh2=hh2: h.reduce_max(sc[6][0:1, 3 + hh2:4 + hh2],
                                                                  sv(0, 1, 15904, 2048), axis=AX.X),
                                  deps=nrm_toks[-4:] + [z6]))
            bst = dma("gpsimd", bc_dram[0:1, 0:8], sc[6][0:1, 0:8], deps=[fv(nrm_toks[-1])] + nrm_toks)
            mhld = dma("sync", sc[2][:, 0:8], bass.AP(tensor=bc_dram.tensor, offset=0, ap=[[0, 128], [1, 8]]),
                       deps=[bst])
            wtoks = []
            for idx, (dt_, dcol) in enumerate(((sc[0], 8), (sc[1], 8), (sc[0], 0), (sc[1], 0))):
                tkc = V(lambda h, dt_=dt_, dcol=dcol, idx=idx: h.tensor_copy(dt_[:, dcol:dcol + 1],
                                                                             sc[2][:, 1 + idx:2 + idx]), deps=[mhld])
                wd = 1
                while wd < 8:
                    tkc = V(lambda h, dt_=dt_, dcol=dcol, wd=wd: h.tensor_copy(
                        dt_[:, dcol + wd:dcol + 2 * wd], dt_[:, dcol:dcol + wd]), deps=[tkc])
                    wd *= 2
                wtoks.append(tkc)
            pA = V(lambda h: h.tensor_tensor(sc[3][:, 0:8], sc[0][:, 0:8], sc[0][:, 8:16], AL.mult), deps=wtoks)
            pB = V(lambda h: h.tensor_tensor(sc[3][:, 8:16], sc[1][:, 0:8], sc[1][:, 8:16], AL.mult), deps=wtoks)
            sq2 = S(lambda h: h.activation(sc[2][:, 0:16], sc[3][:, 0:16], AF.Sqrt), deps=[pA, pB])
            if is0:
                bmA = dma("sync", sc[4][:, 0:8], bass.AP(tensor=bmax_dram.tensor, offset=hA * 8, ap=[[0, 128], [1, 8]]),
                          deps=[tbm])
                bmB = dma("sync", sc[4][:, 8:16], bass.AP(tensor=bmax_dram.tensor, offset=hB * 8, ap=[[0, 128], [1, 8]]),
                          deps=[tbm])
                su = V(lambda h: h.tensor_tensor(sc[5][:, 0:16], sc[2][:, 0:16], sc[4][:, 0:16], AL.add),
                       deps=[sq2, bmA, bmB])
            else:
                su = V(lambda h: h.tensor_single_scalar(sc[5][:, 0:16], sc[2][:, 0:16], 0.0, AL.add), deps=[sq2])
            m1 = V(lambda h: h.tensor_single_scalar(sc[6][:, 0:16], sc[5][:, 0:16], -0.125, AL.mult), deps=[su])
            mh_ap = [sc[6][:, 0:1], sc[6][:, 8:9]]
            mh_ap16 = [sc[6][0:16, 0:1], sc[6][0:16, 8:9]]
            mh_toks = [m1]
            dn_toks = []
            for hh in range(2):
                hcur = hA + hh
                mbase = mA_base if (is0 and hh == 0) else (mB_base if is0 else mA_base)
                for Q0 in range(NQ):
                    if is0:
                        k0s = list(range(0, min(NT, 4 * Q0 + 4)))
                        k0s_pre = []
                    else:
                        k0s = list(range(max(0, 4 * (Q0 - 1)), 4 * Q0 + 4))
                        k0s_pre = [("pre", Q0)]
                    psS = 4
                    psO = 5
                    nk = len(k0s) + len(k0s_pre)
                    acc_i = 0
                    evtok = ps_last.get(psO)
                    for item in k0s_pre + k0s:
                        first = acc_i == 0
                        last = acc_i == nk - 1
                        if item == ("pre", Q0):
                            mm1 = TE(lambda h, hh=hh, Q0=Q0: h.matmul(
                                psums[psS][0:16, :], kp[hh * 64:(hh + 1) * 64, 0:16],
                                qp[hh * 64:(hh + 1) * 64, Q0 * 512:(Q0 + 1) * 512], start=True, stop=True),
                                deps=[lq, lk, ps_last.get(psS)])
                            if Q0 == 0:
                                uu = V(lambda h, Q0=Q0: h.tensor_tensor(
                                    u0[0:16, :], psums[psS][0:16, :], mC[0:16, Q0 * 512 + PAD: Q0 * 512 + PAD + 512], AL.add),
                                    deps=[mm1, ps_last.get("ee0")] + (mtoks if not is0 else []))
                                ee = S(lambda h, hh=hh: h.activation(pp0[0:16, :], u0[0:16, :], AF.Exp,
                                                                     bias=mh_ap16[hh], scale=0.125),
                                       deps=[uu, ps_last.get("mo0")] + mh_toks)
                            else:
                                ee = S(lambda h, hh=hh: h.activation(pp0[0:16, :], psums[psS][0:16, :], AF.Exp,
                                                                     bias=mh_ap16[hh], scale=0.125),
                                       deps=[mm1, ps_last.get("mo0")] + mh_toks)
                            ps_last[psS] = ee
                            ps_last["ee0"] = ee
                            mo = TE(lambda h, hh=hh, first=first, last=last: h.matmul(
                                psums[psO][0:65, :], bass.AP(tensor=scr[:].tensor,
                                                             offset=scr[:].offset + vsb_base + hh * 65,
                                                             ap=[[scr[:].ap[0][0], 16], [1, 65]]),
                                pp0[0:16, :], start=first, stop=last),
                                deps=[ee, lv0, lv1] + ([evtok] if first else []), inc=1)
                            ps_last["mo0"] = mo
                        else:
                            K0 = item
                            base = Q0 * 512 - K0 * 128
                            mb = mbase if (is0 or K0 > 0) else mB_base
                            mm1 = TE(lambda h, hh=hh, K0=K0, Q0=Q0: h.matmul(
                                psums[psS][:, :], kp[hh * 64:(hh + 1) * 64, K0 * 128:(K0 + 1) * 128],
                                qp[hh * 64:(hh + 1) * 64, Q0 * 512:(Q0 + 1) * 512], start=True, stop=True),
                                deps=[lq, lk, ps_last.get(psS)])
                            par2 = acc_i % 2
                            ub = u0 if par2 == 0 else u1
                            ppb = pp0 if par2 == 0 else pp1
                            uu = V(lambda h, ub=ub, mb=mb, base=base: h.tensor_tensor(
                                ub, psums[psS][:, :], sv(0, 128, mb + base + PAD, 512), AL.add),
                                deps=[mm1, ps_last.get("ee%d" % par2)] + mtoks)
                            ps_last[psS] = uu
                            ee = S(lambda h, ub=ub, ppb=ppb, hh=hh: h.activation(
                                ppb, ub, AF.Exp, bias=mh_ap[hh], scale=0.125),
                                deps=[uu, ps_last.get("mo%d" % par2)] + mh_toks)
                            ps_last["ee%d" % par2] = ee
                            mo = TE(lambda h, hh=hh, K0=K0, ppb=ppb, first=first, last=last: h.matmul(
                                psums[psO][0:65, :], bass.AP(tensor=scr[:].tensor,
                                                             offset=scr[:].offset + vsb_base + K0 * 130 + hh * 65,
                                                             ap=[[scr[:].ap[0][0], 128], [1, 65]]),
                                ppb, start=first, stop=last),
                                deps=[ee, lv0, lv1] + ([evtok] if first else []), inc=1)
                            ps_last["mo%d" % par2] = mo
                        acc_i += 1
                    e1 = V(lambda h: h.tensor_copy(evb[0:64, :], psums[psO][0:64, :]),
                           deps=[mo, ps_last.get("evbst")])
                    e2 = V(lambda h, Q0=Q0: h.tensor_copy(qa[64:65, Q0 * 512:(Q0 + 1) * 512], psums[psO][64:65, :]),
                           deps=[mo, ps_last.get("dnst")])
                    ps_last[psO] = e2
                    st = dma("gpsimd", oT_st[hcur * 64:hcur * 64 + 64, Q0 * 512:(Q0 + 1) * 512], evb[0:64, :],
                             deps=[fv(e1)])
                    ps_last["evbst"] = st
                    dn_toks.append(st)
                dn_st = dma("gpsimd", dn_dram[hcur:hcur + 1, :], qa[64:65, :], deps=[fv(ps_last[psO])])
                ps_last["dnst"] = dn_st
                dn_toks.append(dn_st)
                pr.barrier()
            ps_last["attn_done"] = dn_toks[-1]
            pr.barrier()
        ldn = dma("sync", sv(0, 16, 0, T), dn_dram[:, :])
        rdn = V(lambda h: h.reciprocal(sv(0, 16, 2048, T), sv(0, 16, 0, T)), deps=[ldn])
        sdn = dma("gpsimd", dn_dram[:, :], sv(0, 16, 2048, T), deps=[fv(rdn)])
        pr.barrier()

        if is0:
            ps_last["pb"] = dma("sync", sv(0, 128, 18432, 1024),
                                bass.AP(tensor=projb0.tensor, offset=0, ap=[[0, 128], [1, D]]),
                                deps=[pr.tok("vector"), pr.tok("scalar")])
        for ot in range(2):
            lw = dma("sync", bass.AP(tensor=wb1[:].tensor, offset=wb1[:].offset, ap=[[wb1[:].ap[0][0], 128], [512, 8], [1, 512]]),
                     bass.AP(tensor=pT.tensor, offset=ot * 512, ap=[[D, 128], [128 * D, 8], [1, 512]]),
                     deps=[pr.tok("tensor")])
            if ot == 0:
                ntoks = []
                for c in range(ND):
                    lo = dma("sync", sv(0, 128, 0, T), oT_st[c * 128:(c + 1) * 128, :], deps=[sdn, ntoks[-1] if ntoks else None])
                    lr0 = dma("sync", sv(0, 128, 2048, T)[0:64, :],
                              bass.AP(tensor=dn_dram.tensor, offset=2 * c * T, ap=[[0, 64], [1, T]]),
                              deps=[sdn, ntoks[-1] if ntoks else None])
                    lr1 = dma("sync", bass.AP(tensor=scr[:].tensor, offset=scr[:].offset + 2048,
                                              ap=[[scr[:].ap[0][0], 128], [1, T]])[64:128, :],
                              bass.AP(tensor=dn_dram.tensor, offset=(2 * c + 1) * T, ap=[[0, 64], [1, T]]),
                              deps=[sdn, ntoks[-1] if ntoks else None])
                    nm = V(lambda h, c=c: h.tensor_tensor(bigA[:, c * T:(c + 1) * T], sv(0, 128, 0, T),
                                                          sv(0, 128, 2048, T), AL.mult), deps=[lo, lr0, lr1])
                    ntoks.append(nm)
                pr.barrier()
            for tt in range(NT):
                pidx = tt % 2
                mms = []
                for dc in range(ND):
                    mm = TE(lambda h, dc=dc, tt=tt, pidx=pidx: h.matmul(
                        psums[pidx][:, :], bigA[:, dc * T + tt * 128: dc * T + (tt + 1) * 128],
                        wb1[:, dc * 512:(dc + 1) * 512], start=(dc == 0), stop=(dc == ND - 1)),
                        deps=([lw, ps_last.get(pidx)] if dc == 0 else []), inc=(1 if dc == ND - 1 else 0))
                    mms.append(mm)
                if is0:
                    ev = V(lambda h, tt=tt, ot=ot, pidx=pidx: h.scalar_tensor_tensor(
                        sv(0, 128, 16384 + (tt % 2) * 512, 512), psums[pidx][:, :], 1.0,
                        sv(0, 128, 18432 + ot * 512, 512), AL.bypass, AL.add),
                        deps=[mms[-1], ps_last.get("pb"), ps_last.get("qkev%d" % (tt % 2))])
                else:
                    ev = V(lambda h, tt=tt, pidx=pidx: h.tensor_copy(
                        sv(0, 128, 16384 + (tt % 2) * 512, 512), psums[pidx][:, :]),
                        deps=[mms[-1], ps_last.get("qkev%d" % (tt % 2))])
                ps_last[pidx] = ev
                st = dma("gpsimd", sb_out[tt * 128:(tt + 1) * 128, ot * 512:(ot + 1) * 512],
                    sv(0, 128, 16384 + (tt % 2) * 512, 512), deps=[fv(ev)])
                ps_last["qkev%d" % (tt % 2)] = st
                pr.barrier()
        pr.barrier()

    def load_bias_bcast(src_row, tag):
        tk = dma("sync", sv(0, 128, 18432, 1024),
                 bass.AP(tensor=src_row.tensor, offset=0, ap=[[0, 128], [1, D]]))
        ps_last[tag] = tk
        return tk

    def ffn0_body():
        s_v = sv(0, 64, 0, T)
        g0v = sv(0, 64, 2048, T)
        g1v = sv(0, 64, 4096, T)
        tmpv = sv(0, 64, 6144, T)
        s_evs = []
        lw = dma("sync", bass.AP(tensor=wb0[:].tensor, offset=wb0[:].offset, ap=[[wb0[:].ap[0][0], 128], [64, 8], [1, 64]]),
                 bass.AP(tensor=ffw_in.tensor, offset=0, ap=[[64, 128], [128 * 64, 8], [1, 64]]))
        for tt in range(NQ):
            pidx = tt % 2
            mms = []
            for dc in range(ND):
                mm = TE(lambda h, dc=dc, tt=tt, pidx=pidx: h.matmul(
                    psums[pidx][0:64, :], wb0[:, dc * 64:(dc + 1) * 64],
                    bigA[:, dc * T + tt * 512: dc * T + (tt + 1) * 512],
                    start=(dc == 0), stop=(dc == ND - 1)),
                    deps=([lw, ps_last.get(pidx)] if dc == 0 else []), inc=(1 if dc == ND - 1 else 0))
                mms.append(mm)
            ev = S(lambda h, tt=tt, pidx=pidx: h.activation(
                s_v[0:64, tt * 512:(tt + 1) * 512], psums[pidx][0:64, :], AF.Identity,
                bias=fsm[0:64, 0:1]), deps=[mms[-1]] + tfs)
            ps_last[pidx] = ev
            s_evs.append(ev)
        pr.barrier()
        gcur, gprev = g0v, g1v
        gtok = None
        for i in range(16):
            tm = S(lambda h, i=i: h.activation(tmpv, s_v, AF.Relu,
                                               bias=fsm[0:64, 17 + i:18 + i], scale=fsm[0:64, 1 + i:2 + i]),
                   deps=tfs + [pr.tok("vector")] + s_evs)
            if i == 0:
                gtok = V(lambda h: h.tensor_scalar(gcur, tmpv, fsm[0:64, 33:34], None, AL.mult), deps=[tm] + tfs)
            else:
                gtok = V(lambda h, i=i, gc=gcur, gp=gprev: h.scalar_tensor_tensor(
                    gc, tmpv, fsm[0:64, 33 + i:34 + i], gp, AL.mult, AL.add), deps=[tm, gtok])
            gcur, gprev = gprev, gcur
            pr.barrier()
        gtok = V(lambda h, gp=gprev: h.tensor_scalar(s_v, gp, fsm[0:64, 49:50], None, AL.add), deps=[gtok])
        gfin = s_v
        pr.barrier()
        lw2 = dma("sync", wb0[0:64, 0:D], ffw_out[:, :])
        for tt in range(NT):
            for ot in range(2):
                pidx = tt % 2
                mm = TE(lambda h, tt=tt, ot=ot, pidx=pidx: h.matmul(
                    psums[pidx][:, :], gfin[0:64, tt * 128:(tt + 1) * 128],
                    wb0[0:64, ot * 512:(ot + 1) * 512], start=True, stop=True),
                    deps=[lw2, gtok, ps_last.get(pidx)], inc=1)
                ev = V(lambda h, pidx=pidx, ot=ot: h.scalar_tensor_tensor(
                    sv(0, 128, 16384 + pidx * 512, 512), psums[pidx][:, :], 1.0,
                    sv(0, 128, 18432 + ot * 512, 512), AL.bypass, AL.add),
                    deps=[mm, ps_last.get("ob"), ps_last.get("qkev%d" % pidx)])
                ps_last[pidx] = ev
                st = dma("gpsimd", sb_out[tt * 128:(tt + 1) * 128, ot * 512:(ot + 1) * 512],
                    sv(0, 128, 16384 + pidx * 512, 512), deps=[fv(ev)])
                ps_last["qkev%d" % pidx] = st
            pr.barrier()
        pr.barrier()

    def allreduce_max(vals_toks, n):
        cz = V(lambda h: h.memset(sc[6][0:1, 0:8], 0.0))
        czs = dma("gpsimd", cc_in[0:1, 0:8], sc[6][0:1, 0:8], deps=[fv(cz)])
        toks = []
        for idx, tok in enumerate(vals_toks):
            toks.append(dma("gpsimd", cc_in[0:1, idx:idx + 1], tok[0], deps=[fv(tok[1]), czs]))
        ccall = G(lambda h: h.collective_compute(
            "AllReduce", AL.max, ins=[cc_in[0:1, 0:8]], outs=[cc_out[0:1, 0:8]],
            replica_groups=[[0, 1, 2, 3, 4, 5, 6, 7]]), deps=toks, inc=1)
        lb = dma("sync", sc[6][:, 0:8], bass.AP(tensor=cc_out.tensor, offset=0, ap=[[0, 128], [1, 8]]),
                 deps=[ccall])
        if DBG:
            allreduce_max.n = getattr(allreduce_max, "n", 0)
            dma("gpsimd", dbg["d_cc"][allreduce_max.n:allreduce_max.n + 1, :], cc_out[0:1, 0:8], deps=[ccall])
            allreduce_max.n += 1
        return lb

    def qdq_chain(dst, srcap, rs_ap, s_ap, t_y, t_a, t_b, p, w, deps):
        ty = V(lambda h: h.tensor_scalar(t_y, srcap, rs_ap, None, AL.mult), deps=deps)
        ta = V(lambda h: h.tensor_scalar(t_a, t_y, C32, C32, AL.add, AL.subtract), deps=[ty])
        tb = V(lambda h: h.tensor_scalar(t_b, t_y, C64, C64, AL.add, AL.subtract), deps=[ty])
        tq = V(lambda h: h.scalar_tensor_tensor(t_y, t_a, e_t[:p, 0:1], t_b, AL.mult, AL.add),
               deps=[ta, tb, tk_e])
        return V(lambda h: h.tensor_scalar(dst, t_y, s_ap, None, AL.mult), deps=[tq])

    def scale_from_amax(col_out, amax_src_ap, deps):
        tk = V(lambda h: h.tensor_copy(sc[col_out + 1][:, 0:1], amax_src_ap), deps=deps)
        wd = 1
        while wd < 8:
            tk = V(lambda h, wd=wd: h.tensor_copy(sc[col_out + 1][:, wd:2 * wd], sc[col_out + 1][:, 0:wd]),
                   deps=[tk])
            wd *= 2
        t1 = V(lambda h: h.tensor_scalar(sc[col_out][:, 0:8], sc[col_out + 1][:, 0:8],
                                         1.0 / 32767.0, 1e-12, AL.mult, AL.max), deps=[tk])
        t2 = V(lambda h: h.reciprocal(sc[col_out + 1][:, 0:8], sc[col_out][:, 0:8]), deps=[t1])
        return [t1, fv(t2)]

    def global_amax(chunk_reduce_toks, colbuf_ap, tag):
        tr = TE(lambda h: h.transpose(psums[6][0:8, 0:128], colbuf_ap, ident[:]),
                deps=chunk_reduce_toks + [tk_id, ps_last.get(6)])
        rd = V(lambda h: h.reduce_max(sc[5][0:8, 0:1], psums[6][0:8, 0:128], axis=AX.X), deps=[tr])
        ps_last[6] = rd
        rd2 = fv(rd)
        return (sc[5][0:1, 0:1], rd2)

    def ffn1_weight_dequant():
        specs = [(wup, wdqT_up, 24, D), (wgate, wdqT_gt, 24, D), (wdown, wdqT_dn, 8, 3 * D)]
        for w_in, w_out, nchunk, W in specs:
            nblk32 = W // 32
            for ch in range(nchunk):
                lw = dma("sync", wb0[:, 0:W], w_in[ch * 128:(ch + 1) * 128, :], deps=[ps_last.get("dq")])
                bm = V(lambda h, W=W, nblk32=nblk32: h.reduce_max(
                    sv(0, 128, 0, nblk32),
                    wb0[:, 0:W].rearrange("p (a b) -> p a b", b=32), axis=AX.X,
                    apply_absolute_value=True), deps=[lw])
                s1 = V(lambda h, nblk32=nblk32: h.tensor_scalar(
                    sv(0, 128, 128, nblk32), sv(0, 128, 0, nblk32), 1.0 / 7.0, 1e-12, AL.mult, AL.max), deps=[bm])
                s1b = V(lambda h: h.engine_nop(), deps=[s1])
                s2 = V(lambda h, nblk32=nblk32: h.reciprocal(sv(0, 128, 256, nblk32), sv(0, 128, 128, nblk32)),
                       deps=[s1b])
                exp_toks = []
                for colsrc, coldst in ((128, 512), (256, 512 + W)):
                    tcur = V(lambda h, colsrc=colsrc, coldst=coldst, nblk32=nblk32: h.tensor_copy(
                        bass.AP(tensor=scr[:].tensor, offset=scr[:].offset + coldst,
                                ap=[[scr[:].ap[0][0], 128], [32, nblk32], [1, 1]]),
                        sv(0, 128, colsrc, nblk32)), deps=[s2])
                    wd = 1
                    while wd < 32:
                        wn = min(wd, 32 - wd)
                        tcur = V(lambda h, coldst=coldst, wd=wd, wn=wn, nblk32=nblk32: h.tensor_copy(
                            bass.AP(tensor=scr[:].tensor, offset=scr[:].offset + coldst + wd,
                                    ap=[[scr[:].ap[0][0], 128], [32, nblk32], [1, wn]]),
                            bass.AP(tensor=scr[:].tensor, offset=scr[:].offset + coldst,
                                    ap=[[scr[:].ap[0][0], 128], [32, nblk32], [1, wn]])), deps=[tcur])
                        wd += wn
                    exp_toks.append(tcur)
                c_y, c_a, c_b, c_o = 512 + 2 * W, 512 + 3 * W, 512 + 4 * W, 512 + 5 * W
                ty = V(lambda h, W=W, c_y=c_y: h.tensor_tensor(
                    sv(0, 128, c_y, W), wb0[:, 0:W], sv(0, 128, 512 + W, W), AL.mult), deps=exp_toks + [lw])
                ta = V(lambda h, W=W, c_y=c_y, c_a=c_a: h.tensor_scalar(
                    sv(0, 128, c_a, W), sv(0, 128, c_y, W), C32, C32, AL.add, AL.subtract), deps=[ty])
                tb = V(lambda h, W=W, c_y=c_y, c_b=c_b: h.tensor_scalar(
                    sv(0, 128, c_b, W), sv(0, 128, c_y, W), C64, C64, AL.add, AL.subtract), deps=[ty])
                tq = V(lambda h, W=W, c_a=c_a, c_b=c_b, c_y=c_y: h.scalar_tensor_tensor(
                    sv(0, 128, c_y, W), sv(0, 128, c_a, W), e_t[:, 0:1], sv(0, 128, c_b, W),
                    AL.mult, AL.add), deps=[ta, tb, tk_e])
                tw = V(lambda h, W=W, c_y=c_y, c_o=c_o: h.tensor_tensor(
                    sv(0, 128, c_o, W), sv(0, 128, c_y, W), sv(0, 128, 512, W), AL.mult), deps=[tq])
                last_st = None
                for j in range(W // 128):
                    pidx = j % 2
                    tr = TE(lambda h, c_o=c_o, j=j, pidx=pidx: h.transpose(
                        psums[pidx][:, 0:128], sv(0, 128, c_o + j * 128, 128), ident[:]),
                        deps=[tw, tk_id, ps_last.get(pidx)])
                    cp = V(lambda h, pidx=pidx: h.tensor_copy(
                        sv(0, 128, 256 + pidx * 128, 128), psums[pidx][:, 0:128]),
                        deps=[tr, ps_last.get("dqev%d" % pidx)])
                    ps_last[pidx] = cp
                    last_st = dma("gpsimd", w_out[j * 128:(j + 1) * 128, ch * 128:(ch + 1) * 128],
                                  sv(0, 128, 256 + pidx * 128, 128), deps=[fv(cp)])
                    ps_last["dqev%d" % pidx] = last_st
                ps_last["dq"] = last_st
            pr.barrier()

    def ffn1_body():
        zms = V(lambda h: h.memset(sv(0, 128, 8192, 32), 0.0))
        rtoks = [zms]
        for c in range(ND):
            rt = V(lambda h, c=c: h.reduce_max(sv(0, 128, 8192 + c, 1), bigA[:, c * T:(c + 1) * T],
                                               axis=AX.X, apply_absolute_value=True), deps=[zms])
            rtoks.append(rt)
        rall = V(lambda h: h.reduce_max(sv(0, 128, 8200, 1), sv(0, 128, 8192, 8), axis=AX.X), deps=rtoks)
        rallb = V(lambda h: h.tensor_copy(sv(0, 128, 8208, 8), sv(0, 128, 8200, 8)), deps=[rall])
        gx = global_amax([rallb], sv(0, 128, 8208, 8), "x")
        lb = allreduce_max([gx], 1)
        stoks = scale_from_amax(0, sc[6][:, 0:1], [lb])
        pr.barrier()
        for c in range(ND):
            qdq_chain(bigA[:, c * T:(c + 1) * T], bigA[:, c * T:(c + 1) * T],
                      sc[1][:, 0:1], sc[0][:, 0:1], sv(0, 128, 0, T), sv(0, 128, 2048, T),
                      sv(0, 128, 4096, T), 128, T, deps=stoks + [pr.tok("vector")])
        pr.barrier()
        if DBG:
            dma("gpsimd", dbg["d_xq"][:, :], bigA[:, 0:T], deps=[pr.tok("vector")])
        for (wT_dram, st_dram, colpos) in ((wdqT_up, up_st, 0), (wdqT_gt, gt_st, 1)):
            am0 = V(lambda h: h.memset(sv(0, 128, 12800, 40), 0.0), deps=[ps_last.get("r2t")])
            am_toks = [am0]
            for ic in range(24):
                wcol = wb0 if ic % 2 == 0 else wb1
                lw = dma("sync", bass.AP(tensor=wcol[:].tensor, offset=wcol[:].offset,
                                         ap=[[wcol[:].ap[0][0], 128], [128, 8], [1, 128]]),
                         bass.AP(tensor=wT_dram.tensor, offset=ic * 128, ap=[[3 * D, 128], [128 * 3 * D, 8], [1, 128]]),
                         deps=[ps_last.get("u%d" % (ic % 2))])
                for tt in range(NQ):
                    pidx = 4 + tt % 2
                    mms = []
                    for dc in range(ND):
                        mm = TE(lambda h, wcol=wcol, dc=dc, tt=tt, pidx=pidx: h.matmul(
                            psums[pidx][:, :], wcol[:, dc * 128:(dc + 1) * 128],
                            bigA[:, dc * T + tt * 512: dc * T + (tt + 1) * 512],
                            start=(dc == 0), stop=(dc == ND - 1)),
                            deps=([lw, ps_last.get(pidx)] if dc == 0 else []), inc=(1 if dc == ND - 1 else 0))
                        mms.append(mm)
                    ev = S(lambda h, tt=tt, pidx=pidx, ic=ic: h.copy(
                        sv(0, 128, 8704 + (ic % 2) * T + tt * 512, 512), psums[pidx][:, :]),
                        deps=[mms[-1], ps_last.get("rtu%d" % (ic % 2))])
                    ps_last[pidx] = ev
                    am_toks.append(ev)
                rt = V(lambda h, ic=ic: h.reduce_max(sv(0, 128, 12800 + ic, 1),
                                                     sv(0, 128, 8704 + (ic % 2) * T, T), axis=AX.X,
                                                     apply_absolute_value=True), deps=am_toks[-4:] + [am0])
                ps_last["rtu%d" % (ic % 2)] = rt
                st = dma("gpsimd", st_dram[ic * 128:(ic + 1) * 128, :], sv(0, 128, 8704 + (ic % 2) * T, T),
                         deps=[fs(am_toks[-1])] + am_toks[-4:])
                ps_last["u%d" % (ic % 2)] = st
                am_toks.append(rt)
            r1 = V(lambda h: h.reduce_max(sv(0, 128, 12832, 1), sv(0, 128, 12800, 24), axis=AX.X),
                   deps=am_toks)
            r2 = V(lambda h, colpos=colpos: h.tensor_copy(sv(0, 128, 12840 + 8 * colpos, 8), sv(0, 128, 12832, 8)),
                   deps=[r1])
            ps_last["amax%d" % colpos] = r2
            ps_last["r2t"] = r2
            pr.barrier()
        gup = global_amax([ps_last["amax0"]], sv(0, 128, 12840, 8), "up")
        up_tok = dma("gpsimd", bc_dram[1:2, 0:8], sc[5][0:1, 0:8], deps=[gup[1]])
        ggt = global_amax([ps_last["amax1"], up_tok], sv(0, 128, 12848, 8), "gt")
        gt_tok = dma("gpsimd", bc_dram[2:3, 0:8], sc[5][0:1, 0:8], deps=[ggt[1], up_tok])
        lb1 = dma("sync", sc[4][0:1, 0:8], bc_dram[1:2, 0:8], deps=[up_tok])
        lb2 = dma("sync", sc[4][1:2, 0:8], bc_dram[2:3, 0:8], deps=[gt_tok])
        lb = allreduce_max([(sc[4][0:1, 0:1], lb1), (sc[4][1:2, 0:1], lb2)], 2)
        s_up = scale_from_amax(0, sc[6][:, 0:1], [lb])
        s_gt = scale_from_amax(2, sc[6][:, 1:2], [lb])
        pr.barrier()
        ub = sv(0, 128, 0, T)
        gb = sv(0, 128, 2048, T)
        hb = sv(0, 128, 4096, T)
        tb_ = sv(0, 128, 6144, T)
        am0 = V(lambda h: h.memset(sv(0, 128, 12800, 40), 0.0), deps=[ps_last.get("r2t")])
        am_toks = [am0]
        for ic in range(24):
            lg = dma("sync", gb, gt_st[ic * 128:(ic + 1) * 128, :], deps=am_toks[-2:])
            q1 = qdq_chain(gb, gb, sc[3][:, 0:1], sc[2][:, 0:1], ub, hb, tb_, 128, T,
                           deps=[lg] + s_gt)
            sg = S(lambda h: h.activation(ub, gb, AF.Sigmoid), deps=[q1])
            y2 = V(lambda h: h.tensor_single_scalar(hb, ub, 65536.0, AL.mult), deps=[sg])
            a2 = V(lambda h: h.tensor_scalar(tb_, hb, C32, C32, AL.add, AL.subtract), deps=[y2])
            b2 = V(lambda h: h.tensor_scalar(ub, hb, C64, C64, AL.add, AL.subtract), deps=[y2])
            q2 = V(lambda h: h.scalar_tensor_tensor(hb, tb_, e_t[:, 0:1], ub, AL.mult, AL.add),
                   deps=[a2, b2, tk_e])
            sq = V(lambda h: h.tensor_scalar(ub, hb, 65535.0, 2.0 ** -16, AL.min, AL.mult), deps=[q2])
            g2 = V(lambda h: h.tensor_tensor(hb, gb, ub, AL.mult), deps=[sq])
            rt = V(lambda h, ic=ic: h.reduce_max(sv(0, 128, 12800 + ic, 1), hb, axis=AX.X,
                                                 apply_absolute_value=True), deps=[g2, am0])
            st = dma("gpsimd", g2_st[ic * 128:(ic + 1) * 128, :], hb, deps=[fv(g2)])
            am_toks.append(rt)
            am_toks.append(st)
        r1 = V(lambda h: h.reduce_max(sv(0, 128, 12832, 1), sv(0, 128, 12800, 24), axis=AX.X), deps=am_toks)
        r2 = V(lambda h: h.tensor_copy(sv(0, 128, 12840, 8), sv(0, 128, 12832, 8)), deps=[r1])
        ps_last["r2t"] = r2
        gg2 = global_amax([r2], sv(0, 128, 12840, 8), "g2")
        lb = allreduce_max([gg2], 1)
        s_g2 = scale_from_amax(2, sc[6][:, 0:1], [lb])
        pr.barrier()
        am0 = V(lambda h: h.memset(sv(0, 128, 12800, 40), 0.0), deps=[ps_last.get("r2t")])
        am_toks = [am0]
        for ic in range(24):
            lg = dma("sync", gb, g2_st[ic * 128:(ic + 1) * 128, :], deps=am_toks[-2:])
            lu = dma("sync", ub, up_st[ic * 128:(ic + 1) * 128, :], deps=am_toks[-2:])
            q1 = qdq_chain(gb, gb, sc[3][:, 0:1], sc[2][:, 0:1], hb, tb_, sv(0, 128, 8192, T), 128, T,
                           deps=[lg] + s_g2)
            q2 = qdq_chain(ub, ub, sc[1][:, 0:1], sc[0][:, 0:1], hb, tb_, sv(0, 128, 8192, T), 128, T,
                           deps=[lu, q1] + s_up)
            oo = V(lambda h: h.tensor_tensor(hb, gb, ub, AL.mult), deps=[q2])
            rt = V(lambda h, ic=ic: h.reduce_max(sv(0, 128, 12800 + ic, 1), hb, axis=AX.X,
                                                 apply_absolute_value=True), deps=[oo, am0])
            st = dma("gpsimd", gt_st[ic * 128:(ic + 1) * 128, :], hb, deps=[fv(oo)])
            am_toks.append(rt)
            am_toks.append(st)
        r1 = V(lambda h: h.reduce_max(sv(0, 128, 12832, 1), sv(0, 128, 12800, 24), axis=AX.X), deps=am_toks)
        r2 = V(lambda h: h.tensor_copy(sv(0, 128, 12840, 8), sv(0, 128, 12832, 8)), deps=[r1])
        ps_last["r2t"] = r2
        go = global_amax([r2], sv(0, 128, 12840, 8), "o")
        lb = allreduce_max([go], 1)
        s_o = scale_from_amax(0, sc[6][:, 0:1], [lb])
        pr.barrier()
        for grp in range(3):
            qtoks = []
            for j in range(ND):
                ic = grp * ND + j
                lo = dma("sync", gb, gt_st[ic * 128:(ic + 1) * 128, :], deps=qtoks[-1:])
                qt = qdq_chain(bigA[:, j * T:(j + 1) * T], gb, sc[1][:, 0:1], sc[0][:, 0:1],
                               ub, hb, tb_, 128, T, deps=[lo] + s_o)
                qtoks.append(qt)
            lw = dma("sync", bass.AP(tensor=wb0[:].tensor, offset=wb0[:].offset,
                                     ap=[[wb0[:].ap[0][0], 128], [1024, 4], [1, 1024]]),
                     bass.AP(tensor=wdqT_dn.tensor, offset=grp * 8 * 128 * D,
                             ap=[[D, 128], [128 * D, 4], [1, 1024]]))
            lw2 = dma("sync", bass.AP(tensor=wb1[:].tensor, offset=wb1[:].offset,
                                      ap=[[wb1[:].ap[0][0], 128], [1024, 4], [1, 1024]]),
                      bass.AP(tensor=wdqT_dn.tensor, offset=(grp * 8 + 4) * 128 * D,
                              ap=[[D, 128], [128 * D, 4], [1, 1024]]))
            pr.barrier()
            for tt in range(NT):
                for ot in range(2):
                    pidx = (2 * tt + ot) % 2
                    mms = []
                    for j in range(ND):
                        wcol = wb0 if j < 4 else wb1
                        jj = j % 4
                        mm = TE(lambda h, wcol=wcol, j=j, jj=jj, tt=tt, ot=ot, pidx=pidx: h.matmul(
                            psums[pidx][:, :], bigA[:, j * T + tt * 128: j * T + (tt + 1) * 128],
                            wcol[:, jj * 1024 + ot * 512: jj * 1024 + ot * 512 + 512],
                            start=(j == 0), stop=(j == ND - 1)),
                            deps=([lw, lw2, ps_last.get(pidx)] if j == 0 else []), inc=(1 if j == ND - 1 else 0))
                        mms.append(mm)
                    ev = V(lambda h, pidx=pidx: h.tensor_copy(sv(0, 128, 16384 + pidx * 512, 512),
                                                              psums[pidx][:, :]),
                           deps=[mms[-1], ps_last.get("qkev%d" % pidx)])
                    ps_last[pidx] = ev
                    evf = fv(ev)
                    if grp == 0:
                        st = dma("gpsimd", sb_out[tt * 128:(tt + 1) * 128, ot * 512:(ot + 1) * 512],
                            sv(0, 128, 16384 + pidx * 512, 512), deps=[evf])
                    else:
                        st = pr.emit("gpsimd", lambda h, tt=tt, ot=ot, pidx=pidx: h.dma_start(
                            out=sb_out[tt * 128:(tt + 1) * 128, ot * 512:(ot + 1) * 512],
                            in_=sv(0, 128, 16384 + pidx * 512, 512), accum_op=AL.add),
                            deps=[evf, ("gq", pr.cnt["gq"])], inc=16, sem="gq")
                    ps_last["qkev%d" % pidx] = st
            pr.barrier()

    if NBLK >= 1:
        pre_template(0)
        load_bias_bcast(projb0, "pb")
        tvb = dma("sync", sv(0, 128, 18432 + 0, 1024), bass.AP(tensor=vb0.tensor, offset=0, ap=[[0, 128], [1, D]]))
        ps_last["vb"] = tvb
        attn_body(0)
        post_template(0)
    if NBLK >= 2:
        pre_template(1)
        load_bias_bcast(ffb_out, "ob")
        ffn0_body()
        post_template(1)
    if NBLK >= 3:
        pre_template(2)
        attn_body(1)
        post_template(2)
    if NBLK >= 4:
        ffn1_weight_dequant()
        pre_template(3)
        ffn1_body()
        post_template(3)

    if DBG:
        pr.barrier()
        dma("gpsimd", dbg["d_q"][:, :], q_st[:, :])
        dma("gpsimd", dbg["d_k"][:, :], k_st[:, :])
        dma("gpsimd", dbg["d_v"][:, :], v_st[:, :])
        dma("gpsimd", dbg["d_oT"][:, :], oT_st[:, :])
        dma("gpsimd", dbg["d_dn"][:, :], dn_dram[:, :])
        dma("gpsimd", dbg["d_sbout"][:, :], sb_out[:, :])
        dma("gpsimd", dbg["d_mh"][0:1, 0:8], bc_dram[0:1, 0:8])
        dma("gpsimd", dbg["d_vec"][:, :], vecflat[0:1, 0:VW])
        dma("gpsimd", dbg["d_m0"][:, :], m_dram[0:128, 0:MW])
        dma("gpsimd", dbg["d_xres"][:, :], x_res[:, :])
        if NBLK >= 4:
            dma("gpsimd", dbg["d_up0"][:, :], up_st[0:128, :])
            dma("gpsimd", dbg["d_oraw"][:, :], gt_st[0:128, :])
            dma("gpsimd", dbg["d_g2"][:, :], g2_st[0:128, :])
            dma("gpsimd", dbg["d_wdq"][:, :], wdqT_up[0:128, :])


    pr.emit("gpsimd", None, deps=[("gq", pr.cnt["gq"]), ("gpsimd", pr.cnt["gpsimd"])])

    from contextlib import ExitStack
    with ExitStack() as st:
        sems = {s: st.enter_context(nc.semaphore("sem_" + s)) for s in SEMS}
        block = st.enter_context(nc.Block())
        for eng in ENGINES:
            def make(eng):
                def body(h):
                    for fn, waits, inc, semname in pr.steps[eng]:
                        for peng, val in waits:
                            h.wait_ge(sems[peng], val)
                        if fn is None:
                            continue
                        ins = fn(h)
                        if inc:
                            ins.then_inc(sems[semname], inc)
                return body
            getattr(block, eng)(make(eng))
    return nc


_NC_CACHE = None


def kernel(L0_attn_x, L0_attn_qkv_w, L0_attn_qkv_b, L0_attn_proj_w, L0_attn_proj_b,
           L0_attn_rel_pos_emb, L0_ffn_Win, L0_ffn_bin_, L0_ffn_W1, L0_ffn_b1,
           L0_ffn_W2, L0_ffn_b2, L0_ffn_Wout, L0_ffn_bout,
           L1_attn_w_qkv, L1_attn_w_out, L1_ffn_w_gate, L1_ffn_w_up, L1_ffn_w_down):
    global _NC_CACHE
    if _NC_CACHE is None:
        _NC_CACHE = build_program()
    nc = _NC_CACHE
    f32 = np.float32
    common = {
        "qkvwT0": np.ascontiguousarray(np.asarray(L0_attn_qkv_w, f32).T),
        "qkvb0": np.ascontiguousarray(np.asarray(L0_attn_qkv_b, f32).reshape(24, 128).T),
        "vb0": np.ascontiguousarray(np.asarray(L0_attn_qkv_b, f32)[2048:3072].reshape(1, D)),
        "projT0": np.ascontiguousarray(np.asarray(L0_attn_proj_w, f32).T),
        "projb0": np.ascontiguousarray(np.asarray(L0_attn_proj_b, f32).reshape(1, D)),
        "relpos": np.ascontiguousarray(np.asarray(L0_attn_rel_pos_emb, f32)),
        "ffw_in": np.ascontiguousarray(np.asarray(L0_ffn_Win, f32)),
        "ffb_in": np.ascontiguousarray(np.asarray(L0_ffn_bin_, f32).reshape(64, 1)),
        "ffw1": np.ascontiguousarray(np.asarray(L0_ffn_W1, f32)),
        "ffb1": np.ascontiguousarray(np.asarray(L0_ffn_b1, f32)),
        "ffw2": np.ascontiguousarray(np.asarray(L0_ffn_W2, f32)),
        "ffb2": np.ascontiguousarray(np.asarray(L0_ffn_b2, f32).reshape(64, 1)),
        "ffw_out": np.ascontiguousarray(np.asarray(L0_ffn_Wout, f32)),
        "ffb_out": np.ascontiguousarray(np.asarray(L0_ffn_bout, f32).reshape(1, D)),
        "qkvwT1": np.ascontiguousarray(np.asarray(L1_attn_w_qkv, f32).T),
        "woutT1": np.ascontiguousarray(np.asarray(L1_attn_w_out, f32).T),
        "wgate": np.ascontiguousarray(np.asarray(L1_ffn_w_gate, f32)),
        "wup": np.ascontiguousarray(np.asarray(L1_ffn_w_up, f32)),
        "wdown": np.ascontiguousarray(np.asarray(L1_ffn_w_down, f32)),
    }
    x = np.asarray(L0_attn_x, f32)
    in_maps = [dict(common, x_in=np.ascontiguousarray(x[c])) for c in range(8)]
    res = run_bass_kernel_spmd(nc, in_maps, core_ids=list(range(8)))
    kernel.last_results = res.results
    return np.stack([res.results[c]["y_out"] for c in range(8)], axis=0)


# version 18 (final)
# speedup vs baseline: 1.0556x; 1.0525x over previous
import numpy as np
import concourse.bass as bass
import concourse.mybir as mybir
from concourse.bass_utils import run_bass_kernel_spmd

F32 = mybir.dt.float32
I32 = mybir.dt.int32
AL = mybir.AluOpType
AF = mybir.ActivationFunctionType
AX = mybir.AxisListType

B, T, D, H, DH = 8, 2048, 1024, 16, 64
NT, NQ, ND = T // 128, T // 512, D // 128
C32 = 12582912.0
C64 = 6755399441055744.0
BIGNEG = -1.0e30
EPS = 1e-5
PAD = 512
VW = 2688
MW = 2048 + PAD
NBLK = 4

ENGINES = ("sync", "tensor", "vector", "scalar", "gpsimd")
SEMS = ENGINES + ("gq",)


class Prog:
    def __init__(self):
        self.steps = {e: [] for e in ENGINES}
        self.cnt = {s: 0 for s in SEMS}
        self.waited = {(c, p): 0 for c in ENGINES for p in SEMS}

    def emit(self, eng, fn, deps=(), inc=0, sem=None):
        waits = {}
        for tok in deps:
            if tok is None:
                continue
            peng, val = tok
            if val > self.waited[(eng, peng)]:
                waits[peng] = max(waits.get(peng, 0), val)
        for peng, val in waits.items():
            self.waited[(eng, peng)] = val
        tok = None
        if inc:
            s = sem or eng
            self.cnt[s] += inc
            tok = (s, self.cnt[s])
        self.steps[eng].append((fn, sorted(waits.items()), inc, sem or eng))
        return tok

    def tok(self, eng):
        return (eng, self.cnt[eng])

    def barrier(self):
        toks = [(s, self.cnt[s]) for s in SEMS]
        for e in ENGINES:
            for peng, val in toks:
                if peng != e and val > self.waited[(e, peng)]:
                    self.waited[(e, peng)] = val
                    self.steps[e].append((None, [(peng, val)], 0, e))


def build_program():
    nc = bass.Bass("TRN2", target_bir_lowering=False, debug=False)

    def din(name, shape):
        return nc.dram_tensor(name, list(shape), F32, kind="ExternalInput").ap()

    x_in = din("x_in", (T, D))
    qkvwT0 = din("qkvwT0", (D, 3 * D))
    qkvb0 = din("qkvb0", (128, 24))
    vb0 = din("vb0", (1, D))
    projT0 = din("projT0", (D, D))
    projb0 = din("projb0", (1, D))
    relpos = din("relpos", (2047, D))
    ffw_in = din("ffw_in", (D, 64))
    ffb_in = din("ffb_in", (64, 1))
    ffw1 = din("ffw1", (64, 16))
    ffb1 = din("ffb1", (64, 16))
    ffw2 = din("ffw2", (64, 16))
    ffb2 = din("ffb2", (64, 1))
    ffw_out = din("ffw_out", (64, D))
    ffb_out = din("ffb_out", (1, D))
    qkvwT1 = din("qkvwT1", (D, 3 * D))
    woutT1 = din("woutT1", (D, D))
    wgate = din("wgate", (3 * D, D))
    wup = din("wup", (3 * D, D))
    wdown = din("wdown", (D, 3 * D))
    y_out = nc.dram_tensor("y_out", [T, D], F32, kind="ExternalOutput").ap()
    DBG = False
    dbg = {}
    if DBG:
        for nm, shp in (("d_q", (D, T)), ("d_k", (D, T)), ("d_v", (T, D)), ("d_oT", (D, T)),
                        ("d_dn", (16, T)), ("d_sbout", (T, D)), ("d_mh", (1, 32)), ("d_vec", (1, VW)), ("d_m0", (128, MW)), ("d_tab", (16, 2047)), ("d_xres", (T, D)),
                        ("d_up0", (128, T)), ("d_oraw", (128, T)), ("d_g2", (128, T)),
                        ("d_wdq", (128, 3 * D)), ("d_cc", (4, 8)), ("d_xq", (128, T))):
            dbg[nm] = nc.dram_tensor(nm, list(shp), F32, kind="ExternalOutput").ap()

    def dsc(name, shape):
        return nc.dram_tensor(name, list(shape), F32, kind="Internal").ap()

    x_res = dsc("x_res", (T, D))
    sb_out = dsc("sb_out", (T, D))
    q_st = dsc("q_st", (D, T))
    k_st = dsc("k_st", (D, T))
    v_st = dsc("v_st", (T, D))
    oT_st = dsc("oT_st", (D, T))
    vec_dram = dsc("vec_dram", (18, VW))
    vecflat = dsc("vecflat", (1, 18 * VW + 128))
    m_dram = dsc("m_dram", (128, 18 * VW))
    dn_dram = dsc("dn_dram", (16, T))
    bc_dram = dsc("bc_dram", (4, 8))
    bmax_dram = dsc("bmax_dram", (16, 8))
    cc_in = dsc("cc_in", (1, 8))
    cc_out = dsc("cc_out", (1, 8))
    up_st = dsc("up_st", (3 * D, T))
    gt_st = dsc("gt_st", (3 * D, T))
    g2_st = dsc("g2_st", (3 * D, T))
    wdqT_up = dsc("wdqT_up", (D, 3 * D))
    wdqT_gt = dsc("wdqT_gt", (D, 3 * D))
    wdqT_dn = dsc("wdqT_dn", (3 * D, D))

    def sbuf(name, shape, dtype=F32):
        return nc.alloc_sbuf_tensor(name, list(shape), dtype)

    bigA = sbuf("bigA", (128, ND * T))
    wb0 = sbuf("wb0", (128, 4096))
    wb1 = sbuf("wb1", (128, 4096))
    scr = sbuf("scr", (128, 19456))
    ident = sbuf("ident", (128, 128))
    e_t = sbuf("e_t", (128, 8))
    e_t0 = sbuf("e_t0", (128, 8))
    sc = [sbuf("sc%d" % i, (128, 16)) for i in range(8)]
    bias_sb = sbuf("bias_sb", (128, 24))
    fsm = sbuf("fsm", (64, 72))

    psums = [nc.alloc_psum_tensor("ps%d" % i, [128, 512], F32) for i in range(7)]
    ps_last = {}

    pr = Prog()

    def dma(qeng, out, in_, deps=()):
        qs = "gq" if qeng == "gpsimd" else qeng
        gate = (qs, pr.cnt[qs])
        return pr.emit(qeng, lambda h, o=out, i=in_: h.dma_start(out=o, in_=i),
                       deps=list(deps) + [gate], inc=16, sem=qs)

    def V(fn, deps=(), inc=1):
        return pr.emit("vector", fn, deps=deps, inc=inc)

    def S(fn, deps=(), inc=1):
        return pr.emit("scalar", fn, deps=deps, inc=inc)

    def TE(fn, deps=(), inc=1):
        return pr.emit("tensor", fn, deps=deps, inc=inc)

    def G(fn, deps=(), inc=1):
        return pr.emit("gpsimd", fn, deps=deps, inc=inc)

    def sv(p0, np_, c0, nc_):
        return scr[p0:p0 + np_, c0:c0 + nc_]

    fence_prev = {"v": None, "s": None}

    def fv(tok):
        t = V(lambda h: h.memset(junkV[:, 0:8], 1.0), deps=[tok, fence_prev["v"]])
        fence_prev["v"] = t
        return t

    def fs(tok):
        t = S(lambda h: h.copy(junkS[:, 0:8], e_t[:, 0:8]), deps=[tok, fence_prev["s"]])
        fence_prev["s"] = t
        return t

    t0 = V(lambda h: h.memset(e_t0[:], 1.0))
    t1 = V(lambda h: h.tensor_single_scalar(e_t[:, 0:8], e_t0[:, 0:8], 2.0 ** -30, AL.add), deps=[t0])
    t2 = V(lambda h: h.tensor_single_scalar(e_t0[:, 0:8], e_t[:, 0:8], 1.0, AL.is_le), deps=[t1])
    tk_e = V(lambda h: h.tensor_copy(e_t[:, 0:8], e_t0[:, 0:8]), deps=[t2])
    tk_eps = V(lambda h: h.memset(sc[7][:, 8:16], EPS))
    ones8 = sbuf("ones8", (128, 8))
    junkV = sbuf("junkV", (128, 8))
    junkS = sbuf("junkS", (128, 8))
    tk_ones = V(lambda h: h.memset(ones8[:], 1.0))

    iotv = sv(0, 128, 0, 128).bitcast(I32)
    tio = G(lambda h: h.iota(iotv, [[1, 128]], base=0, channel_multiplier=-1))
    tk_id = V(lambda h: h.tensor_single_scalar(ident[:], iotv, 0.0, AL.is_equal), deps=[tio])

    tk_qb = dma("sync", bias_sb[:, :], qkvb0[:, :])
    tfs = [dma("sync", fsm[0:64, 0:1], ffb_in[:, :]),
           dma("sync", fsm[0:64, 1:17], ffw1[:, :]),
           dma("sync", fsm[0:64, 17:33], ffb1[:, :]),
           dma("sync", fsm[0:64, 33:49], ffw2[:, :]),
           dma("sync", fsm[0:64, 49:50], ffb2[:, :])]

    tabT = sv(0, 16, 4096, 2047)
    prev_tr = {0: None, 1: None}
    prev_rd = {0: None, 1: None}
    for rc in range(16):
        rows = 128 if rc < 15 else 2047 - 15 * 128
        par = rc % 2
        c0 = par * 1280
        ld = dma("sync", wb0[:rows, c0:c0 + D], relpos[rc * 128: rc * 128 + rows, :],
                 deps=[prev_rd[par]])
        rd = V(lambda h, rows=rows, c0=c0: h.reduce_sum(
            wb0[:rows, c0 + D:c0 + D + 16], wb0[:rows, c0:c0 + D].rearrange("p (a b) -> p a b", b=64), axis=AX.X),
            deps=[ld, prev_tr[par]])
        prev_rd[par] = rd
        tr = TE(lambda h, rows=rows, c0=c0: h.transpose(psums[6][0:16, 0:rows], wb0[:rows, c0 + D:c0 + D + 16],
                                                        ident[:rows, :rows]),
                deps=[rd, tk_id, ps_last.get(6)])
        prev_tr[par] = tr
        cp = V(lambda h, rc=rc, rows=rows: h.tensor_copy(
            sv(0, 16, 4096 + rc * 128, rows), psums[6][0:16, 0:rows]), deps=[tr])
        ps_last[6] = cp

    vecp = sv(0, 16, 6800, VW)
    OFF0 = PAD + 127
    tv0 = V(lambda h: h.memset(sv(0, 32, 6800, VW), BIGNEG * 8.0))
    tv1 = V(lambda h: h.tensor_single_scalar(
        sv(0, 16, 6800 + OFF0, 1024), sv(0, 16, 4096 + 1023, 1024), 8.0, AL.mult),
        deps=[tv0, ps_last[6]])
    tv2 = tv1
    wdt = 0
    tv2 = V(lambda h: h.tensor_copy(sv(0, 16, 6800 + OFF0 + 1024, 1), sv(0, 16, 6800 + OFF0 + 1023, 1)), deps=[tv1])
    wdt = 1
    while wdt < 1024:
        wn = min(wdt, 1024 - wdt)
        tv2 = V(lambda h, wdt=wdt, wn=wn: h.tensor_copy(
            sv(0, 16, 6800 + OFF0 + 1024 + wdt, wn), sv(0, 16, 6800 + OFF0 + 1024, wn)), deps=[tv2])
        wdt += wn
    tv3 = V(lambda h: h.memset(sv(0, 1, 9488, VW), BIGNEG * 8.0))
    tv3 = V(lambda h: h.memset(sv(0, 1, 9488 + OFF0, VW - OFF0), 0.0), deps=[tv3])
    tv4 = V(lambda h: h.memset(sv(0, 1, 12176, VW), BIGNEG * 8.0))
    tv4 = V(lambda h: h.memset(sv(0, 1, 12176 + OFF0, 512), 0.0), deps=[tv4])
    tbm = V(lambda h: h.reduce_max(sc[7][0:16, 0:1], sv(0, 16, 6800 + OFF0, 2048), axis=AX.X), deps=[tv2])
    wd = 1
    while wd < 8:
        tbm = V(lambda h, wd=wd: h.tensor_copy(sc[7][0:16, wd:2 * wd], sc[7][0:16, 0:wd]), deps=[tbm])
        wd *= 2
    tbm = dma("gpsimd", bmax_dram[:, :], sc[7][0:16, 0:8], deps=[fv(tbm)])
    if DBG:
        dma("gpsimd", dbg["d_tab"][:, :], sv(0, 16, 4096, 2047), deps=[pr.tok("vector")])
    tvf = fv(tv2)
    tvf = V(lambda h: h.memset(junkV[:, 0:8], 4.0), deps=[tv3, tv4, tvf, fence_prev["v"]])
    fence_prev["v"] = tvf
    tvds = [dma("gpsimd", vecflat[0:1, hh * VW:(hh + 1) * VW], scr[hh:hh + 1, 6800:6800 + VW],
                deps=[tvf]) for hh in range(16)]
    tvds.append(dma("gpsimd", vecflat[0:1, 16 * VW:17 * VW], sv(0, 1, 9488, VW), deps=[tvf]))
    tvds.append(dma("gpsimd", vecflat[0:1, 17 * VW:18 * VW], sv(0, 1, 12176, VW), deps=[tvf]))
    tz = V(lambda h: h.memset(sc[6][0:1, 0:8], 0.0), deps=[fence_prev["v"]])
    tz = fv(tz)
    for j in range(16):
        tvds.append(dma("gpsimd", vecflat[0:1, 18 * VW + j * 8:18 * VW + (j + 1) * 8], sc[6][0:1, 0:8], deps=[tz]))
    tvd = None
    for p in range(128):
        tvd = dma("gpsimd", m_dram[p:p + 1, :], vecflat[0:1, 127 - p:127 - p + 18 * VW], deps=tvds)
    pr.barrier()

    def ln_stats(src_dram, nblk_tag):
        toks = []
        for tt in range(NT):
            buf = sv(0, 128, (tt % 2) * D, D)
            ld = dma("sync", buf, src_dram[tt * 128:(tt + 1) * 128, :], deps=toks[-2:])
            tv = V(lambda h, buf=buf, tt=tt: h.reduce_sum(sc[0][:, tt:tt + 1], buf, axis=AX.X), deps=[ld])
            ts = S(lambda h, buf=buf, tt=tt: h.activation(
                sv(0, 128, 2 * D, D), buf, AF.Square, accum_out=sc[1][:, tt:tt + 1]), deps=[ld, toks[-1] if toks else None])
            toks.append(tv)
            toks.append(ts)
        return toks[-2:]

    def ln_mr(deps):
        a1 = V(lambda h: h.tensor_single_scalar(sc[2][:], sc[0][:], 1.0 / D, AL.mult), deps=deps)
        a2 = V(lambda h: h.tensor_single_scalar(sc[4][:], sc[1][:], 1.0 / D, AL.mult), deps=deps)
        a3 = V(lambda h: h.tensor_tensor(sc[5][:], sc[2][:], sc[2][:], AL.mult), deps=[a1])
        a4 = V(lambda h: h.tensor_tensor(sc[6][:], sc[4][:], sc[5][:], AL.subtract), deps=[a2, a3])
        a4c = V(lambda h: h.tensor_single_scalar(sc[4][:], sc[6][:], 0.0, AL.max), deps=[a4])
        a5 = S(lambda h: h.activation(sc[5][:], sc[4][:], AF.Sqrt, bias=sc[7][:, 8:9]), deps=[a4c, tk_eps])
        a6 = V(lambda h: h.reciprocal(sc[3][:], sc[5][:]), deps=[a5])
        return [a1, a6]

    def pre_template(blk):
        src = x_in if blk == 0 else x_res
        stok = ln_stats(src, blk)
        mr = ln_mr(stok)
        pr.barrier()
        prev = []
        for tt in range(NT):
            buf = sv(0, 128, (tt % 2) * D, D)
            nbuf = sv(0, 128, (2 + tt % 2) * D, D)
            ld = dma("sync", buf, src[tt * 128:(tt + 1) * 128, :], deps=prev)
            nm = V(lambda h, buf=buf, nbuf=nbuf, tt=tt: h.tensor_scalar(
                nbuf, buf, sc[2][:, tt:tt + 1], sc[3][:, tt:tt + 1], AL.subtract, AL.mult),
                deps=[ld] + mr)
            prev = []
            for c in range(ND):
                pidx = c % 4
                tr = TE(lambda h, nbuf=nbuf, c=c, pidx=pidx: h.transpose(
                    psums[pidx][:, 0:128], nbuf[:, c * 128:(c + 1) * 128], ident[:]),
                    deps=[nm, tk_id, ps_last.get(pidx)])
                cp = V(lambda h, c=c, tt=tt, pidx=pidx: h.tensor_copy(
                    bigA[:, c * T + tt * 128: c * T + (tt + 1) * 128], psums[pidx][:, 0:128]),
                    deps=[tr])
                ps_last[pidx] = cp
                prev.append(cp)
        pr.barrier()

    def post_template(blk):
        stok = ln_stats(sb_out, blk)
        mr = ln_mr(stok)
        pr.barrier()
        src = x_in if blk == 0 else x_res
        dst = y_out if blk == NBLK - 1 else x_res
        prev = []
        for tt in range(NT):
            ob = sv(0, 128, (tt % 2) * D, D)
            xb = sv(0, 128, (2 + tt % 2) * D, D)
            lo = dma("sync", ob, sb_out[tt * 128:(tt + 1) * 128, :], deps=prev)
            lx = dma("sync", xb, x_res[tt * 128:(tt + 1) * 128, :] if blk > 0 else x_in[tt * 128:(tt + 1) * 128, :], deps=prev)
            nm = V(lambda h, ob=ob, tt=tt: h.tensor_scalar(
                ob, ob, sc[2][:, tt:tt + 1], sc[3][:, tt:tt + 1], AL.subtract, AL.mult),
                deps=[lo] + mr)
            ad = V(lambda h, ob=ob, xb=xb: h.tensor_tensor(xb, xb, ob, AL.add), deps=[nm, lx])
            st = dma("gpsimd", dst[tt * 128:(tt + 1) * 128, :], xb, deps=[fv(ad)])
            prev = [st]
        pr.barrier()

    def attn_body(layer):
        is0 = layer == 0
        wT = qkvwT0 if is0 else qkvwT1
        pT = projT0 if is0 else woutT1
        for oc in range(16):
            wcol = wb0 if oc % 2 == 0 else wb1
            lw = dma("sync", bass.AP(tensor=wcol[:].tensor, offset=wcol[:].offset, ap=[[wcol[:].ap[0][0], 128], [128, 8], [1, 128]]),
                     bass.AP(tensor=wT.tensor, offset=oc * 128, ap=[[3 * D, 128], [128 * 3 * D, 8], [1, 128]]),
                     deps=[ps_last.get("w%d" % (oc % 2))])
            for tt in range(NQ):
                pidx = 4 + tt % 2
                mms = []
                for dc in range(ND):
                    mm = TE(lambda h, wcol=wcol, dc=dc, tt=tt, pidx=pidx: h.matmul(
                        psums[pidx][:, :], wcol[:, dc * 128:(dc + 1) * 128],
                        bigA[:, dc * T + tt * 512: dc * T + (tt + 1) * 512],
                        start=(dc == 0), stop=(dc == ND - 1)),
                        deps=([lw, ps_last.get(pidx)] if dc == 0 else []), inc=(1 if dc == ND - 1 else 0))
                    mms.append(mm)
                if is0:
                    ev = S(lambda h, oc=oc, tt=tt, pidx=pidx: h.activation(
                        sv(0, 128, 16384 + (tt % 2) * 512, 512), psums[pidx][:, :], AF.Identity,
                        bias=bias_sb[:, oc:oc + 1]), deps=[mms[-1], tk_qb, ps_last.get("qkev%d" % (tt % 2))])
                else:
                    ev = S(lambda h, tt=tt, pidx=pidx: h.copy(
                        sv(0, 128, 16384 + (tt % 2) * 512, 512), psums[pidx][:, :]),
                        deps=[mms[-1], ps_last.get("qkev%d" % (tt % 2))])
                ps_last[pidx] = ev
                dstt = q_st if oc < 8 else k_st
                row0 = (oc % 8) * 128
                st = dma("gpsimd", dstt[row0:row0 + 128, tt * 512:(tt + 1) * 512],
                         sv(0, 128, 16384 + (tt % 2) * 512, 512), deps=[fs(ev) if is0 else fs(ev)])
                ps_last["w%d" % (oc % 2)] = st
                ps_last["qkev%d" % (tt % 2)] = st
        pr.barrier()
        for ot in range(2):
            lw = dma("sync", bass.AP(tensor=wb0[:].tensor, offset=wb0[:].offset, ap=[[wb0[:].ap[0][0], 128], [512, 8], [1, 512]]),
                     bass.AP(tensor=wT.tensor, offset=2 * D + ot * 512, ap=[[3 * D, 128], [128 * 3 * D, 8], [1, 512]]),
                     deps=[pr.tok("tensor")])
            for tt in range(NT):
                pidx = 4 + tt % 2
                mms = []
                for dc in range(ND):
                    mm = TE(lambda h, dc=dc, tt=tt, ot=ot, pidx=pidx: h.matmul(
                        psums[pidx][:, :], bigA[:, dc * T + tt * 128: dc * T + (tt + 1) * 128],
                        wb0[:, dc * 512:(dc + 1) * 512],
                        start=(dc == 0), stop=(dc == ND - 1)),
                        deps=([lw, ps_last.get(pidx)] if dc == 0 else []), inc=(1 if dc == ND - 1 else 0))
                    mms.append(mm)
                if is0:
                    ev = V(lambda h, tt=tt, ot=ot, pidx=pidx: h.scalar_tensor_tensor(
                        sv(0, 128, 16384 + (tt % 2) * 512, 512), psums[pidx][:, :], 1.0,
                        sv(0, 128, 18432 + ot * 512, 512), AL.bypass, AL.add),
                        deps=[mms[-1], ps_last.get("vb"), ps_last.get("qkev%d" % (tt % 2))])
                else:
                    ev = V(lambda h, tt=tt, pidx=pidx: h.tensor_copy(
                        sv(0, 128, 16384 + (tt % 2) * 512, 512), psums[pidx][:, :]),
                        deps=[mms[-1], ps_last.get("qkev%d" % (tt % 2))])
                ps_last[pidx] = ev
                st = dma("gpsimd", v_st[tt * 128:(tt + 1) * 128, ot * 512:(ot + 1) * 512],
                         sv(0, 128, 16384 + (tt % 2) * 512, 512), deps=[fv(ev)])
                ps_last["qkev%d" % (tt % 2)] = st
        pr.barrier()

        qp = sv(0, 128, 0, 2048)
        kp = sv(0, 128, 2048, 2048)
        vsb_base = 4096
        mA_base = 6176
        mB_base = 8736
        mC = sv(0, 16, 11296, MW)
        qa = sv(0, 128, 13856, 2048)
        u0 = sv(0, 128, 15904, 512)
        u1 = sv(0, 128, 16416, 512)
        pp0 = sv(0, 128, 16928, 512)
        pp1 = sv(0, 128, 17440, 512)
        evb = sv(0, 128, 17952, 512)
        tko = V(lambda h: h.memset(
            bass.AP(tensor=scr[:].tensor, offset=scr[:].offset + vsb_base + 64,
                    ap=[[scr[:].ap[0][0], 128], [130, NT], [65, 2]]), 1.0))
        if not is0:
            mwA = dma("sync", sv(0, 128, mA_base, MW), m_dram[0:128, 17 * VW:17 * VW + MW], deps=[tvd])
            mwB = dma("sync", sv(0, 128, mB_base, MW), m_dram[0:128, 17 * VW:17 * VW + MW], deps=[tvd])
            mwB = V(lambda h: h.memset(sv(0, 16, mB_base, MW), BIGNEG * 8.0), deps=[mwB])
            mwC = dma("sync", mC, m_dram[0:16, 16 * VW:16 * VW + MW], deps=[tvd])
            mtoks = [mwA, mwB, mwC]
        for p in range(8):
            hA, hB = 2 * p, 2 * p + 1
            lq = dma("sync", qp, q_st[p * 128:(p + 1) * 128, :], deps=[ps_last.get("attn_done")])
            lk = dma("sync", kp, k_st[p * 128:(p + 1) * 128, :], deps=[ps_last.get("attn_done")])
            lv0 = dma("sync", bass.AP(tensor=scr[:].tensor, offset=scr[:].offset + vsb_base,
                                      ap=[[scr[:].ap[0][0], 128], [130, NT], [1, 64]]),
                      bass.AP(tensor=v_st.tensor, offset=p * 128, ap=[[D, 128], [128 * D, NT], [1, 64]]),
                      deps=[tko, ps_last.get("attn_done")])
            lv1 = dma("sync", bass.AP(tensor=scr[:].tensor, offset=scr[:].offset + vsb_base + 65,
                                      ap=[[scr[:].ap[0][0], 128], [130, NT], [1, 64]]),
                      bass.AP(tensor=v_st.tensor, offset=p * 128 + 64, ap=[[D, 128], [128 * D, NT], [1, 64]]),
                      deps=[tko, ps_last.get("attn_done")])
            lv = lv1
            if is0:
                mwA = dma("sync", sv(0, 128, mA_base, MW), m_dram[0:128, hA * VW:hA * VW + MW],
                          deps=[tvd, ps_last.get("attn_done")])
                mwB = dma("sync", sv(0, 128, mB_base, MW), m_dram[0:128, hB * VW:hB * VW + MW],
                          deps=[tvd, ps_last.get("attn_done")])
                mtoks = [mwA, mwB]
            z6 = V(lambda h: h.memset(sc[6][0:1, 0:8], 0.0))
            sqk = S(lambda h: h.activation(qa, kp, AF.Square), deps=[lk, z6])
            nrm_toks = []
            for hh2 in range(2):
                for qt in range(NQ):
                    mm = TE(lambda h, hh2=hh2, qt=qt: h.matmul(
                        psums[6][0:1, :], ones8[hh2 * 64:hh2 * 64 + 64, 0:1],
                        qa[hh2 * 64:hh2 * 64 + 64, qt * 512:(qt + 1) * 512],
                        start=True, stop=True), deps=[sqk, tk_ones, ps_last.get(6)])
                    cx = V(lambda h, qt=qt: h.tensor_copy(sv(0, 1, 15904 + qt * 512, 512), psums[6][0:1, :]),
                           deps=[mm, nrm_toks[-1] if nrm_toks else None])
                    ps_last[6] = cx
                    nrm_toks.append(cx)
                nrm_toks.append(V(lambda h, hh2=hh2: h.reduce_max(sc[6][0:1, 1 + hh2:2 + hh2],
                                                                  sv(0, 1, 15904, 2048), axis=AX.X),
                                  deps=nrm_toks[-4:] + [z6]))
            sqq = S(lambda h: h.activation(qa, qp, AF.Square), deps=[lq] + nrm_toks)
            for hh2 in range(2):
                for qt in range(NQ):
                    mm = TE(lambda h, hh2=hh2, qt=qt: h.matmul(
                        psums[6][0:1, :], ones8[hh2 * 64:hh2 * 64 + 64, 0:1],
                        qa[hh2 * 64:hh2 * 64 + 64, qt * 512:(qt + 1) * 512],
                        start=True, stop=True), deps=[sqq, tk_ones, ps_last.get(6)])
                    cx = V(lambda h, qt=qt: h.tensor_copy(sv(0, 1, 15904 + qt * 512, 512), psums[6][0:1, :]),
                           deps=[mm, nrm_toks[-1] if nrm_toks else None])
                    ps_last[6] = cx
                    nrm_toks.append(cx)
                nrm_toks.append(V(lambda h, hh2=hh2: h.reduce_max(sc[6][0:1, 3 + hh2:4 + hh2],
                                                                  sv(0, 1, 15904, 2048), axis=AX.X),
                                  deps=nrm_toks[-4:] + [z6]))
            bst = dma("gpsimd", bc_dram[0:1, 0:8], sc[6][0:1, 0:8], deps=[fv(nrm_toks[-1])] + nrm_toks)
            mhld = dma("sync", sc[2][:, 0:8], bass.AP(tensor=bc_dram.tensor, offset=0, ap=[[0, 128], [1, 8]]),
                       deps=[bst])
            wtoks = []
            for idx, (dt_, dcol) in enumerate(((sc[0], 8), (sc[1], 8), (sc[0], 0), (sc[1], 0))):
                tkc = V(lambda h, dt_=dt_, dcol=dcol, idx=idx: h.tensor_copy(dt_[:, dcol:dcol + 1],
                                                                             sc[2][:, 1 + idx:2 + idx]), deps=[mhld])
                wd = 1
                while wd < 8:
                    tkc = V(lambda h, dt_=dt_, dcol=dcol, wd=wd: h.tensor_copy(
                        dt_[:, dcol + wd:dcol + 2 * wd], dt_[:, dcol:dcol + wd]), deps=[tkc])
                    wd *= 2
                wtoks.append(tkc)
            pA = V(lambda h: h.tensor_tensor(sc[3][:, 0:8], sc[0][:, 0:8], sc[0][:, 8:16], AL.mult), deps=wtoks)
            pB = V(lambda h: h.tensor_tensor(sc[3][:, 8:16], sc[1][:, 0:8], sc[1][:, 8:16], AL.mult), deps=wtoks)
            sq2 = S(lambda h: h.activation(sc[2][:, 0:16], sc[3][:, 0:16], AF.Sqrt), deps=[pA, pB])
            if is0:
                bmA = dma("sync", sc[4][:, 0:8], bass.AP(tensor=bmax_dram.tensor, offset=hA * 8, ap=[[0, 128], [1, 8]]),
                          deps=[tbm])
                bmB = dma("sync", sc[4][:, 8:16], bass.AP(tensor=bmax_dram.tensor, offset=hB * 8, ap=[[0, 128], [1, 8]]),
                          deps=[tbm])
                su = V(lambda h: h.tensor_tensor(sc[5][:, 0:16], sc[2][:, 0:16], sc[4][:, 0:16], AL.add),
                       deps=[sq2, bmA, bmB])
            else:
                su = V(lambda h: h.tensor_single_scalar(sc[5][:, 0:16], sc[2][:, 0:16], 0.0, AL.add), deps=[sq2])
            m1 = V(lambda h: h.tensor_single_scalar(sc[6][:, 0:16], sc[5][:, 0:16], -0.125, AL.mult), deps=[su])
            mh_ap = [sc[6][:, 0:1], sc[6][:, 8:9]]
            mh_ap16 = [sc[6][0:16, 0:1], sc[6][0:16, 8:9]]
            mh_toks = [m1]
            dn_toks = []
            for hh in range(2):
                hcur = hA + hh
                mbase = mA_base if (is0 and hh == 0) else (mB_base if is0 else mA_base)
                for Q0 in range(NQ):
                    if is0:
                        k0s = list(range(0, min(NT, 4 * Q0 + 4)))
                        k0s_pre = []
                    else:
                        k0s = list(range(max(0, 4 * (Q0 - 1)), 4 * Q0 + 4))
                        k0s_pre = [("pre", Q0)]
                    psS = 4
                    psO = 5
                    nk = len(k0s) + len(k0s_pre)
                    acc_i = 0
                    evtok = ps_last.get(psO)
                    for item in k0s_pre + k0s:
                        first = acc_i == 0
                        last = acc_i == nk - 1
                        if item == ("pre", Q0):
                            mm1 = TE(lambda h, hh=hh, Q0=Q0: h.matmul(
                                psums[psS][0:16, :], kp[hh * 64:(hh + 1) * 64, 0:16],
                                qp[hh * 64:(hh + 1) * 64, Q0 * 512:(Q0 + 1) * 512], start=True, stop=True),
                                deps=[lq, lk, ps_last.get(psS)])
                            if Q0 == 0:
                                uu = V(lambda h, Q0=Q0: h.tensor_tensor(
                                    u0[0:16, :], psums[psS][0:16, :], mC[0:16, Q0 * 512 + PAD: Q0 * 512 + PAD + 512], AL.add),
                                    deps=[mm1, ps_last.get("ee0")] + (mtoks if not is0 else []))
                                ee = S(lambda h, hh=hh: h.activation(pp0[0:16, :], u0[0:16, :], AF.Exp,
                                                                     bias=mh_ap16[hh], scale=0.125),
                                       deps=[uu, ps_last.get("mo0")] + mh_toks)
                            else:
                                ee = S(lambda h, hh=hh: h.activation(pp0[0:16, :], psums[psS][0:16, :], AF.Exp,
                                                                     bias=mh_ap16[hh], scale=0.125),
                                       deps=[mm1, ps_last.get("mo0")] + mh_toks)
                            ps_last[psS] = ee
                            ps_last["ee0"] = ee
                            mo = TE(lambda h, hh=hh, first=first, last=last: h.matmul(
                                psums[psO][0:65, :], bass.AP(tensor=scr[:].tensor,
                                                             offset=scr[:].offset + vsb_base + hh * 65,
                                                             ap=[[scr[:].ap[0][0], 16], [1, 65]]),
                                pp0[0:16, :], start=first, stop=last),
                                deps=[ee, lv0, lv1] + ([evtok] if first else []), inc=1)
                            ps_last["mo0"] = mo
                        else:
                            K0 = item
                            base = Q0 * 512 - K0 * 128
                            mb = mbase if (is0 or K0 > 0) else mB_base
                            mm1 = TE(lambda h, hh=hh, K0=K0, Q0=Q0: h.matmul(
                                psums[psS][:, :], kp[hh * 64:(hh + 1) * 64, K0 * 128:(K0 + 1) * 128],
                                qp[hh * 64:(hh + 1) * 64, Q0 * 512:(Q0 + 1) * 512], start=True, stop=True),
                                deps=[lq, lk, ps_last.get(psS)])
                            par2 = acc_i % 2
                            ub = u0 if par2 == 0 else u1
                            ppb = pp0 if par2 == 0 else pp1
                            uu = V(lambda h, ub=ub, mb=mb, base=base: h.tensor_tensor(
                                ub, psums[psS][:, :], sv(0, 128, mb + base + PAD, 512), AL.add),
                                deps=[mm1, ps_last.get("ee%d" % par2)] + mtoks)
                            ps_last[psS] = uu
                            ee = S(lambda h, ub=ub, ppb=ppb, hh=hh: h.activation(
                                ppb, ub, AF.Exp, bias=mh_ap[hh], scale=0.125),
                                deps=[uu, ps_last.get("mo%d" % par2)] + mh_toks)
                            ps_last["ee%d" % par2] = ee
                            mo = TE(lambda h, hh=hh, K0=K0, ppb=ppb, first=first, last=last: h.matmul(
                                psums[psO][0:65, :], bass.AP(tensor=scr[:].tensor,
                                                             offset=scr[:].offset + vsb_base + K0 * 130 + hh * 65,
                                                             ap=[[scr[:].ap[0][0], 128], [1, 65]]),
                                ppb, start=first, stop=last),
                                deps=[ee, lv0, lv1] + ([evtok] if first else []), inc=1)
                            ps_last["mo%d" % par2] = mo
                        acc_i += 1
                    e1 = V(lambda h: h.tensor_copy(evb[0:64, :], psums[psO][0:64, :]),
                           deps=[mo, ps_last.get("evbst")])
                    e2 = V(lambda h, Q0=Q0: h.tensor_copy(qa[64:65, Q0 * 512:(Q0 + 1) * 512], psums[psO][64:65, :]),
                           deps=[mo, ps_last.get("dnst")])
                    ps_last[psO] = e2
                    st = dma("gpsimd", oT_st[hcur * 64:hcur * 64 + 64, Q0 * 512:(Q0 + 1) * 512], evb[0:64, :],
                             deps=[fv(e1)])
                    ps_last["evbst"] = st
                    dn_toks.append(st)
                dn_st = dma("gpsimd", dn_dram[hcur:hcur + 1, :], qa[64:65, :], deps=[fv(ps_last[psO])])
                ps_last["dnst"] = dn_st
                dn_toks.append(dn_st)
                pr.barrier()
            ps_last["attn_done"] = dn_toks[-1]
            pr.barrier()
        ldn = dma("sync", sv(0, 16, 0, T), dn_dram[:, :])
        rdn = V(lambda h: h.reciprocal(sv(0, 16, 2048, T), sv(0, 16, 0, T)), deps=[ldn])
        sdn = dma("gpsimd", dn_dram[:, :], sv(0, 16, 2048, T), deps=[fv(rdn)])
        pr.barrier()

        if is0:
            ps_last["pb"] = dma("sync", sv(0, 128, 18432, 1024),
                                bass.AP(tensor=projb0.tensor, offset=0, ap=[[0, 128], [1, D]]),
                                deps=[pr.tok("vector"), pr.tok("scalar")])
        for ot in range(2):
            lw = dma("sync", bass.AP(tensor=wb1[:].tensor, offset=wb1[:].offset, ap=[[wb1[:].ap[0][0], 128], [512, 8], [1, 512]]),
                     bass.AP(tensor=pT.tensor, offset=ot * 512, ap=[[D, 128], [128 * D, 8], [1, 512]]),
                     deps=[pr.tok("tensor")])
            if ot == 0:
                ntoks = []
                for c in range(ND):
                    lo = dma("sync", sv(0, 128, 0, T), oT_st[c * 128:(c + 1) * 128, :], deps=[sdn, ntoks[-1] if ntoks else None])
                    lr0 = dma("sync", sv(0, 128, 2048, T)[0:64, :],
                              bass.AP(tensor=dn_dram.tensor, offset=2 * c * T, ap=[[0, 64], [1, T]]),
                              deps=[sdn, ntoks[-1] if ntoks else None])
                    lr1 = dma("sync", bass.AP(tensor=scr[:].tensor, offset=scr[:].offset + 2048,
                                              ap=[[scr[:].ap[0][0], 128], [1, T]])[64:128, :],
                              bass.AP(tensor=dn_dram.tensor, offset=(2 * c + 1) * T, ap=[[0, 64], [1, T]]),
                              deps=[sdn, ntoks[-1] if ntoks else None])
                    nm = V(lambda h, c=c: h.tensor_tensor(bigA[:, c * T:(c + 1) * T], sv(0, 128, 0, T),
                                                          sv(0, 128, 2048, T), AL.mult), deps=[lo, lr0, lr1])
                    ntoks.append(nm)
                pr.barrier()
            for tt in range(NT):
                pidx = tt % 2
                mms = []
                for dc in range(ND):
                    mm = TE(lambda h, dc=dc, tt=tt, pidx=pidx: h.matmul(
                        psums[pidx][:, :], bigA[:, dc * T + tt * 128: dc * T + (tt + 1) * 128],
                        wb1[:, dc * 512:(dc + 1) * 512], start=(dc == 0), stop=(dc == ND - 1)),
                        deps=([lw, ps_last.get(pidx)] if dc == 0 else []), inc=(1 if dc == ND - 1 else 0))
                    mms.append(mm)
                if is0:
                    ev = V(lambda h, tt=tt, ot=ot, pidx=pidx: h.scalar_tensor_tensor(
                        sv(0, 128, 16384 + (tt % 2) * 512, 512), psums[pidx][:, :], 1.0,
                        sv(0, 128, 18432 + ot * 512, 512), AL.bypass, AL.add),
                        deps=[mms[-1], ps_last.get("pb"), ps_last.get("qkev%d" % (tt % 2))])
                else:
                    ev = V(lambda h, tt=tt, pidx=pidx: h.tensor_copy(
                        sv(0, 128, 16384 + (tt % 2) * 512, 512), psums[pidx][:, :]),
                        deps=[mms[-1], ps_last.get("qkev%d" % (tt % 2))])
                ps_last[pidx] = ev
                st = dma("gpsimd", sb_out[tt * 128:(tt + 1) * 128, ot * 512:(ot + 1) * 512],
                    sv(0, 128, 16384 + (tt % 2) * 512, 512), deps=[fv(ev)])
                ps_last["qkev%d" % (tt % 2)] = st
        pr.barrier()

    def load_bias_bcast(src_row, tag):
        tk = dma("sync", sv(0, 128, 18432, 1024),
                 bass.AP(tensor=src_row.tensor, offset=0, ap=[[0, 128], [1, D]]))
        ps_last[tag] = tk
        return tk

    def ffn0_body():
        s_v = sv(0, 64, 0, T)
        g0v = sv(0, 64, 2048, T)
        g1v = sv(0, 64, 4096, T)
        tmpv = sv(0, 64, 6144, T)
        s_evs = []
        lw = dma("sync", bass.AP(tensor=wb0[:].tensor, offset=wb0[:].offset, ap=[[wb0[:].ap[0][0], 128], [64, 8], [1, 64]]),
                 bass.AP(tensor=ffw_in.tensor, offset=0, ap=[[64, 128], [128 * 64, 8], [1, 64]]))
        for tt in range(NQ):
            pidx = tt % 2
            mms = []
            for dc in range(ND):
                mm = TE(lambda h, dc=dc, tt=tt, pidx=pidx: h.matmul(
                    psums[pidx][0:64, :], wb0[:, dc * 64:(dc + 1) * 64],
                    bigA[:, dc * T + tt * 512: dc * T + (tt + 1) * 512],
                    start=(dc == 0), stop=(dc == ND - 1)),
                    deps=([lw, ps_last.get(pidx)] if dc == 0 else []), inc=(1 if dc == ND - 1 else 0))
                mms.append(mm)
            ev = S(lambda h, tt=tt, pidx=pidx: h.activation(
                s_v[0:64, tt * 512:(tt + 1) * 512], psums[pidx][0:64, :], AF.Identity,
                bias=fsm[0:64, 0:1]), deps=[mms[-1]] + tfs)
            ps_last[pidx] = ev
            s_evs.append(ev)
        pr.barrier()
        gcur, gprev = g0v, g1v
        gtok = None
        for i in range(16):
            tm = S(lambda h, i=i: h.activation(tmpv, s_v, AF.Relu,
                                               bias=fsm[0:64, 17 + i:18 + i], scale=fsm[0:64, 1 + i:2 + i]),
                   deps=tfs + [pr.tok("vector")] + s_evs)
            if i == 0:
                gtok = V(lambda h: h.tensor_scalar(gcur, tmpv, fsm[0:64, 33:34], None, AL.mult), deps=[tm] + tfs)
            else:
                gtok = V(lambda h, i=i, gc=gcur, gp=gprev: h.scalar_tensor_tensor(
                    gc, tmpv, fsm[0:64, 33 + i:34 + i], gp, AL.mult, AL.add), deps=[tm, gtok])
            gcur, gprev = gprev, gcur
        gtok = V(lambda h, gp=gprev: h.tensor_scalar(s_v, gp, fsm[0:64, 49:50], None, AL.add), deps=[gtok])
        gfin = s_v
        pr.barrier()
        lw2 = dma("sync", wb0[0:64, 0:D], ffw_out[:, :])
        for tt in range(NT):
            for ot in range(2):
                pidx = tt % 2
                mm = TE(lambda h, tt=tt, ot=ot, pidx=pidx: h.matmul(
                    psums[pidx][:, :], gfin[0:64, tt * 128:(tt + 1) * 128],
                    wb0[0:64, ot * 512:(ot + 1) * 512], start=True, stop=True),
                    deps=[lw2, gtok, ps_last.get(pidx)], inc=1)
                ev = V(lambda h, pidx=pidx, ot=ot: h.scalar_tensor_tensor(
                    sv(0, 128, 16384 + pidx * 512, 512), psums[pidx][:, :], 1.0,
                    sv(0, 128, 18432 + ot * 512, 512), AL.bypass, AL.add),
                    deps=[mm, ps_last.get("ob"), ps_last.get("qkev%d" % pidx)])
                ps_last[pidx] = ev
                st = dma("gpsimd", sb_out[tt * 128:(tt + 1) * 128, ot * 512:(ot + 1) * 512],
                    sv(0, 128, 16384 + pidx * 512, 512), deps=[fv(ev)])
                ps_last["qkev%d" % pidx] = st
            pr.barrier()
        pr.barrier()

    def allreduce_max(vals_toks, n):
        cz = V(lambda h: h.memset(sc[6][0:1, 0:8], 0.0))
        czs = dma("gpsimd", cc_in[0:1, 0:8], sc[6][0:1, 0:8], deps=[fv(cz)])
        toks = []
        for idx, tok in enumerate(vals_toks):
            toks.append(dma("gpsimd", cc_in[0:1, idx:idx + 1], tok[0], deps=[fv(tok[1]), czs]))
        ccall = G(lambda h: h.collective_compute(
            "AllReduce", AL.max, ins=[cc_in[0:1, 0:8]], outs=[cc_out[0:1, 0:8]],
            replica_groups=[[0, 1, 2, 3, 4, 5, 6, 7]]), deps=toks, inc=1)
        lb = dma("sync", sc[6][:, 0:8], bass.AP(tensor=cc_out.tensor, offset=0, ap=[[0, 128], [1, 8]]),
                 deps=[ccall])
        if DBG:
            allreduce_max.n = getattr(allreduce_max, "n", 0)
            dma("gpsimd", dbg["d_cc"][allreduce_max.n:allreduce_max.n + 1, :], cc_out[0:1, 0:8], deps=[ccall])
            allreduce_max.n += 1
        return lb

    def qdq_chain(dst, srcap, rs_ap, s_ap, t_y, t_a, t_b, p, w, deps):
        ty = V(lambda h: h.tensor_scalar(t_y, srcap, rs_ap, None, AL.mult), deps=deps)
        ta = V(lambda h: h.tensor_scalar(t_a, t_y, C32, C32, AL.add, AL.subtract), deps=[ty])
        tb = V(lambda h: h.tensor_scalar(t_b, t_y, C64, C64, AL.add, AL.subtract), deps=[ty])
        tq = V(lambda h: h.scalar_tensor_tensor(t_y, t_a, e_t[:p, 0:1], t_b, AL.mult, AL.add),
               deps=[ta, tb, tk_e])
        return V(lambda h: h.tensor_scalar(dst, t_y, s_ap, None, AL.mult), deps=[tq])

    def scale_from_amax(col_out, amax_src_ap, deps):
        tk = V(lambda h: h.tensor_copy(sc[col_out + 1][:, 0:1], amax_src_ap), deps=deps)
        wd = 1
        while wd < 8:
            tk = V(lambda h, wd=wd: h.tensor_copy(sc[col_out + 1][:, wd:2 * wd], sc[col_out + 1][:, 0:wd]),
                   deps=[tk])
            wd *= 2
        t1 = V(lambda h: h.tensor_scalar(sc[col_out][:, 0:8], sc[col_out + 1][:, 0:8],
                                         1.0 / 32767.0, 1e-12, AL.mult, AL.max), deps=[tk])
        t2 = V(lambda h: h.reciprocal(sc[col_out + 1][:, 0:8], sc[col_out][:, 0:8]), deps=[t1])
        return [t1, fv(t2)]

    def global_amax(chunk_reduce_toks, colbuf_ap, tag):
        tr = TE(lambda h: h.transpose(psums[6][0:8, 0:128], colbuf_ap, ident[:]),
                deps=chunk_reduce_toks + [tk_id, ps_last.get(6)])
        rd = V(lambda h: h.reduce_max(sc[5][0:8, 0:1], psums[6][0:8, 0:128], axis=AX.X), deps=[tr])
        ps_last[6] = rd
        rd2 = fv(rd)
        return (sc[5][0:1, 0:1], rd2)

    def ffn1_weight_dequant():
        specs = [(wup, wdqT_up, 24, D), (wgate, wdqT_gt, 24, D), (wdown, wdqT_dn, 8, 3 * D)]
        for w_in, w_out, nchunk, W in specs:
            nblk32 = W // 32
            for ch in range(nchunk):
                lw = dma("sync", wb0[:, 0:W], w_in[ch * 128:(ch + 1) * 128, :], deps=[ps_last.get("dq")])
                bm = V(lambda h, W=W, nblk32=nblk32: h.reduce_max(
                    sv(0, 128, 0, nblk32),
                    wb0[:, 0:W].rearrange("p (a b) -> p a b", b=32), axis=AX.X,
                    apply_absolute_value=True), deps=[lw])
                s1 = V(lambda h, nblk32=nblk32: h.tensor_scalar(
                    sv(0, 128, 128, nblk32), sv(0, 128, 0, nblk32), 1.0 / 7.0, 1e-12, AL.mult, AL.max), deps=[bm])
                s1b = V(lambda h: h.engine_nop(), deps=[s1])
                s2 = V(lambda h, nblk32=nblk32: h.reciprocal(sv(0, 128, 256, nblk32), sv(0, 128, 128, nblk32)),
                       deps=[s1b])
                exp_toks = []
                for colsrc, coldst in ((128, 512), (256, 512 + W)):
                    tcur = V(lambda h, colsrc=colsrc, coldst=coldst, nblk32=nblk32: h.tensor_copy(
                        bass.AP(tensor=scr[:].tensor, offset=scr[:].offset + coldst,
                                ap=[[scr[:].ap[0][0], 128], [32, nblk32], [1, 1]]),
                        sv(0, 128, colsrc, nblk32)), deps=[s2])
                    wd = 1
                    while wd < 32:
                        wn = min(wd, 32 - wd)
                        tcur = V(lambda h, coldst=coldst, wd=wd, wn=wn, nblk32=nblk32: h.tensor_copy(
                            bass.AP(tensor=scr[:].tensor, offset=scr[:].offset + coldst + wd,
                                    ap=[[scr[:].ap[0][0], 128], [32, nblk32], [1, wn]]),
                            bass.AP(tensor=scr[:].tensor, offset=scr[:].offset + coldst,
                                    ap=[[scr[:].ap[0][0], 128], [32, nblk32], [1, wn]])), deps=[tcur])
                        wd += wn
                    exp_toks.append(tcur)
                c_y, c_a, c_b, c_o = 512 + 2 * W, 512 + 3 * W, 512 + 4 * W, 512 + 5 * W
                ty = V(lambda h, W=W, c_y=c_y: h.tensor_tensor(
                    sv(0, 128, c_y, W), wb0[:, 0:W], sv(0, 128, 512 + W, W), AL.mult), deps=exp_toks + [lw])
                ta = V(lambda h, W=W, c_y=c_y, c_a=c_a: h.tensor_scalar(
                    sv(0, 128, c_a, W), sv(0, 128, c_y, W), C32, C32, AL.add, AL.subtract), deps=[ty])
                tb = V(lambda h, W=W, c_y=c_y, c_b=c_b: h.tensor_scalar(
                    sv(0, 128, c_b, W), sv(0, 128, c_y, W), C64, C64, AL.add, AL.subtract), deps=[ty])
                tq = V(lambda h, W=W, c_a=c_a, c_b=c_b, c_y=c_y: h.scalar_tensor_tensor(
                    sv(0, 128, c_y, W), sv(0, 128, c_a, W), e_t[:, 0:1], sv(0, 128, c_b, W),
                    AL.mult, AL.add), deps=[ta, tb, tk_e])
                tw = V(lambda h, W=W, c_y=c_y, c_o=c_o: h.tensor_tensor(
                    sv(0, 128, c_o, W), sv(0, 128, c_y, W), sv(0, 128, 512, W), AL.mult), deps=[tq])
                last_st = None
                for j in range(W // 128):
                    pidx = j % 2
                    tr = TE(lambda h, c_o=c_o, j=j, pidx=pidx: h.transpose(
                        psums[pidx][:, 0:128], sv(0, 128, c_o + j * 128, 128), ident[:]),
                        deps=[tw, tk_id, ps_last.get(pidx)])
                    cp = V(lambda h, pidx=pidx: h.tensor_copy(
                        sv(0, 128, 256 + pidx * 128, 128), psums[pidx][:, 0:128]),
                        deps=[tr, ps_last.get("dqev%d" % pidx)])
                    ps_last[pidx] = cp
                    last_st = dma("gpsimd", w_out[j * 128:(j + 1) * 128, ch * 128:(ch + 1) * 128],
                                  sv(0, 128, 256 + pidx * 128, 128), deps=[fv(cp)])
                    ps_last["dqev%d" % pidx] = last_st
                ps_last["dq"] = last_st
            pr.barrier()

    def ffn1_body():
        zms = V(lambda h: h.memset(sv(0, 128, 8192, 32), 0.0))
        rtoks = [zms]
        for c in range(ND):
            rt = V(lambda h, c=c: h.reduce_max(sv(0, 128, 8192 + c, 1), bigA[:, c * T:(c + 1) * T],
                                               axis=AX.X, apply_absolute_value=True), deps=[zms])
            rtoks.append(rt)
        rall = V(lambda h: h.reduce_max(sv(0, 128, 8200, 1), sv(0, 128, 8192, 8), axis=AX.X), deps=rtoks)
        rallb = V(lambda h: h.tensor_copy(sv(0, 128, 8208, 8), sv(0, 128, 8200, 8)), deps=[rall])
        gx = global_amax([rallb], sv(0, 128, 8208, 8), "x")
        lb = allreduce_max([gx], 1)
        stoks = scale_from_amax(0, sc[6][:, 0:1], [lb])
        pr.barrier()
        for c in range(ND):
            qdq_chain(bigA[:, c * T:(c + 1) * T], bigA[:, c * T:(c + 1) * T],
                      sc[1][:, 0:1], sc[0][:, 0:1], sv(0, 128, 0, T), sv(0, 128, 2048, T),
                      sv(0, 128, 4096, T), 128, T, deps=stoks + [pr.tok("vector")])
        pr.barrier()
        if DBG:
            dma("gpsimd", dbg["d_xq"][:, :], bigA[:, 0:T], deps=[pr.tok("vector")])
        for (wT_dram, st_dram, colpos) in ((wdqT_up, up_st, 0), (wdqT_gt, gt_st, 1)):
            am0 = V(lambda h: h.memset(sv(0, 128, 12800, 40), 0.0), deps=[ps_last.get("r2t")])
            am_toks = [am0]
            for ic in range(24):
                wcol = wb0 if ic % 2 == 0 else wb1
                lw = dma("sync", bass.AP(tensor=wcol[:].tensor, offset=wcol[:].offset,
                                         ap=[[wcol[:].ap[0][0], 128], [128, 8], [1, 128]]),
                         bass.AP(tensor=wT_dram.tensor, offset=ic * 128, ap=[[3 * D, 128], [128 * 3 * D, 8], [1, 128]]),
                         deps=[ps_last.get("u%d" % (ic % 2))])
                for tt in range(NQ):
                    pidx = 4 + tt % 2
                    mms = []
                    for dc in range(ND):
                        mm = TE(lambda h, wcol=wcol, dc=dc, tt=tt, pidx=pidx: h.matmul(
                            psums[pidx][:, :], wcol[:, dc * 128:(dc + 1) * 128],
                            bigA[:, dc * T + tt * 512: dc * T + (tt + 1) * 512],
                            start=(dc == 0), stop=(dc == ND - 1)),
                            deps=([lw, ps_last.get(pidx)] if dc == 0 else []), inc=(1 if dc == ND - 1 else 0))
                        mms.append(mm)
                    ev = S(lambda h, tt=tt, pidx=pidx, ic=ic: h.copy(
                        sv(0, 128, 8704 + (ic % 2) * T + tt * 512, 512), psums[pidx][:, :]),
                        deps=[mms[-1], ps_last.get("rtu%d" % (ic % 2))])
                    ps_last[pidx] = ev
                    am_toks.append(ev)
                rt = V(lambda h, ic=ic: h.reduce_max(sv(0, 128, 12800 + ic, 1),
                                                     sv(0, 128, 8704 + (ic % 2) * T, T), axis=AX.X,
                                                     apply_absolute_value=True), deps=am_toks[-4:] + [am0])
                ps_last["rtu%d" % (ic % 2)] = rt
                st = dma("gpsimd", st_dram[ic * 128:(ic + 1) * 128, :], sv(0, 128, 8704 + (ic % 2) * T, T),
                         deps=[fs(am_toks[-1])] + am_toks[-4:])
                ps_last["u%d" % (ic % 2)] = st
                am_toks.append(rt)
            r1 = V(lambda h: h.reduce_max(sv(0, 128, 12832, 1), sv(0, 128, 12800, 24), axis=AX.X),
                   deps=am_toks)
            r2 = V(lambda h, colpos=colpos: h.tensor_copy(sv(0, 128, 12840 + 8 * colpos, 8), sv(0, 128, 12832, 8)),
                   deps=[r1])
            ps_last["amax%d" % colpos] = r2
            ps_last["r2t"] = r2
            pr.barrier()
        gup = global_amax([ps_last["amax0"]], sv(0, 128, 12840, 8), "up")
        up_tok = dma("gpsimd", bc_dram[1:2, 0:8], sc[5][0:1, 0:8], deps=[gup[1]])
        ggt = global_amax([ps_last["amax1"], up_tok], sv(0, 128, 12848, 8), "gt")
        gt_tok = dma("gpsimd", bc_dram[2:3, 0:8], sc[5][0:1, 0:8], deps=[ggt[1], up_tok])
        lb1 = dma("sync", sc[4][0:1, 0:8], bc_dram[1:2, 0:8], deps=[up_tok])
        lb2 = dma("sync", sc[4][1:2, 0:8], bc_dram[2:3, 0:8], deps=[gt_tok])
        lb = allreduce_max([(sc[4][0:1, 0:1], lb1), (sc[4][1:2, 0:1], lb2)], 2)
        s_up = scale_from_amax(0, sc[6][:, 0:1], [lb])
        s_gt = scale_from_amax(2, sc[6][:, 1:2], [lb])
        pr.barrier()
        ub = sv(0, 128, 0, T)
        gb = sv(0, 128, 2048, T)
        hb = sv(0, 128, 4096, T)
        tb_ = sv(0, 128, 6144, T)
        am0 = V(lambda h: h.memset(sv(0, 128, 12800, 40), 0.0), deps=[ps_last.get("r2t")])
        am_toks = [am0]
        for ic in range(24):
            lg = dma("sync", gb, gt_st[ic * 128:(ic + 1) * 128, :], deps=am_toks[-2:])
            q1 = qdq_chain(gb, gb, sc[3][:, 0:1], sc[2][:, 0:1], ub, hb, tb_, 128, T,
                           deps=[lg] + s_gt)
            sg = S(lambda h: h.activation(ub, gb, AF.Sigmoid), deps=[q1])
            y2 = V(lambda h: h.tensor_single_scalar(hb, ub, 65536.0, AL.mult), deps=[sg])
            a2 = V(lambda h: h.tensor_scalar(tb_, hb, C32, C32, AL.add, AL.subtract), deps=[y2])
            b2 = V(lambda h: h.tensor_scalar(ub, hb, C64, C64, AL.add, AL.subtract), deps=[y2])
            q2 = V(lambda h: h.scalar_tensor_tensor(hb, tb_, e_t[:, 0:1], ub, AL.mult, AL.add),
                   deps=[a2, b2, tk_e])
            sq = V(lambda h: h.tensor_scalar(ub, hb, 65535.0, 2.0 ** -16, AL.min, AL.mult), deps=[q2])
            g2 = V(lambda h: h.tensor_tensor(hb, gb, ub, AL.mult), deps=[sq])
            rt = V(lambda h, ic=ic: h.reduce_max(sv(0, 128, 12800 + ic, 1), hb, axis=AX.X,
                                                 apply_absolute_value=True), deps=[g2, am0])
            st = dma("gpsimd", g2_st[ic * 128:(ic + 1) * 128, :], hb, deps=[fv(g2)])
            am_toks.append(rt)
            am_toks.append(st)
        r1 = V(lambda h: h.reduce_max(sv(0, 128, 12832, 1), sv(0, 128, 12800, 24), axis=AX.X), deps=am_toks)
        r2 = V(lambda h: h.tensor_copy(sv(0, 128, 12840, 8), sv(0, 128, 12832, 8)), deps=[r1])
        ps_last["r2t"] = r2
        gg2 = global_amax([r2], sv(0, 128, 12840, 8), "g2")
        lb = allreduce_max([gg2], 1)
        s_g2 = scale_from_amax(2, sc[6][:, 0:1], [lb])
        pr.barrier()
        am0 = V(lambda h: h.memset(sv(0, 128, 12800, 40), 0.0), deps=[ps_last.get("r2t")])
        am_toks = [am0]
        for ic in range(24):
            lg = dma("sync", gb, g2_st[ic * 128:(ic + 1) * 128, :], deps=am_toks[-2:])
            lu = dma("sync", ub, up_st[ic * 128:(ic + 1) * 128, :], deps=am_toks[-2:])
            q1 = qdq_chain(gb, gb, sc[3][:, 0:1], sc[2][:, 0:1], hb, tb_, sv(0, 128, 8192, T), 128, T,
                           deps=[lg] + s_g2)
            q2 = qdq_chain(ub, ub, sc[1][:, 0:1], sc[0][:, 0:1], hb, tb_, sv(0, 128, 8192, T), 128, T,
                           deps=[lu, q1] + s_up)
            oo = V(lambda h: h.tensor_tensor(hb, gb, ub, AL.mult), deps=[q2])
            rt = V(lambda h, ic=ic: h.reduce_max(sv(0, 128, 12800 + ic, 1), hb, axis=AX.X,
                                                 apply_absolute_value=True), deps=[oo, am0])
            st = dma("gpsimd", gt_st[ic * 128:(ic + 1) * 128, :], hb, deps=[fv(oo)])
            am_toks.append(rt)
            am_toks.append(st)
        r1 = V(lambda h: h.reduce_max(sv(0, 128, 12832, 1), sv(0, 128, 12800, 24), axis=AX.X), deps=am_toks)
        r2 = V(lambda h: h.tensor_copy(sv(0, 128, 12840, 8), sv(0, 128, 12832, 8)), deps=[r1])
        ps_last["r2t"] = r2
        go = global_amax([r2], sv(0, 128, 12840, 8), "o")
        lb = allreduce_max([go], 1)
        s_o = scale_from_amax(0, sc[6][:, 0:1], [lb])
        pr.barrier()
        for grp in range(3):
            qtoks = []
            for j in range(ND):
                ic = grp * ND + j
                lo = dma("sync", gb, gt_st[ic * 128:(ic + 1) * 128, :], deps=qtoks[-1:])
                qt = qdq_chain(bigA[:, j * T:(j + 1) * T], gb, sc[1][:, 0:1], sc[0][:, 0:1],
                               ub, hb, tb_, 128, T, deps=[lo] + s_o)
                qtoks.append(qt)
            lw = dma("sync", bass.AP(tensor=wb0[:].tensor, offset=wb0[:].offset,
                                     ap=[[wb0[:].ap[0][0], 128], [1024, 4], [1, 1024]]),
                     bass.AP(tensor=wdqT_dn.tensor, offset=grp * 8 * 128 * D,
                             ap=[[D, 128], [128 * D, 4], [1, 1024]]))
            lw2 = dma("sync", bass.AP(tensor=wb1[:].tensor, offset=wb1[:].offset,
                                      ap=[[wb1[:].ap[0][0], 128], [1024, 4], [1, 1024]]),
                      bass.AP(tensor=wdqT_dn.tensor, offset=(grp * 8 + 4) * 128 * D,
                              ap=[[D, 128], [128 * D, 4], [1, 1024]]))
            pr.barrier()
            for tt in range(NT):
                for ot in range(2):
                    pidx = (2 * tt + ot) % 2
                    mms = []
                    for j in range(ND):
                        wcol = wb0 if j < 4 else wb1
                        jj = j % 4
                        mm = TE(lambda h, wcol=wcol, j=j, jj=jj, tt=tt, ot=ot, pidx=pidx: h.matmul(
                            psums[pidx][:, :], bigA[:, j * T + tt * 128: j * T + (tt + 1) * 128],
                            wcol[:, jj * 1024 + ot * 512: jj * 1024 + ot * 512 + 512],
                            start=(j == 0), stop=(j == ND - 1)),
                            deps=([lw, lw2, ps_last.get(pidx)] if j == 0 else []), inc=(1 if j == ND - 1 else 0))
                        mms.append(mm)
                    ev = V(lambda h, pidx=pidx: h.tensor_copy(sv(0, 128, 16384 + pidx * 512, 512),
                                                              psums[pidx][:, :]),
                           deps=[mms[-1], ps_last.get("qkev%d" % pidx)])
                    ps_last[pidx] = ev
                    evf = fv(ev)
                    if grp == 0:
                        st = dma("gpsimd", sb_out[tt * 128:(tt + 1) * 128, ot * 512:(ot + 1) * 512],
                            sv(0, 128, 16384 + pidx * 512, 512), deps=[evf])
                    else:
                        st = pr.emit("gpsimd", lambda h, tt=tt, ot=ot, pidx=pidx: h.dma_start(
                            out=sb_out[tt * 128:(tt + 1) * 128, ot * 512:(ot + 1) * 512],
                            in_=sv(0, 128, 16384 + pidx * 512, 512), accum_op=AL.add),
                            deps=[evf, ("gq", pr.cnt["gq"])], inc=16, sem="gq")
                    ps_last["qkev%d" % pidx] = st
            pr.barrier()

    if NBLK >= 1:
        pre_template(0)
        load_bias_bcast(projb0, "pb")
        tvb = dma("sync", sv(0, 128, 18432 + 0, 1024), bass.AP(tensor=vb0.tensor, offset=0, ap=[[0, 128], [1, D]]))
        ps_last["vb"] = tvb
        attn_body(0)
        post_template(0)
    if NBLK >= 2:
        pre_template(1)
        load_bias_bcast(ffb_out, "ob")
        ffn0_body()
        post_template(1)
    if NBLK >= 3:
        pre_template(2)
        attn_body(1)
        post_template(2)
    if NBLK >= 4:
        ffn1_weight_dequant()
        pre_template(3)
        ffn1_body()
        post_template(3)

    if DBG:
        pr.barrier()
        dma("gpsimd", dbg["d_q"][:, :], q_st[:, :])
        dma("gpsimd", dbg["d_k"][:, :], k_st[:, :])
        dma("gpsimd", dbg["d_v"][:, :], v_st[:, :])
        dma("gpsimd", dbg["d_oT"][:, :], oT_st[:, :])
        dma("gpsimd", dbg["d_dn"][:, :], dn_dram[:, :])
        dma("gpsimd", dbg["d_sbout"][:, :], sb_out[:, :])
        dma("gpsimd", dbg["d_mh"][0:1, 0:8], bc_dram[0:1, 0:8])
        dma("gpsimd", dbg["d_vec"][:, :], vecflat[0:1, 0:VW])
        dma("gpsimd", dbg["d_m0"][:, :], m_dram[0:128, 0:MW])
        dma("gpsimd", dbg["d_xres"][:, :], x_res[:, :])
        if NBLK >= 4:
            dma("gpsimd", dbg["d_up0"][:, :], up_st[0:128, :])
            dma("gpsimd", dbg["d_oraw"][:, :], gt_st[0:128, :])
            dma("gpsimd", dbg["d_g2"][:, :], g2_st[0:128, :])
            dma("gpsimd", dbg["d_wdq"][:, :], wdqT_up[0:128, :])


    pr.emit("gpsimd", None, deps=[("gq", pr.cnt["gq"]), ("gpsimd", pr.cnt["gpsimd"])])

    from contextlib import ExitStack
    with ExitStack() as st:
        sems = {s: st.enter_context(nc.semaphore("sem_" + s)) for s in SEMS}
        block = st.enter_context(nc.Block())
        for eng in ENGINES:
            def make(eng):
                def body(h):
                    for fn, waits, inc, semname in pr.steps[eng]:
                        for peng, val in waits:
                            h.wait_ge(sems[peng], val)
                        if fn is None:
                            continue
                        ins = fn(h)
                        if inc:
                            ins.then_inc(sems[semname], inc)
                return body
            getattr(block, eng)(make(eng))
    return nc


_NC_CACHE = None


def kernel(L0_attn_x, L0_attn_qkv_w, L0_attn_qkv_b, L0_attn_proj_w, L0_attn_proj_b,
           L0_attn_rel_pos_emb, L0_ffn_Win, L0_ffn_bin_, L0_ffn_W1, L0_ffn_b1,
           L0_ffn_W2, L0_ffn_b2, L0_ffn_Wout, L0_ffn_bout,
           L1_attn_w_qkv, L1_attn_w_out, L1_ffn_w_gate, L1_ffn_w_up, L1_ffn_w_down):
    global _NC_CACHE
    if _NC_CACHE is None:
        _NC_CACHE = build_program()
    nc = _NC_CACHE
    f32 = np.float32
    common = {
        "qkvwT0": np.ascontiguousarray(np.asarray(L0_attn_qkv_w, f32).T),
        "qkvb0": np.ascontiguousarray(np.asarray(L0_attn_qkv_b, f32).reshape(24, 128).T),
        "vb0": np.ascontiguousarray(np.asarray(L0_attn_qkv_b, f32)[2048:3072].reshape(1, D)),
        "projT0": np.ascontiguousarray(np.asarray(L0_attn_proj_w, f32).T),
        "projb0": np.ascontiguousarray(np.asarray(L0_attn_proj_b, f32).reshape(1, D)),
        "relpos": np.ascontiguousarray(np.asarray(L0_attn_rel_pos_emb, f32)),
        "ffw_in": np.ascontiguousarray(np.asarray(L0_ffn_Win, f32)),
        "ffb_in": np.ascontiguousarray(np.asarray(L0_ffn_bin_, f32).reshape(64, 1)),
        "ffw1": np.ascontiguousarray(np.asarray(L0_ffn_W1, f32)),
        "ffb1": np.ascontiguousarray(np.asarray(L0_ffn_b1, f32)),
        "ffw2": np.ascontiguousarray(np.asarray(L0_ffn_W2, f32)),
        "ffb2": np.ascontiguousarray(np.asarray(L0_ffn_b2, f32).reshape(64, 1)),
        "ffw_out": np.ascontiguousarray(np.asarray(L0_ffn_Wout, f32)),
        "ffb_out": np.ascontiguousarray(np.asarray(L0_ffn_bout, f32).reshape(1, D)),
        "qkvwT1": np.ascontiguousarray(np.asarray(L1_attn_w_qkv, f32).T),
        "woutT1": np.ascontiguousarray(np.asarray(L1_attn_w_out, f32).T),
        "wgate": np.ascontiguousarray(np.asarray(L1_ffn_w_gate, f32)),
        "wup": np.ascontiguousarray(np.asarray(L1_ffn_w_up, f32)),
        "wdown": np.ascontiguousarray(np.asarray(L1_ffn_w_down, f32)),
    }
    x = np.asarray(L0_attn_x, f32)
    in_maps = [dict(common, x_in=np.ascontiguousarray(x[c])) for c in range(8)]
    res = run_bass_kernel_spmd(nc, in_maps, core_ids=list(range(8)))
    kernel.last_results = res.results
    return np.stack([res.results[c]["y_out"] for c in range(8)], axis=0)
